# version 05
# baseline (speedup 1.0000x reference)
import math
from contextlib import ExitStack

import numpy as np
import ml_dtypes

import concourse.bass as bass
import concourse.mybir as mybir
from concourse.bass_utils import run_bass_kernel_spmd

F32 = mybir.dt.float32
BF16 = mybir.dt.bfloat16
I32 = mybir.dt.int32
AF = mybir.ActivationFunctionType
ALU = mybir.AluOpType
AXX = mybir.AxisListType.X
BF = ml_dtypes.bfloat16

B, S, D = 4, 4096, 2048
H = 16
NBLK, LBLK = 32, 16
T = 2048
NC = 16
NT = 16
NTC = 4
E, I_FF = 4, 2048
EPS = 1e-5
ISQ = 1.0 / math.sqrt(128.0)
C_OUT = 8.0
C_IN = 24.0
POLL_MAX = 500000
N_CORES = 8


class SemC:
    def __init__(self, h):
        self.h = h
        self.v = 0


def build_nc(dbg=(), upto=99, skipo=False, n_experts=E):
    nc = bass.Bass("TRN2", target_bir_lowering=False)
    es = ExitStack()
    G = {}

    def guard(eng, *keys):
        for k in keys:
            if isinstance(k, tuple) and len(k) == 1:
                k = k[0]
            g = G.get(k)
            if g is not None:
                eng.wait_ge(g[0].h, g[1])

    def dram(name, shape, dtype, kind="Internal", **kw):
        k = "ExternalOutput" if name in dbg else kind
        return nc.dram_tensor(name, shape, dtype, kind=k, **kw).ap()

    x_in = dram("x_shard", [T, D], F32, kind="ExternalInput")
    w_qkv = dram("w_qkv", [D, 3 * D], F32, kind="ExternalInput")
    w_o = dram("w_o", [D, D], F32, kind="ExternalInput")
    wq = dram("wq", [D, D], F32, kind="ExternalInput")
    wk = dram("wk", [D, D], F32, kind="ExternalInput")
    wv = dram("wv", [D, D], F32, kind="ExternalInput")
    wo2 = dram("wo2", [D, D], F32, kind="ExternalInput")
    brow_in = dram("brow_in", [8, D], F32, kind="ExternalInput")
    wg_in = dram("wg", [D, E], F32, kind="ExternalInput")
    we1 = dram("we1", [E, D, I_FF], F32, kind="ExternalInput")
    be1_in = dram("be1", [E, I_FF], F32, kind="ExternalInput")
    we2 = dram("we2", [E, I_FF, D], F32, kind="ExternalInput")
    be2_in = dram("be2", [E, D], F32, kind="ExternalInput")
    cos_in = dram("rope_cos", [128, T], F32, kind="ExternalInput")
    sin_in = dram("rope_sin", [128, T], F32, kind="ExternalInput")
    masks_in = dram("masks", [8, 128, 512], F32, kind="ExternalInput")
    nonce_in = dram("nonce", [1, 1], I32, kind="ExternalInput")
    slot_in = dram("slot", [1, 1], I32, kind="ExternalInput")
    y_out = dram("y", [T, D], F32, kind="ExternalOutput")

    q_out_d = dram("q_out_d", [D, T], BF16)
    v_out_d = dram("v_out_d", [D, T], BF16)
    x1_d = dram("x1_d", [T, D], F32)
    hin_d = dram("hin_d", [T, D], F32)
    h2_d = dram("h2_d", [T, D], F32)
    qin_d = dram("qin_d", [D, T], BF16)
    xnf_d = dram("xnf_d", [D, T], BF16)
    hn1f_d = dram("hn1f_d", [D, T], BF16)
    hn2f_d = dram("hn2f_d", [D, T], BF16)

    k_out_sh = dram("k_out_sh", [2, D, T], BF16, addr_space="Shared")
    k_in_sh = dram("k_in_sh", [2, D, T], BF16, addr_space="Shared")
    v_in_sh = dram("v_in_sh", [2, H, T, 128], BF16, addr_space="Shared")
    cs_sh = dram("cs_sh", [2, H, S], BF16, addr_space="Shared")
    flags_sh = dram("flags_sh", [2, 16], I32, addr_space="Shared")

    ident_np = np.zeros((128, 128), BF)
    np.fill_diagonal(ident_np, 1)
    ident_c = nc.inline_tensor(np.ascontiguousarray(ident_np), "ident_c").ap()
    sel4_np = np.zeros((4, 4 * 128), BF)
    for e_ in range(4):
        sel4_np[e_, e_ * 128:(e_ + 1) * 128] = 1
    sel4_c = nc.inline_tensor(np.ascontiguousarray(sel4_np), "sel4_c").ap()
    ones_c = nc.inline_tensor(np.ones((1, 512), BF), "ones_c").ap()
    tieb_np = np.zeros((128, 4), np.float32)
    tieb_np[:, :] = np.array([3e-6, 2e-6, 1e-6, 0.0], np.float32)[None, :]
    tieb_c = nc.inline_tensor(tieb_np, "tieb_c").ap()

    def sb(name, shape, dtype):
        return es.enter_context(nc.sbuf_tensor(name, shape, dtype))

    ident = sb("ident", [128, 128], BF16)
    sel4 = sb("sel4", [4, 4 * 128], BF16)
    onesr = sb("onesr", [1, 512], BF16)
    ones32 = sb("ones32", [128, 32], BF16)
    ones128 = sb("ones128", [128, 128], BF16)
    stats = sb("stats", [128, 64], F32)
    epsc = sb("epsc", [128, 1], F32)
    nco = sb("nco", [128, 1], F32)
    nci = sb("nci", [128, 1], F32)
    zcol = sb("zcol", [128, 8], F32)
    invz = sb("invz", [128, 1], F32)
    invzb = sb("invzb", [128, 32], BF16)
    invzf = sb("invzf", [128, 512], F32)
    graw = sb("graw", [128, 4], F32)
    tieb = sb("tieb", [128, 4], F32)
    gst = sb("gst", [128, 24], F32)
    combf = sb("combf", [128, 4], F32)
    combb = sb("combb", [128, 4], BF16)
    combT = sb("combT", [4, T], BF16)
    be2r = sb("be2r", [4, D], BF16)
    browR = sb("browR", [1, 4 * D], BF16)
    csr0 = sb("csr0", [1, T], BF16)
    csr1 = sb("csr1", [1, T], BF16)
    flg = sb("flg", [1, 32], I32)
    myflag = sb("myflag", [1, 16], I32)
    noncev = sb("noncev", [1, 1], I32)
    slotv = sb("slotv", [1, 1], I32)
    fmbuf = sb("fmbuf", [128, NC * T], BF16)
    AR_COLS = 44 * 1024
    arena = sb("arena", [128, AR_COLS], BF16)

    def arf32(off_b, n):
        assert off_b % 4 == 0 and off_b + 4 * n <= AR_COLS * 2, off_b
        return arena[:, off_b // 2: off_b // 2 + 2 * n].bitcast(F32)

    def arbf(off_b, n):
        assert off_b % 2 == 0 and off_b + 2 * n <= AR_COLS * 2, off_b
        return arena[:, off_b // 2: off_b // 2 + n]

    pb = [es.enter_context(nc.psum_tensor(f"pb{i}", [128, 512], F32))
          for i in range(6)]
    ptr = [es.enter_context(nc.psum_tensor(f"ptr{i}", [128, 1024], BF16))
           for i in range(2)]

    def sem(name):
        return SemC(es.enter_context(nc.semaphore(name)))

    LD, WT, MM, EV, AC, GP, ST, BR, GS = (sem(n) for n in
                                          "LD WT MM EV AC GP ST BR GS".split())

    REGS = {}

    def get_reg(eng, name):
        key = (eng.engine, name)
        if key not in REGS:
            REGS[key] = eng.alloc_register(name)
        return REGS[key]

    def dq(q, smc, out, in_, **kw):
        q.wait_ge(smc.h, smc.v)
        q.dma_start(out=out, in_=in_, **kw).then_inc(smc.h, 16)
        smc.v += 16
    BARS = [sem(f"PB{i}") for i in range(4)]

    block = es.enter_context(nc.Block())

    def emit_tail():
        fins = [(LD, LD.v), (WT, WT.v), (MM, MM.v), (EV, EV.v), (AC, AC.v),
                (GP, GP.v), (ST, ST.v), (BR, BR.v), (GS, GS.v)]

        @block.sync
        def _(q):
            for smc, vv in fins:
                if vv > 0:
                    q.wait_ge(smc.h, vv)
        es.close()
        return nc

    def barrier(stage, payload_sem, payload_val):
        PB = BARS[stage]

        @block.gpsimd
        def _(g):
            g.wait_ge(payload_sem.h, payload_val)
            slot = get_reg(g, "bar_s")
            g.reg_load(slot, slotv[0:1, 0:1])
            with g.If_eq(slot, 0):
                g.dma_start(out=flags_sh[0:1, stage:stage + 1],
                            in_=myflag[0:1, stage:stage + 1]).then_inc(PB.h, 16)
            with g.Else():
                g.dma_start(out=flags_sh[1:2, stage:stage + 1],
                            in_=myflag[0:1, stage:stage + 1]).then_inc(PB.h, 16)
            it = get_reg(g, "bar_i")
            g.reg_mov(it, 0)
            done = get_reg(g, "bar_d")
            g.reg_mov(done, 0)
            tgt = get_reg(g, "bar_t")
            c = get_reg(g, "bar_c")
            f0 = get_reg(g, "bar_f0")
            f1 = get_reg(g, "bar_f1")
            nn = get_reg(g, "bar_n")
            t1 = get_reg(g, "bar_u")
            t2 = get_reg(g, "bar_v")
            g.reg_load(nn, noncev[0:1, 0:1])

            def cond():
                g.reg_alu(c, done, 0, ALU.is_equal)
                return c
            with g.While(cond):
                g.dma_start(out=flg[0:1, :],
                            in_=flags_sh.rearrange("a b -> (a b)")[None, :]
                            ).then_inc(PB.h, 16)
                g.reg_add(it, it, 1)
                g.reg_mul(tgt, it, 16)
                g.reg_add(tgt, tgt, 16)
                g.wait_ge(PB.h, tgt)
                g.reg_load(f0, flg[0:1, stage:stage + 1])
                g.reg_load(f1, flg[0:1, 16 + stage:17 + stage])
                g.reg_alu(f0, f0, f1, ALU.min)
                g.reg_sub(f0, f0, nn)
                g.reg_alu(t1, f0, stage, ALU.is_ge)
                g.reg_alu(t2, f0, 1000000, ALU.is_le)
                g.reg_alu(t1, t1, t2, ALU.bitwise_and)
                g.reg_alu(t2, it, POLL_MAX, ALU.is_ge)
                g.reg_alu(done, t1, t2, ALU.bitwise_or)
            g.nop().then_inc(BR.h, 1)
            BR.v += 1
        return ("BARR", stage), (BR, BR.v)

    def slot_dma(g, dst0, dst1, src, sem_c, tag, inc=16):
        g.wait_ge(sem_c.h, sem_c.v)
        slot = get_reg(g, "slotreg")
        g.reg_load(slot, slotv[0:1, 0:1])
        with g.If_eq(slot, 0):
            g.dma_start(out=dst0, in_=src).then_inc(sem_c.h, inc)
        with g.Else():
            g.dma_start(out=dst1, in_=src).then_inc(sem_c.h, inc)
        sem_c.v += inc

    @block.sync
    def _(q):
        for dst, src in ((ident, ident_c), (sel4, sel4_c), (onesr, ones_c)):
            dq(q, LD, dst[:, :], src[:, :])
        dq(q, LD, tieb[:, :], tieb_c[:, :])
        dq(q, LD, noncev[:, :], nonce_in[:, :])
        dq(q, LD, slotv[:, :], slot_in[:, :])
    ld_consts = LD.v
    G["consts"] = (LD, ld_consts)

    @block.gpsimd
    def _(g):
        for tile_, val_ in ((ones32, 1.0), (ones128, 1.0), (epsc, EPS),
                            (nco, -C_OUT), (nci, -C_IN)):
            g.memset(tile_[:, :], val_).then_inc(GP.h, 1)
            GP.v += 1
        dq(g, WT, be2r[:, :], be2_in[:, :])
        g.wait_ge(LD.h, ld_consts)
        nn = get_reg(g, "nn0")
        g.reg_load(nn, noncev[0:1, 0:1])
        vv = get_reg(g, "vv0")
        for s_ in range(16):
            g.reg_add(vv, nn, s_)
            g.store(myflag[0:1, s_:s_ + 1], vv)
        g.nop().then_inc(GP.h, 1)
        GP.v += 1
    G["gp_init"] = (GP, GP.v)

    def ln_stats_apply(tag, tt, xa, dst, accum_col=None):
        @block.vector
        def _(v):
            guard(v, (tag, "ld", tt))
            v.reduce_sum(stats[:, 0:1], xa[:, :], axis=AXX).then_inc(EV.h, 1)
            EV.v += 1
        G[(tag, "sum", tt)] = (EV, EV.v)

        @block.scalar
        def _(s):
            guard(s, (tag, "ld", tt), (tag, "sqfree", tt - 1),
                  (tag, "var", tt - 1))
            s.activation(SQS[:, :], xa[:, :], AF.Square,
                         accum_out=stats[:, 1:2]).then_inc(AC.h, 1)
            AC.v += 1
        G[(tag, "sq", tt)] = (AC, AC.v)

        @block.vector
        def _(v):
            guard(v, (tag, "sum", tt), (tag, "sq", tt))
            v.tensor_scalar_mul(stats[:, 2:3], stats[:, 0:1], 1.0 / D
                                ).then_inc(EV.h, 1); EV.v += 1
            v.tensor_scalar_mul(stats[:, 3:4], stats[:, 1:2], 1.0 / D
                                ).then_inc(EV.h, 1); EV.v += 1
            v.wait_ge(EV.h, EV.v)
            v.tensor_tensor(stats[:, 4:5], stats[:, 2:3], stats[:, 2:3],
                            ALU.mult).then_inc(EV.h, 1); EV.v += 1
            v.wait_ge(EV.h, EV.v)
            v.tensor_tensor(stats[:, 5:6], stats[:, 3:4], stats[:, 4:5],
                            ALU.subtract).then_inc(EV.h, 1)
            EV.v += 1
        G[(tag, "var", tt)] = (EV, EV.v)

        @block.scalar
        def _(s):
            guard(s, (tag, "var", tt), ("gp_init",))
            s.activation(stats[:, 6:7], stats[:, 5:6], AF.Sqrt, bias=epsc[:, :]
                         ).then_inc(AC.h, 1)
            AC.v += 1
        G[(tag, "std", tt)] = (AC, AC.v)

        @block.vector
        def _(v):
            guard(v, (tag, "std", tt))
            v.reciprocal(stats[:, 7:8], stats[:, 6:7]).then_inc(EV.h, 1)
            EV.v += 1
            v.wait_ge(EV.h, EV.v)
            guard(v, (tag, "applyfree", tt), (tag, "sq2", tt - 1))
            v.tensor_scalar(dst[:, :], xa[:, :], stats[:, 2:3], stats[:, 7:8],
                            ALU.subtract, ALU.mult).then_inc(EV.h, 1)
            EV.v += 1
            if accum_col is not None:
                v.wait_ge(EV.h, EV.v)
                v.reduce_sum(stats[:, accum_col:accum_col + 1], dst[:, :],
                             axis=AXX).then_inc(EV.h, 1)
                EV.v += 1
        G[(tag, "h", tt)] = (EV, EV.v)

    def transpose_to_fm(tag, tt):
        for g4 in range(4):
            bank = ptr[g4 % 2]

            @block.tensor
            def _(t, g4=g4, bank=bank):
                if g4 == 0:
                    guard(t, (tag, "hn", tt))
                guard(t, (tag, "tb", tt, g4 - 2), (tag, "tb", tt - 1, g4 + 2))
                for c4 in range(4):
                    c = g4 * 4 + c4
                    ins = t.transpose(bank[:, c4 * 128:(c4 + 1) * 128],
                                      SQS[:, c * 128:(c + 1) * 128],
                                      ident[:, :])
                ins.then_inc(MM.h, 1)
                MM.v += 1
            G[(tag, "tmm", tt, g4)] = (MM, MM.v)

            @block.scalar
            def _(s, g4=g4, bank=bank):
                guard(s, (tag, "tmm", tt, g4))
                s.copy(fmbuf.rearrange("p (c t) -> p c t", c=NC)
                       [:, g4 * 4:g4 * 4 + 4, tt * 128:(tt + 1) * 128],
                       bank[:, 0:512].rearrange("p (c t) -> p c t", c=4)[:, :, :]
                       ).then_inc(AC.h, 1)
                AC.v += 1
            G[(tag, "tb", tt, g4)] = (AC, AC.v)
        G[(tag, "tr", tt)] = G[(tag, "tmm", tt, 3)]
        G[(tag, "sqfree", tt)] = G[(tag, "tmm", tt, 3)]

    SQS = arbf(16384, D)
    for tt in range(NT):
        xa = arf32(0 if tt % 2 == 0 else 8192, D)

        @block.sync
        def _(q, tt=tt, xa=xa):
            guard(q, ("l0", "xfree", tt - 2))
            dq(q, LD, xa[:, :], x_in[tt * 128:(tt + 1) * 128, :])
        G[("l0", "ld", tt)] = (LD, LD.v)

        G[("l0", "applyfree", tt)] = G.get(("l0", "tr", tt - 1))
        ln_stats_apply("l0", tt, xa, SQS)
        G[("l0", "hn", tt)] = G[("l0", "h", tt)]
        G[("l0", "xfree", tt)] = G[("l0", "h", tt)]
        transpose_to_fm("l0", tt)
    ph0_ac = AC.v
    G["xn_fm"] = (AC, ph0_ac)
    if "xnf_d" in dbg:
        @block.sync
        def _(q):
            q.wait_ge(AC.h, ph0_ac)
            dq(q, ST, xnf_d.rearrange("(c p) t -> p c t", p=128),
               fmbuf.rearrange("p (c t) -> p c t", c=NC))
    if upto == 0:
        return emit_tail()

    WCOL = [arbf(20480, NC * 128), arbf(28672, NC * 128)]
    EVST = [arf32(36864, 512), arf32(38912, 512)]
    PSTG = [arbf(40960, 512), arbf(41984, 512)]
    KT1 = arf32(43008, 512)
    KT2 = arf32(45056, 512)
    RCOS = arf32(49152, T)
    RSIN = arf32(57344, T)

    def load_wcol(g, dst, wsrc, col0):
        src3 = wsrc[:, col0:col0 + 128].rearrange("(c p) j -> p c j", p=128)
        dst3 = dst.rearrange("p (c j) -> p c j", c=NC)
        for hh in range(4):
            dq(g, WT, dst3[:, hh * 4:(hh + 1) * 4, :],
               src3[:, hh * 4:(hh + 1) * 4, :])

    def emit_fm_proj(name, wsrc, col_base, bias_slot, rope_flag, sink,
                     first_waits):
        for do in range(NC):
            wb = WCOL[do % 2]

            @block.gpsimd
            def _(g, do=do, wb=wb):
                guard(g, (name, "mmdone", do - 2))
                if do < 2:
                    for w_ in first_waits:
                        guard(g, w_)
                load_wcol(g, wb, wsrc, col_base + do * 128)
            G[(name, "wt", do)] = (WT, WT.v)

            for tc in range(NTC):
                bank = pb[tc % 2]
                idx = do * NTC + tc

                @block.tensor
                def _(t, do=do, tc=tc, wb=wb, bank=bank, idx=idx):
                    guard(t, (name, "wt", do))
                    if idx == 0:
                        for w_ in first_waits:
                            guard(t, w_)
                    guard(t, (name, "ev", idx - 2))
                    for c in range(NC):
                        ins = t.matmul(bank[:, :], wb[:, c * 128:(c + 1) * 128],
                                       fmbuf[:, c * T + tc * 512:
                                             c * T + (tc + 1) * 512],
                                       start=(c == 0),
                                       stop=(bias_slot is None and c == NC - 1))
                    if bias_slot is not None:
                        ins = t.matmul(bank[:, :],
                                       browR[0:1, bias_slot * D + do * 128:
                                             bias_slot * D + (do + 1) * 128],
                                       onesr[0:1, :], start=False, stop=True)
                    ins.then_inc(MM.h, 1)
                    MM.v += 1
                G[(name, "mm", idx)] = (MM, MM.v)

                stg = PSTG[idx % 2]
                if not rope_flag:
                    @block.scalar
                    def _(s, bank=bank, idx=idx, stg=stg):
                        guard(s, (name, "mm", idx), (name, "sink", idx - 2))
                        if idx < 2:
                            for w_ in first_waits:
                                guard(s, w_)
                        s.copy(stg[:, :], bank[:, :]).then_inc(AC.h, 1)
                        AC.v += 1
                    G[(name, "ev", idx)] = (AC, AC.v)
                    G[(name, "stg", idx)] = (AC, AC.v)
                else:
                    @block.scalar
                    def _(s, bank=bank, idx=idx):
                        guard(s, (name, "mm", idx), (name, "rope", idx - 1))
                        if idx < 2:
                            for w_ in first_waits:
                                guard(s, w_)
                        s.copy(EVST[0][:, :], bank[:, :]).then_inc(AC.h, 1)
                        AC.v += 1
                    G[(name, "ev", idx)] = (AC, AC.v)
                    ac_here = AC.v

                    @block.vector
                    def _(v, tc=tc, idx=idx, stg=stg, ac_here=ac_here):
                        v.wait_ge(AC.h, ac_here)
                        guard(v, (name, "sink", idx - 2))
                        cs_ = RCOS[:, tc * 512:(tc + 1) * 512]
                        sn_ = RSIN[:, tc * 512:(tc + 1) * 512]
                        ev = EVST[0]
                        v.tensor_tensor(KT1[:, :], ev[:, :], cs_, ALU.mult
                                        ).then_inc(EV.h, 1); EV.v += 1
                        v.tensor_copy(KT2[0:64, :], ev[64:128, :]
                                      ).then_inc(EV.h, 1); EV.v += 1
                        v.tensor_copy(KT2[64:128, :], ev[0:64, :]
                                      ).then_inc(EV.h, 1); EV.v += 1
                        v.wait_ge(EV.h, EV.v)
                        v.tensor_tensor(KT2[:, :], KT2[:, :], sn_, ALU.mult
                                        ).then_inc(EV.h, 1); EV.v += 1
                        v.wait_ge(EV.h, EV.v)
                        v.tensor_tensor(stg[:, :], KT1[:, :], KT2[:, :],
                                        ALU.add).then_inc(EV.h, 1)
                        EV.v += 1
                    G[(name, "rope", idx)] = (EV, EV.v)
                    G[(name, "stg", idx)] = (EV, EV.v)

                sink(do, tc, stg, (name, "stg", idx), (name, "sink", idx))
            G[(name, "mmdone", do)] = (MM, MM.v)
        G[(name, "alldone")] = (MM, MM.v)
        sk = G.get((name, "sink", NC * NTC - 1))
        if sk is not None:
            G[(name, "sinkdone")] = sk

    def sink_dram(dst):
        def f(do, tc, stg, ready_key, done_key):
            @block.sync
            def _(q, do=do, tc=tc, stg=stg):
                guard(q, ready_key)
                dq(q, ST, dst[do * 128:(do + 1) * 128, tc * 512:(tc + 1) * 512],
                   stg[:, :])
            G[done_key] = (ST, ST.v)
        return f

    def sink_shared(dst_sh):
        def f(do, tc, stg, ready_key, done_key):
            @block.gpsimd
            def _(g, do=do, tc=tc, stg=stg):
                guard(g, ready_key)
                slot_dma(g,
                         dst_sh[0, do * 128:(do + 1) * 128,
                                tc * 512:(tc + 1) * 512],
                         dst_sh[1, do * 128:(do + 1) * 128,
                                tc * 512:(tc + 1) * 512],
                         stg[:, :], GS, f"sk{do}_{tc}")
            G[done_key] = (GS, GS.v)
        return f

    if upto == 7:
        emit_fm_proj("Kout", w_qkv, D, None, False, sink_dram(q_out_d),
                     [("xn_fm",), ("gp_init",)])
        return emit_tail()
    emit_fm_proj("Kout", w_qkv, D, None, False, sink_shared(k_out_sh),
                 [("xn_fm",), ("gp_init",)])
    gp_kout = GS.v
    if upto == 8:
        return emit_tail()
    emit_fm_proj("Qout", w_qkv, 0, None, False, sink_dram(q_out_d),
                 [("Kout", "alldone"), ("Kout", "sinkdone")])
    emit_fm_proj("Vout", w_qkv, 2 * D, None, False, sink_dram(v_out_d),
                 [("Qout", "alldone"), ("Qout", "sinkdone")])
    G["qv_st"] = (ST, ST.v)
    if upto == 9:
        return emit_tail()
    _, brk = barrier(1, GS, gp_kout)
    G["br_kout"] = brk
    if upto == 1:
        return emit_tail()

    KH2 = [arbf(20480, S), arbf(28672, S)]
    QH2 = [arbf(36864, T), arbf(40960, T)]
    PR2 = [arbf(49152, S), arbf(57344, S)]
    CS32 = arbf(45056, 1024)
    VST2 = [arbf(65536, 512), arbf(66560, 512)]

    for h in (range(0) if skipo else range(H)):
        kh, qh = KH2[h % 2], QH2[h % 2]

        @block.sync
        def _(q, h=h, kh=kh, qh=qh):
            guard(q, ("oatt_mm", h - 2))
            if h < 2:
                guard(q, ("br_kout",), ("Vout", "alldone"),
                      ("Qout", "sink", NC * NTC - 1),
                      ("Vout", "sink", NC * NTC - 1))
            dq(q, LD, kh[:, 0:T], k_out_sh[0, h * 128:(h + 1) * 128, :])
            dq(q, LD, kh[:, T:S], k_out_sh[1, h * 128:(h + 1) * 128, :])
            dq(q, LD, qh[:, :], q_out_d[h * 128:(h + 1) * 128, :])
        G[("oatt_ld", h)] = (LD, LD.v)

        for qt in range(NT):
            prb = PR2[qt % 2]
            for kc in range(8):
                bank = pb[kc % 2]

                @block.tensor
                def _(t, h=h, qt=qt, kc=kc, bank=bank, kh=kh, qh=qh):
                    if qt == 0 and kc == 0:
                        guard(t, ("oatt_ld", h))
                    guard(t, ("oexp", h, qt, kc - 2))
                    t.matmul(bank[:, :], qh[:, qt * 128:(qt + 1) * 128],
                             kh[:, kc * 512:(kc + 1) * 512],
                             start=True, stop=True).then_inc(MM.h, 1)
                    MM.v += 1
                G[("osc", h, qt, kc)] = (MM, MM.v)

                @block.scalar
                def _(s, h=h, qt=qt, kc=kc, bank=bank, prb=prb):
                    guard(s, ("osc", h, qt, kc), ("ocs", h, qt - 2),
                          ("gp_init",))
                    s.activation(prb[:, kc * 512:(kc + 1) * 512], bank[:, :],
                                 AF.Exp, bias=nco[:, :], scale=ISQ,
                                 accum_out=zcol[:, kc:kc + 1]).then_inc(AC.h, 1)
                    AC.v += 1
                G[("oexp", h, qt, kc)] = (AC, AC.v)

            @block.vector
            def _(v, h=h, qt=qt):
                guard(v, ("oexp", h, qt, 7))
                v.reduce_sum(stats[:, 8:9], zcol[:, :], axis=AXX
                             ).then_inc(EV.h, 1)
                EV.v += 1
                v.wait_ge(EV.h, EV.v)
                v.reciprocal(invz[:, :], stats[:, 8:9]).then_inc(EV.h, 1)
                EV.v += 1
                v.wait_ge(EV.h, EV.v)
                v.tensor_scalar_mul(invzb[:, :], ones32[:, :], invz[:, :]
                                    ).then_inc(EV.h, 1)
                EV.v += 1
            G[("oiz", h, qt)] = (EV, EV.v)

            @block.tensor
            def _(t, h=h, qt=qt, prb=prb):
                guard(t, ("oiz", h, qt))
                if qt == 0:
                    guard(t, ("ocse", h - 1, 7))
                for kc in range(8):
                    pcs = pb[2 + kc // 4]
                    base = (kc % 4) * 32
                    ins = t.matmul(pcs[base:base + 32, :], invzb[:, :],
                                   prb[:, kc * 512:(kc + 1) * 512],
                                   start=(qt == 0), stop=(qt == NT - 1),
                                   tile_position=(0, base))
                ins.then_inc(MM.h, 1)
                MM.v += 1
            G[("ocs", h, qt)] = (MM, MM.v)
        G[("oatt_mm", h)] = (MM, MM.v)

        for kc in range(8):
            @block.vector
            def _(v, h=h, kc=kc):
                if kc == 0:
                    guard(v, ("oatt_mm", h))
                pcs = pb[2 + kc // 4]
                base = (kc % 4) * 32
                guard(v, ("ocst", h, kc - 1))
                v.tensor_copy(CS32[0:32, 0:512], pcs[base:base + 32, :]
                              ).then_inc(EV.h, 1)
                EV.v += 1
            G[("ocse", h, kc)] = (EV, EV.v)

            @block.gpsimd
            def _(g, h=h, kc=kc):
                guard(g, ("ocse", h, kc))
                slot_dma(g, cs_sh[0, h:h + 1, kc * 512:(kc + 1) * 512],
                         cs_sh[1, h:h + 1, kc * 512:(kc + 1) * 512],
                         CS32[0:1, 0:512], GS, f"cst{h}_{kc}")
            G[("ocst", h, kc)] = (GS, GS.v)
    gp_csall = GS.v
    _, brc = barrier(2, GS, gp_csall)
    G["br_cs"] = brc
    if upto == 2:
        return emit_tail()

    for c in range(NC):
        @block.gpsimd
        def _(g, c=c):
            if c == 0:
                guard(g, ("br_cs",))
            guard(g, ("valdone", c - 1))
            g.wait_ge(GS.h, GS.v)
            slot = get_reg(g, "slotreg")
            g.reg_load(slot, slotv[0:1, 0:1])
            with g.If_eq(slot, 0):
                g.dma_start(out=csr0[0:1, :], in_=cs_sh[0, c:c + 1, 0:T]
                            ).then_inc(GS.h, 16)
                g.dma_start(out=csr1[0:1, :], in_=cs_sh[1, c:c + 1, 0:T]
                            ).then_inc(GS.h, 16)
            with g.Else():
                g.dma_start(out=csr0[0:1, :], in_=cs_sh[0, c:c + 1, T:S]
                            ).then_inc(GS.h, 16)
                g.dma_start(out=csr1[0:1, :], in_=cs_sh[1, c:c + 1, T:S]
                            ).then_inc(GS.h, 16)
            GS.v += 32
        G[("csrld", c)] = (GS, GS.v)

        for tc in range(NTC):
            idx = c * NTC + tc
            bankb = pb[4 + idx % 2]

            @block.tensor
            def _(t, c=c, tc=tc, bankb=bankb, idx=idx):
                guard(t, ("csrld", c), ("vbe", idx - 2))
                t.matmul(bankb[:, :], onesr[0:1, 0:128],
                         csr0[0:1, tc * 512:(tc + 1) * 512],
                         start=True, stop=False)
                t.matmul(bankb[:, :], onesr[0:1, 0:128],
                         csr1[0:1, tc * 512:(tc + 1) * 512],
                         start=False, stop=True).then_inc(MM.h, 1)
                MM.v += 1
            G[("vbmm", idx)] = (MM, MM.v)

            @block.sync
            def _(q, c=c, tc=tc, idx=idx):
                guard(q, ("vbe", idx - 2))
                dq(q, LD, VST2[idx % 2][:, :],
                   v_out_d[c * 128:(c + 1) * 128, tc * 512:(tc + 1) * 512])
            G[("vld", idx)] = (LD, LD.v)

            @block.vector
            def _(v, c=c, tc=tc, bankb=bankb, idx=idx):
                guard(v, ("vbmm", idx), ("vld", idx))
                v.tensor_copy(invzf[:, :], bankb[:, :]).then_inc(EV.h, 1)
                EV.v += 1
                v.wait_ge(EV.h, EV.v)
                v.tensor_tensor(fmbuf[:, c * T + tc * 512:c * T + (tc + 1) * 512],
                                VST2[idx % 2][:, :], invzf[:, :], ALU.mult
                                ).then_inc(EV.h, 1)
                EV.v += 1
            G[("vbe", idx)] = (EV, EV.v)
        G[("valdone", c)] = (EV, EV.v)
    G["values"] = (EV, EV.v)

    WRHS = [arbf(20480, NC * 512), arbf(49152, NC * 512)]
    XS = [arf32(65536, 512), arf32(67584, 512)]
    YS = [arf32(69632, 512), arf32(71680, 512)]

    def emit_tm_proj(name, wsrc, bias_slot, writer, first_waits):
        for do in range(4):
            wr = WRHS[do % 2]

            @block.gpsimd
            def _(g, do=do, wr=wr):
                guard(g, (name, "mmdone", do - 2))
                if do < 2:
                    for w_ in first_waits:
                        guard(g, w_)
                src3 = wsrc[:, do * 512:(do + 1) * 512].rearrange(
                    "(c p) j -> p c j", p=128)
                dst3 = wr.rearrange("p (c j) -> p c j", c=NC)
                for hh in range(4):
                    dq(g, WT, dst3[:, hh * 4:(hh + 1) * 4, :],
                       src3[:, hh * 4:(hh + 1) * 4, :])
            G[(name, "wt", do)] = (WT, WT.v)

            for tt in range(NT):
                bank = pb[tt % 2]
                idx = do * NT + tt

                @block.tensor
                def _(t, do=do, tt=tt, wr=wr, bank=bank, idx=idx):
                    guard(t, (name, "wt", do))
                    if idx == 0:
                        for w_ in first_waits:
                            guard(t, w_)
                    guard(t, (name, "pev", idx - 2))
                    for c in range(NC):
                        ins = t.matmul(bank[:, :],
                                       fmbuf[:, c * T + tt * 128:
                                             c * T + (tt + 1) * 128],
                                       wr[:, c * 512:(c + 1) * 512],
                                       start=(c == 0),
                                       stop=(bias_slot is None and c == NC - 1))
                    if bias_slot is not None:
                        ins = t.matmul(bank[:, :], onesr[0:1, 0:128],
                                       browR[0:1, bias_slot * D + do * 512:
                                             bias_slot * D + (do + 1) * 512],
                                       start=False, stop=True)
                    ins.then_inc(MM.h, 1)
                    MM.v += 1
                G[(name, "mm", idx)] = (MM, MM.v)
                writer(name, do, tt, bank, idx)
            G[(name, "mmdone", do)] = (MM, MM.v)
        G[(name, "alldone")] = (MM, MM.v)
        sk = G.get((name, "sink", NC * NTC - 1))
        if sk is not None:
            G[(name, "sinkdone")] = sk

    def write_resid(src_d, dst_d, entry_key):
        def f(name, do, tt, bank, idx):
            @block.sync
            def _(q, do=do, tt=tt, idx=idx):
                guard(q, (name, "radd", idx - 2))
                if idx < 2:
                    guard(q, entry_key)
                dq(q, LD, XS[idx % 2][:, :],
                   src_d[tt * 128:(tt + 1) * 128, do * 512:(do + 1) * 512])
            G[(name, "rld", idx)] = (LD, LD.v)

            @block.vector
            def _(v, do=do, tt=tt, bank=bank, idx=idx):
                guard(v, (name, "mm", idx), (name, "rld", idx),
                      (name, "rst", idx - 2))
                v.tensor_add(YS[idx % 2][:, :], bank[:, :], XS[idx % 2][:, :]
                             ).then_inc(EV.h, 1)
                EV.v += 1
            G[(name, "radd", idx)] = (EV, EV.v)
            G[(name, "pev", idx)] = (EV, EV.v)

            @block.sync
            def _(q, do=do, tt=tt, idx=idx):
                guard(q, (name, "radd", idx))
                dq(q, ST, dst_d[tt * 128:(tt + 1) * 128, do * 512:(do + 1) * 512],
                   YS[idx % 2][:, :])
            G[(name, "rst", idx)] = (ST, ST.v)
        return f

    emit_tm_proj("oproj", w_o, None, write_resid(x_in, x1_d, ("values",)), [("values",)])
    G["x1_done"] = (ST, ST.v)
    if upto == 3:
        return emit_tail()

    @block.gpsimd
    def _(g):
        guard(g, ("x1_done",))
        dq(g, WT, browR[0:1, 0:D], brow_in[4:5, :])
        dq(g, WT, browR[0:1, D:2 * D], brow_in[5:6, :])
    G["g1rows"] = (WT, WT.v)

    G1T = arbf(36864, D)
    B1T = arbf(40960, D)

    def build_bcast_tiles(key, row_offs, dsts, waitkey):
        for r in range(2):
            for ch in range(4):
                @block.tensor
                def _(t, r=r, ch=ch):
                    if r == 0 and ch == 0:
                        guard(t, waitkey)
                    guard(t, (key, "ev", r, ch - 2),
                          (key, "ev", r - 1, ch + 2))
                    t.matmul(pb[4 + ch % 2][:, :], onesr[0:1, 0:128],
                             browR[0:1, row_offs[r] + ch * 512:
                                   row_offs[r] + (ch + 1) * 512],
                             start=True, stop=True).then_inc(MM.h, 1)
                    MM.v += 1
                G[(key, "mm", r, ch)] = (MM, MM.v)

                @block.vector
                def _(v, r=r, ch=ch):
                    guard(v, (key, "mm", r, ch))
                    v.tensor_copy(dsts[r][:, ch * 512:(ch + 1) * 512],
                                  pb[4 + ch % 2][:, :]).then_inc(EV.h, 1)
                    EV.v += 1
                G[(key, "ev", r, ch)] = (EV, EV.v)
        G[(key, "done")] = G[(key, "ev", 1, 3)]

    build_bcast_tiles("gb1", [0, D], [G1T, B1T], ("g1rows",))

    TMC = arf32(73728, D)

    def emit_ln_stack(tag, src_d, store_hd, gt, bt, extra_first):
        for tt in range(NT):
            xa = arf32(0 if tt % 2 == 0 else 8192, D)

            @block.sync
            def _(q, tt=tt, xa=xa):
                guard(q, (tag, "xfree", tt - 2))
                if tt < 2:
                    for w_ in extra_first:
                        guard(q, w_)
                dq(q, LD, xa[:, :], src_d[tt * 128:(tt + 1) * 128, :])
            G[(tag, "ld", tt)] = (LD, LD.v)

            G[(tag, "applyfree", tt)] = G.get((tag, "st1", tt - 1))
            ln_stats_apply(tag, tt, xa, TMC, accum_col=9)
            G[(tag, "xfree", tt)] = G[(tag, "h", tt)]

            if store_hd is not None:
                @block.sync
                def _(q, tt=tt):
                    guard(q, (tag, "h", tt))
                    dq(q, ST, store_hd[tt * 128:(tt + 1) * 128, :], TMC[:, :])
                G[(tag, "hst", tt)] = (ST, ST.v)
                G[(tag, "st1", tt)] = (ST, ST.v)
            else:
                G[(tag, "st1", tt)] = G[(tag, "h", tt)]

            @block.scalar
            def _(s, tt=tt):
                guard(s, (tag, "h", tt), (tag, "sqfree2", tt - 1),
                      (tag, "var2", tt - 1))
                s.activation(SQS[:, :], TMC[:, :], AF.Square,
                             accum_out=stats[:, 10:11]).then_inc(AC.h, 1)
                AC.v += 1
            G[(tag, "sq2", tt)] = (AC, AC.v)

            @block.vector
            def _(v, tt=tt):
                guard(v, (tag, "sq2", tt))
                v.tensor_scalar_mul(stats[:, 12:13], stats[:, 9:10], 1.0 / D
                                    ).then_inc(EV.h, 1); EV.v += 1
                v.tensor_scalar_mul(stats[:, 13:14], stats[:, 10:11], 1.0 / D
                                    ).then_inc(EV.h, 1); EV.v += 1
                v.wait_ge(EV.h, EV.v)
                v.tensor_tensor(stats[:, 14:15], stats[:, 12:13],
                                stats[:, 12:13], ALU.mult).then_inc(EV.h, 1)
                EV.v += 1
                v.wait_ge(EV.h, EV.v)
                v.tensor_tensor(stats[:, 15:16], stats[:, 13:14],
                                stats[:, 14:15], ALU.subtract).then_inc(EV.h, 1)
                EV.v += 1
            G[(tag, "var2", tt)] = (EV, EV.v)

            @block.scalar
            def _(s, tt=tt):
                guard(s, (tag, "var2", tt), ("gp_init",))
                s.activation(stats[:, 16:17], stats[:, 15:16], AF.Sqrt,
                             bias=epsc[:, :]).then_inc(AC.h, 1)
                AC.v += 1
            G[(tag, "std2", tt)] = (AC, AC.v)

            @block.vector
            def _(v, tt=tt, gt=gt, bt=bt):
                guard(v, (tag, "std2", tt))
                v.reciprocal(stats[:, 17:18], stats[:, 16:17]).then_inc(EV.h, 1)
                EV.v += 1
                v.wait_ge(EV.h, EV.v)
                guard(v, (tag, "sqfree", tt - 1), (tag, "st1", tt))
                v.tensor_scalar(TMC[:, :], TMC[:, :], stats[:, 12:13],
                                stats[:, 17:18], ALU.subtract, ALU.mult
                                ).then_inc(EV.h, 1); EV.v += 1
                v.wait_ge(EV.h, EV.v)
                v.tensor_tensor(TMC[:, :], TMC[:, :], gt[:, :], ALU.mult
                                ).then_inc(EV.h, 1); EV.v += 1
                v.wait_ge(EV.h, EV.v)
                v.tensor_tensor(SQS[:, :], TMC[:, :], bt[:, :], ALU.add
                                ).then_inc(EV.h, 1)
                EV.v += 1
            G[(tag, "hn", tt)] = (EV, EV.v)
            transpose_to_fm(tag, tt)
            G[(tag, "sqfree2", tt)] = G[(tag, "sqfree", tt)]
        G[(tag, "fmdone")] = (AC, AC.v)

    emit_ln_stack("l1", x1_d, hin_d, G1T, B1T, [("gb1", "done")])
    G["hn1_fm"] = G[("l1", "fmdone")]
    if "hn1f_d" in dbg:
        @block.sync
        def _(q):
            guard(q, ("hn1_fm",))
            dq(q, ST, hn1f_d.rearrange("(c p) t -> p c t", p=128),
               fmbuf.rearrange("p (c t) -> p c t", c=NC))
    if upto == 35:
        return emit_tail()

    @block.gpsimd
    def _(g):
        guard(g, ("gb1", "done"), ("hn1_fm",))
        for r in range(4):
            dq(g, WT, browR[0:1, r * D:(r + 1) * D], brow_in[r:r + 1, :])
    G["biasrows"] = (WT, WT.v)

    @block.sync
    def _(q):
        guard(q, ("hn1_fm",))
        dq(q, LD, RCOS[:, :], cos_in[:, :])
        dq(q, LD, RSIN[:, :], sin_in[:, :])
    G["ropeld"] = (LD, LD.v)

    emit_fm_proj("Kin", wk, 0, 1, True, sink_shared(k_in_sh),
                 [("hn1_fm",), ("ropeld",), ("biasrows",)])
    gp_kin = GS.v
    emit_fm_proj("Qin", wq, 0, 0, True, sink_dram(qin_d),
                 [("Kin", "alldone"), ("Kin", "sinkdone")])

    def write_vin(name, do, tt, bank, idx):
        @block.vector
        def _(v, do=do, tt=tt, bank=bank, idx=idx):
            guard(v, (name, "mm", idx), (name, "vst", idx - 2))
            if idx < 2:
                guard(v, ("Qin", "sink", NC * NTC - 1))
            v.tensor_copy(PSTG[idx % 2][:, :], bank[:, :]).then_inc(EV.h, 1)
            EV.v += 1
        G[(name, "vev", idx)] = (EV, EV.v)
        G[(name, "pev", idx)] = (EV, EV.v)

        @block.gpsimd
        def _(g, do=do, tt=tt, idx=idx):
            guard(g, (name, "vev", idx))
            src3 = PSTG[idx % 2].rearrange("p (h d) -> p h d", h=4)
            slot_dma(g, v_in_sh[0, 4 * do:4 * do + 4,
                                tt * 128:(tt + 1) * 128, :]
                     .rearrange("h t d -> t h d"),
                     v_in_sh[1, 4 * do:4 * do + 4,
                             tt * 128:(tt + 1) * 128, :]
                     .rearrange("h t d -> t h d"),
                     src3, GS, f"vi{idx}")
        G[(name, "vst", idx)] = (GS, GS.v)

    emit_tm_proj("vin", wv, 2, write_vin, [("Qin", "alldone"), ("biasrows",)])
    gp_vin = GS.v
    _, brkv = barrier(3, GS, max(gp_kin, gp_vin))
    G["br_kv"] = brkv
    if upto == 4:
        return emit_tail()

    KH5 = [arbf(20480, S), arbf(28672, S)]
    VH5 = [arbf(57344, S), arbf(65536, S)]
    QH5 = [arbf(49152, T), arbf(53248, T)]
    MSK = arbf(73728, 8 * 512)
    PB5 = [arbf(81920, 512), arbf(82944, 512)]

    @block.gpsimd
    def _(g):
        guard(g, ("br_kv",), ("Kin", "alldone"), ("Qin", "alldone"))
        dq(g, GS, MSK.rearrange("p (m f) -> p m f", m=8),
           masks_in.rearrange("m p f -> p m f"))
    G["mskld"] = (GS, GS.v)

    for h in range(H):
        kh, vh, qh = KH5[h % 2], VH5[h % 2], QH5[h % 2]

        @block.sync
        def _(q, h=h, kh=kh, vh=vh, qh=qh):
            guard(q, ("iatt_mm", h - 2))
            if h < 2:
                guard(q, ("mskld",), ("vin", "alldone"),
                      ("vin", "pev", 4 * NT - 1), ("vin", "vst", 4 * NT - 1),
                      ("Qin", "sink", NC * NTC - 1))
            dq(q, LD, kh[:, 0:T], k_in_sh[0, h * 128:(h + 1) * 128, :])
            dq(q, LD, kh[:, T:S], k_in_sh[1, h * 128:(h + 1) * 128, :])
            dq(q, LD, qh[:, :], qin_d[h * 128:(h + 1) * 128, :])
            for half in range(2):
                dq(q, LD, vh.rearrange("p (kt d) -> p kt d", kt=2 * NT)
                   [:, half * NT:(half + 1) * NT, :],
                   v_in_sh[half, h].rearrange("(kt p) d -> p kt d", p=128))
        G[("iatt_ld", h)] = (LD, LD.v)

        for qc in range(NTC):
            blocks = [(sh_, j) for sh_ in range(2) for j in range(4 * qc + 4)]
            nblk = len(blocks)
            for bi, (sh_, j) in enumerate(blocks):
                bank = pb[bi % 2]
                pbuf = PB5[bi % 2]
                jrel = j - 4 * qc
                mask_id = (sh_ * 4 + jrel) if jrel >= 0 else None

                @block.tensor
                def _(t, h=h, qc=qc, sh_=sh_, j=j, bi=bi, bank=bank, kh=kh,
                      qh=qh):
                    if qc == 0 and bi == 0:
                        guard(t, ("iatt_ld", h))
                    guard(t, ("iexp", h, qc, bi - 2))
                    t.matmul(bank[:, :],
                             kh[:, (sh_ * NT + j) * 128:(sh_ * NT + j + 1) * 128],
                             qh[:, qc * 512:(qc + 1) * 512],
                             start=True, stop=True).then_inc(MM.h, 1)
                    MM.v += 1
                G[("isc", h, qc, bi)] = (MM, MM.v)

                @block.scalar
                def _(s, h=h, qc=qc, bi=bi, bank=bank, pbuf=pbuf):
                    guard(s, ("isc", h, qc, bi), ("ipu", h, qc, bi - 2),
                          ("gp_init",))
                    s.activation(pbuf[:, :], bank[:, :], AF.Exp,
                                 bias=nci[:, :], scale=ISQ).then_inc(AC.h, 1)
                    AC.v += 1
                G[("iexp", h, qc, bi)] = (AC, AC.v)

                dep = ("iexp", h, qc, bi)
                if mask_id is not None:
                    @block.vector
                    def _(v, h=h, qc=qc, bi=bi, pbuf=pbuf, mask_id=mask_id):
                        guard(v, ("iexp", h, qc, bi))
                        v.tensor_tensor(pbuf[:, :], pbuf[:, :],
                                        MSK[:, mask_id * 512:(mask_id + 1) * 512],
                                        ALU.mult).then_inc(EV.h, 1)
                        EV.v += 1
                    G[("imsk", h, qc, bi)] = (EV, EV.v)
                    dep = ("imsk", h, qc, bi)

                @block.tensor
                def _(t, h=h, qc=qc, sh_=sh_, j=j, bi=bi, pbuf=pbuf, vh=vh,
                      dep=dep, nblk=nblk):
                    guard(t, dep)
                    if bi == 0:
                        guard(t, ("izav", h, qc - 1))
                    t.matmul(pb[2][:, :], ones128[:, :], pbuf[:, :],
                             start=(bi == 0), stop=(bi == nblk - 1))
                    t.matmul(pb[3][:, :],
                             vh[:, (sh_ * NT + j) * 128:(sh_ * NT + j + 1) * 128],
                             pbuf[:, :], start=(bi == 0), stop=(bi == nblk - 1)
                             ).then_inc(MM.h, 1)
                    MM.v += 1
                G[("ipu", h, qc, bi)] = (MM, MM.v)
            G[("iblk", h, qc)] = (MM, MM.v)

            @block.vector
            def _(v, h=h, qc=qc):
                guard(v, ("iblk", h, qc))
                v.reciprocal(invzf[:, :], pb[2][:, :]).then_inc(EV.h, 1)
                EV.v += 1
                v.wait_ge(EV.h, EV.v)
                v.tensor_tensor(fmbuf[:, h * T + qc * 512:h * T + (qc + 1) * 512],
                                pb[3][:, :], invzf[:, :], ALU.mult
                                ).then_inc(EV.h, 1)
                EV.v += 1
            G[("izav", h, qc)] = (EV, EV.v)
        G[("iatt_mm", h)] = (MM, MM.v)
        G[("iatt_ev", h)] = (EV, EV.v)
    G["att_fm"] = (EV, EV.v)
    if upto == 5:
        return emit_tail()

    emit_tm_proj("aproj", wo2, 3, write_resid(hin_d, h2_d, ("att_fm",)), [("att_fm",)])
    G["h2_done"] = (ST, ST.v)

    @block.gpsimd
    def _(g):
        guard(g, ("h2_done",), ("aproj", "alldone"))
        dq(g, WT, browR[0:1, 0:D], brow_in[6:7, :])
        dq(g, WT, browR[0:1, D:2 * D], brow_in[7:8, :])
    G["g2rows"] = (WT, WT.v)

    G2T = arbf(36864, D)
    B2T = arbf(40960, D)
    build_bcast_tiles("gb2", [0, D], [G2T, B2T], ("g2rows",))
    emit_ln_stack("l2", h2_d, None, G2T, B2T, [("gb2", "done")])
    G["hn2_fm"] = G[("l2", "fmdone")]
    if "hn2f_d" in dbg:
        @block.sync
        def _(q):
            guard(q, ("hn2_fm",))
            dq(q, ST, hn2f_d.rearrange("(c p) t -> p c t", p=128),
               fmbuf.rearrange("p (c t) -> p c t", c=NC))
    if upto == 6:
        return emit_tail()

    WGB = arbf(84992, NC * 4)

    @block.gpsimd
    def _(g):
        guard(g, ("hn2_fm",))
        dq(g, WT, WGB.rearrange("p (c j) -> p c j", c=NC),
           wg_in.rearrange("(c p) j -> p c j", p=128))
    G["wtg"] = (WT, WT.v)

    for tt in range(NT):
        @block.tensor
        def _(t, tt=tt):
            guard(t, ("wtg",), ("gdone", tt - 1))
            for c in range(NC):
                ins = t.matmul(pb[4][:, 0:4],
                               fmbuf[:, c * T + tt * 128:c * T + (tt + 1) * 128],
                               WGB[:, c * 4:(c + 1) * 4],
                               start=(c == 0), stop=(c == NC - 1))
            ins.then_inc(MM.h, 1)
            MM.v += 1
        G[("gmm", tt)] = (MM, MM.v)

        @block.vector
        def _(v, tt=tt):
            guard(v, ("gmm", tt))
            v.tensor_copy(graw[:, :], pb[4][:, 0:4]).then_inc(EV.h, 1)
            EV.v += 1
            v.wait_ge(EV.h, EV.v)
            v.reduce_max(gst[:, 0:1], graw[:, :], axis=AXX, negate=True
                         ).then_inc(EV.h, 1)
            EV.v += 1
        G[("gnm", tt)] = (EV, EV.v)

        @block.scalar
        def _(s, tt=tt):
            guard(s, ("gnm", tt))
            s.activation(graw[:, :], graw[:, :], AF.Exp, bias=gst[:, 0:1],
                         accum_out=gst[:, 1:2]).then_inc(AC.h, 1)
            AC.v += 1
        G[("gexp", tt)] = (AC, AC.v)

        @block.vector
        def _(v, tt=tt):
            guard(v, ("gexp", tt))
            v.reciprocal(gst[:, 2:3], gst[:, 1:2]).then_inc(EV.h, 1); EV.v += 1
            v.wait_ge(EV.h, EV.v)
            v.tensor_scalar_mul(graw[:, :], graw[:, :], gst[:, 2:3]
                                ).then_inc(EV.h, 1); EV.v += 1
            v.wait_ge(EV.h, EV.v)
            v.tensor_tensor(gst[:, 12:16], graw[:, :], tieb[:, :], ALU.add
                            ).then_inc(EV.h, 1); EV.v += 1
            v.wait_ge(EV.h, EV.v)
            v.reduce_max(gst[:, 3:4], gst[:, 12:16], axis=AXX).then_inc(EV.h, 1)
            EV.v += 1
            v.wait_ge(EV.h, EV.v)
            v.tensor_scalar(gst[:, 8:12], gst[:, 12:16], gst[:, 3:4], None,
                            ALU.is_equal).then_inc(EV.h, 1); EV.v += 1
            v.wait_ge(EV.h, EV.v)
            v.scalar_tensor_tensor(gst[:, 12:16], gst[:, 8:12], -2.0,
                                   gst[:, 12:16], ALU.mult, ALU.add
                                   ).then_inc(EV.h, 1); EV.v += 1
            v.wait_ge(EV.h, EV.v)
            v.reduce_max(gst[:, 4:5], gst[:, 12:16], axis=AXX).then_inc(EV.h, 1)
            EV.v += 1
            v.wait_ge(EV.h, EV.v)
            v.tensor_scalar(gst[:, 16:20], gst[:, 12:16], gst[:, 4:5], None,
                            ALU.is_equal).then_inc(EV.h, 1); EV.v += 1
            v.wait_ge(EV.h, EV.v)
            v.reduce_max(gst[:, 21:22], graw[:, :], axis=AXX).then_inc(EV.h, 1)
            EV.v += 1
            v.tensor_tensor(gst[:, 12:16], graw[:, :], gst[:, 16:20], ALU.mult
                            ).then_inc(EV.h, 1); EV.v += 1
            v.wait_ge(EV.h, EV.v)
            v.reduce_sum(gst[:, 22:23], gst[:, 12:16], axis=AXX).then_inc(EV.h, 1)
            EV.v += 1
        G[("gtop", tt)] = (EV, EV.v)

        @block.scalar
        def _(s, tt=tt):
            guard(s, ("gtop", tt))
            s.activation(gst[:, 5:7], gst[:, 21:23], AF.Exp,
                         accum_out=gst[:, 7:8]).then_inc(AC.h, 1)
            AC.v += 1
        G[("gexp2", tt)] = (AC, AC.v)

        @block.vector
        def _(v, tt=tt):
            guard(v, ("gexp2", tt))
            v.reciprocal(gst[:, 20:21], gst[:, 7:8]).then_inc(EV.h, 1)
            EV.v += 1
            v.wait_ge(EV.h, EV.v)
            v.tensor_scalar_mul(gst[:, 5:7], gst[:, 5:7], gst[:, 20:21]
                                ).then_inc(EV.h, 1); EV.v += 1
            v.wait_ge(EV.h, EV.v)
            v.tensor_scalar_mul(combf[:, :], gst[:, 8:12], gst[:, 5:6]
                                ).then_inc(EV.h, 1); EV.v += 1
            v.tensor_scalar_mul(gst[:, 16:20], gst[:, 16:20], gst[:, 6:7]
                                ).then_inc(EV.h, 1); EV.v += 1
            v.wait_ge(EV.h, EV.v)
            v.tensor_tensor(combf[:, :], combf[:, :], gst[:, 16:20], ALU.add
                            ).then_inc(EV.h, 1); EV.v += 1
            v.wait_ge(EV.h, EV.v)
            v.tensor_copy(combb[:, :], combf[:, :]).then_inc(EV.h, 1)
            EV.v += 1
        G[("gcomb", tt)] = (EV, EV.v)

        @block.tensor
        def _(t, tt=tt):
            guard(t, ("gcomb", tt), ("gtrev", tt - 1))
            if tt == 0:
                guard(t, ("l2", "tb", NT - 1, 2), ("l2", "tb", NT - 1, 3))
            t.transpose(ptr[0][0:4, 0:128], combb[:, :], ident[:, :]
                        ).then_inc(MM.h, 1)
            MM.v += 1
        G[("gtr", tt)] = (MM, MM.v)

        @block.vector
        def _(v, tt=tt):
            guard(v, ("gtr", tt))
            v.tensor_copy(combT[0:4, tt * 128:(tt + 1) * 128],
                          ptr[0][0:4, 0:128]).then_inc(EV.h, 1)
            EV.v += 1
        G[("gtrev", tt)] = (EV, EV.v)
        G[("gdone", tt)] = (EV, EV.v)
    G["gate_done"] = (EV, EV.v)
    if upto == 65:
        return emit_tail()

    X7 = [arf32(0, 512), arf32(2048, 512)]
    H7 = [arf32(4096, 512), arf32(6144, 512)]
    HMID = arbf(8192, NC * 1024)
    W1C = [arbf(40960, NC * 128), arbf(49152, NC * 128)]
    W2R = [arbf(57344, 8 * 512), arbf(65536, 8 * 512)]
    Y7 = [arf32(73728, 512), arf32(75776, 512)]
    GE7 = [arf32(77824, 512), arf32(79872, 512)]
    CB7 = arbf(81920, 2 * 512)
    BE1R = arbf(86016, 2048)

    w1i_cnt = [0]
    w2i_cnt = [0]
    y_cnt = [0]

    for half in range(2):
        tt0 = half * 8
        for e in range(n_experts):
            he = half * E + e

            @block.gpsimd
            def _(g, half=half, e=e, he=he):
                guard(g, ("be1free", he - 1), ("gate_done",))
                dq(g, WT, BE1R[0:1, :], be1_in[e:e + 1, :])
            G[("be1ld", he)] = (WT, WT.v)

            @block.tensor
            def _(t, half=half, e=e, he=he):
                guard(t, ("gate_done",), ("cbfree", he - 1))
                for tci in range(2):
                    tc = half * 2 + tci
                    guard(t, ("cbev", he, tci - 2))
                    t.matmul(pb[4 + tci % 2][:, :], sel4[0:4, e * 128:(e + 1) * 128],
                             combT[0:4, tc * 512:(tc + 1) * 512],
                             start=True, stop=True).then_inc(MM.h, 1)
                    MM.v += 1
                    G[("cbmm", he, tci)] = (MM, MM.v)

            @block.vector
            def _(v, he=he):
                for tci in range(2):
                    guard(v, ("cbmm", he, tci))
                    v.tensor_copy(CB7[:, tci * 512:(tci + 1) * 512],
                                  pb[4 + tci % 2][:, :]).then_inc(EV.h, 1)
                    EV.v += 1
                    G[("cbev", he, tci)] = (EV, EV.v)
            G[("cbdone", he)] = (EV, EV.v)
            if upto == 172 and he == 1:
                return emit_tail()

            for ic in range(NC):
                w1i = w1i_cnt[0]
                w1i_cnt[0] += 1
                wb = W1C[w1i % 2]

                @block.gpsimd
                def _(g, e=e, ic=ic, wb=wb, w1i=w1i):
                    guard(g, ("w1use", w1i - 2))
                    if w1i < 2:
                        guard(g, ("hn2_fm",), ("gate_done",))
                    load_wcol(g, wb, we1[e], ic * 128)
                G[("w1wt", w1i)] = (WT, WT.v)

                for tci in range(2):
                    tc = half * 2 + tci
                    idx = ic * 2 + tci
                    fi = he * 32 + idx
                    bank = pb[idx % 2]

                    @block.tensor
                    def _(t, he=he, e=e, ic=ic, tci=tci, tc=tc, wb=wb,
                          bank=bank, idx=idx, w1i=w1i, fi=fi):
                        guard(t, ("w1wt", w1i), ("be1ld", he),
                              ("cbdone", he), ("hmgeF", fi - 2))
                        if idx == 0:
                            guard(t, ("hmuse", he - 1))
                        for c in range(NC):
                            t.matmul(bank[:, :], wb[:, c * 128:(c + 1) * 128],
                                     fmbuf[:, c * T + tc * 512:
                                           c * T + (tc + 1) * 512],
                                     start=(c == 0), stop=False)
                        t.matmul(bank[:, :],
                                 BE1R[0:1, ic * 128:(ic + 1) * 128],
                                 onesr[0:1, :], start=False, stop=True
                                 ).then_inc(MM.h, 1)
                        MM.v += 1
                    G[("w1mm", he, idx)] = (MM, MM.v)
                    if tci == 1:
                        G[("w1use", w1i)] = (MM, MM.v)

                    @block.scalar
                    def _(s, he=he, idx=idx, bank=bank, fi=fi):
                        guard(s, ("w1mm", he, idx), ("hmscF", fi - 2))
                        s.activation(GE7[idx % 2][:, :], bank[:, :], AF.Gelu
                                     ).then_inc(AC.h, 1)
                        AC.v += 1
                    G[("hmge", he, idx)] = (AC, AC.v)
                    G[("hmgeF", fi)] = (AC, AC.v)

                    @block.vector
                    def _(v, he=he, ic=ic, tci=tci, idx=idx, fi=fi):
                        guard(v, ("hmge", he, idx))
                        if idx < 2:
                            guard(v, ("hmuse", he - 1))
                        v.tensor_tensor(HMID[:, ic * 1024 + tci * 512:
                                             ic * 1024 + (tci + 1) * 512],
                                        GE7[idx % 2][:, :],
                                        CB7[:, tci * 512:(tci + 1) * 512],
                                        ALU.mult).then_inc(EV.h, 1)
                        EV.v += 1
                    G[("hmsc", he, idx)] = (EV, EV.v)
                    G[("hmscF", fi)] = (EV, EV.v)
            G[("hmid_done", he)] = (EV, EV.v)
            if upto == 173 and he == 1:
                return emit_tail()
            G[("be1free", he)] = (MM, MM.v)
            G[("cbfree", he)] = (EV, EV.v)
            if upto == 70 and he == 0:
                return emit_tail()

            for do in range(4):
                for hw in range(2):
                    w2i = w2i_cnt[0]
                    w2i_cnt[0] += 1
                    wr = W2R[w2i % 2]

                    @block.gpsimd
                    def _(g, e=e, do=do, hw=hw, wr=wr, w2i=w2i):
                        guard(g, ("w2use", w2i - 2))
                        if w2i < 2:
                            guard(g, ("gate_done",))
                        dq(g, WT, wr.rearrange("p (c j) -> p c j", c=8),
                           we2[e, :, do * 512:(do + 1) * 512].rearrange(
                               "(c p) j -> p c j", p=128)
                           [:, hw * 8:(hw + 1) * 8, :])
                    G[("w2wt", w2i)] = (WT, WT.v)

                for ti in range(8):
                    tt = tt0 + ti
                    idx = do * 8 + ti
                    bank = pb[2 + idx % 2]
                    w2i0 = w2i_cnt[0] - 2
                    w2i1 = w2i_cnt[0] - 1
                    yc = y_cnt[0]
                    y_cnt[0] += 1

                    @block.tensor
                    def _(t, he=he, e=e, do=do, ti=ti, tt=tt, bank=bank,
                          idx=idx, w2i0=w2i0, w2i1=w2i1):
                        guard(t, ("w2wt", w2i0), ("hmid_done", he),
                              ("yevF", yc - 2))
                        for c in range(NC):
                            if c == 8:
                                guard(t, ("w2wt", w2i1))
                            wr = W2R[(w2i0 if c < 8 else w2i1) % 2]
                            ins = t.matmul(bank[:, :],
                                           HMID.rearrange("p (c x) -> p c x",
                                                          c=NC)
                                           [:, c, ti * 128:(ti + 1) * 128],
                                           wr[:, (c % 8) * 512:(c % 8 + 1) * 512],
                                           start=(c == 0),
                                           stop=(e != 0 and c == NC - 1))
                        if e == 0:
                            ins = t.matmul(bank[:, :],
                                           combT[0:4, tt * 128:(tt + 1) * 128],
                                           be2r[0:4, do * 512:(do + 1) * 512],
                                           start=False, stop=True)
                        ins.then_inc(MM.h, 1)
                        MM.v += 1
                    G[("w2mmv", he, idx)] = (MM, MM.v)
                    if ti == 7:
                        G[("w2use", w2i0)] = (MM, MM.v)
                        G[("w2use", w2i1)] = (MM, MM.v)

                    @block.sync
                    def _(q, he=he, e=e, half=half, do=do, tt=tt, idx=idx,
                          yc=yc):
                        guard(q, ("yldF", yc - 2), ("ywr", half, e - 1, idx))
                        if half == 0 and e == 0 and idx < 2:
                            guard(q, ("gate_done",))
                        if e == 0:
                            dq(q, LD, X7[idx % 2][:, :],
                               x1_d[tt * 128:(tt + 1) * 128,
                                    do * 512:(do + 1) * 512])
                            dq(q, LD, H7[idx % 2][:, :],
                               h2_d[tt * 128:(tt + 1) * 128,
                                    do * 512:(do + 1) * 512])
                        else:
                            dq(q, LD, X7[idx % 2][:, :],
                               y_out[tt * 128:(tt + 1) * 128,
                                     do * 512:(do + 1) * 512])
                    G[("yld", he, idx)] = (LD, LD.v)

                    @block.vector
                    def _(v, he=he, e=e, idx=idx, bank=bank, yc=yc):
                        guard(v, ("w2mmv", he, idx), ("yld", he, idx),
                              ("ystF", yc - 2))
                        v.tensor_add(Y7[idx % 2][:, :], bank[:, :],
                                     X7[idx % 2][:, :]).then_inc(EV.h, 1)
                        EV.v += 1
                        if e == 0:
                            v.wait_ge(EV.h, EV.v)
                            v.tensor_add(Y7[idx % 2][:, :], Y7[idx % 2][:, :],
                                         H7[idx % 2][:, :]).then_inc(EV.h, 1)
                            EV.v += 1
                    G[("yadd", he, idx)] = (EV, EV.v)
                    G[("yevF", yc)] = (EV, EV.v)
                    G[("yldF", yc)] = (EV, EV.v)

                    @block.gpsimd
                    def _(g, he=he, e=e, do=do, tt=tt, idx=idx):
                        guard(g, ("yadd", he, idx))
                        dq(g, GS, y_out[tt * 128:(tt + 1) * 128,
                                        do * 512:(do + 1) * 512],
                           Y7[idx % 2][:, :])
                    G[("ystF", yc)] = (GS, GS.v)
                    G[("ywr", half, e, idx)] = (GS, GS.v)
            G[("hmuse", he)] = (MM, MM.v)
            if upto == 71 + he or (upto == 176 and he == 1):
                return emit_tail()

    return emit_tail()


_CACHE = {}


def get_nc(dbg=(), upto=99):
    key = (tuple(sorted(dbg)), upto)
    if key not in _CACHE:
        _CACHE[key] = build_nc(dbg=dbg, upto=upto)
    return _CACHE[key]


def _rope_tables(parity):
    pos = np.zeros(T, np.float64)
    for j in range(LBLK):
        pos[j * 128:(j + 1) * 128] = (2 * j + parity) * 128 + np.arange(128)
    inv = 1.0 / (10000.0 ** (np.arange(0, 128, 2, dtype=np.float64) / 128.0))
    freqs = np.concatenate([inv, inv])[:, None] * pos[None, :]
    sin = np.sin(freqs)
    sin[:64, :] *= -1.0
    return (np.cos(freqs).astype(np.float32), sin.astype(np.float32))


def _masks(parity):
    m = np.zeros((8, 128, 512), np.float32)
    tri = (np.arange(128)[:, None] <= np.arange(128)[None, :]).astype(np.float32)
    for s in range(2):
        for jrel in range(4):
            blk = np.zeros((128, 512), np.float32)
            gk = 2 * jrel + s
            for ip in range(4):
                gq = 2 * ip + parity
                sub = slice(ip * 128, (ip + 1) * 128)
                if gk < gq:
                    blk[:, sub] = 1.0
                elif gk == gq:
                    blk[:, sub] = tri
            m[s * 4 + jrel] = blk
    return m


def _make_in_maps(inputs):
    xs = np.ascontiguousarray(inputs["L0_attn_x"])
    brow = np.ascontiguousarray(np.stack(
        [inputs["L0_ffn_bq"], inputs["L0_ffn_bk"], inputs["L0_ffn_bv"],
         inputs["L0_ffn_bo"], inputs["L0_ffn_ln1_w"], inputs["L0_ffn_ln1_b"],
         inputs["L0_ffn_ln2_w"], inputs["L0_ffn_ln2_b"]]).astype(np.float32))
    import os as _os
    nonce = int.from_bytes(_os.urandom(4), "little") % (1 << 28) + (1 << 20)
    shared = {
        "w_qkv": np.ascontiguousarray(inputs["L0_attn_w_qkv"]),
        "w_o": np.ascontiguousarray(inputs["L0_attn_w_o"]),
        "wq": np.ascontiguousarray(inputs["L0_ffn_wq"]),
        "wk": np.ascontiguousarray(inputs["L0_ffn_wk"]),
        "wv": np.ascontiguousarray(inputs["L0_ffn_wv"]),
        "wo2": np.ascontiguousarray(inputs["L0_ffn_wo"]),
        "brow_in": brow,
        "wg": np.ascontiguousarray(inputs["L0_ffn_w_gate"]),
        "we1": np.ascontiguousarray(inputs["L0_ffn_we1"]),
        "be1": np.ascontiguousarray(inputs["L0_ffn_be1"]),
        "we2": np.ascontiguousarray(inputs["L0_ffn_we2"]),
        "be2": np.ascontiguousarray(inputs["L0_ffn_be2"]),
    }
    in_maps = []
    for core in range(N_CORES):
        b, p = core // 2, core % 2
        xb = xs[b].reshape(NBLK, 128, D)[p::2].reshape(T, D)
        cos, sin = _rope_tables(p)
        im = dict(shared)
        im["x_shard"] = np.ascontiguousarray(xb)
        im["rope_cos"] = cos
        im["rope_sin"] = sin
        im["masks"] = _masks(p)
        im["nonce"] = np.array([[nonce]], np.int32)
        im["slot"] = np.array([[p]], np.int32)
        in_maps.append(im)
    return in_maps


def kernel(**inputs):
    nc = get_nc()
    in_maps = _make_in_maps(inputs)
    res = run_bass_kernel_spmd(nc, in_maps, core_ids=list(range(8)))
    out = np.zeros((B, S, D), np.float32)
    for core in range(8):
        b, p = core // 2, core % 2
        yb = np.asarray(res.results[core]["y"]).reshape(LBLK, 128, D)
        out[b].reshape(NBLK, 128, D)[p::2] = yb
    return out


# version 08
# speedup vs baseline: 1.5757x; 1.5757x over previous
import math
from contextlib import ExitStack

import numpy as np
import ml_dtypes

import concourse.bass as bass
import concourse.mybir as mybir
from concourse.bass_utils import run_bass_kernel_spmd

F32 = mybir.dt.float32
BF16 = mybir.dt.bfloat16
I32 = mybir.dt.int32
AF = mybir.ActivationFunctionType
ALU = mybir.AluOpType
AXX = mybir.AxisListType.X
BF = ml_dtypes.bfloat16

B, S, D = 4, 4096, 2048
H = 16
NBLK, LBLK = 32, 16
T = 2048
NC = 16
NT = 16
NTC = 4
E, I_FF = 4, 2048
EPS = 1e-5
ISQ = 1.0 / math.sqrt(128.0)
C_OUT = 8.0
C_IN = 24.0
POLL_MAX = 500000
N_CORES = 8


class SemC:
    def __init__(self, h):
        self.h = h
        self.v = 0


def build_nc(dbg=(), upto=99, skipo=False, n_experts=E):
    nc = bass.Bass("TRN2", target_bir_lowering=False)
    es = ExitStack()
    G = {}

    def guard(eng, *keys):
        for k in keys:
            if isinstance(k, tuple) and len(k) == 1:
                k = k[0]
            g = G.get(k)
            if g is not None:
                eng.wait_ge(g[0].h, g[1])

    def dram(name, shape, dtype, kind="Internal", **kw):
        k = "ExternalOutput" if name in dbg else kind
        return nc.dram_tensor(name, shape, dtype, kind=k, **kw).ap()

    x_in = dram("x_shard", [T, D], F32, kind="ExternalInput")
    w_qkv = dram("w_qkv", [D, 3 * D], F32, kind="ExternalInput")
    w_o = dram("w_o", [D, D], F32, kind="ExternalInput")
    wq = dram("wq", [D, D], F32, kind="ExternalInput")
    wk = dram("wk", [D, D], F32, kind="ExternalInput")
    wv = dram("wv", [D, D], F32, kind="ExternalInput")
    wo2 = dram("wo2", [D, D], F32, kind="ExternalInput")
    brow_in = dram("brow_in", [8, D], F32, kind="ExternalInput")
    wg_in = dram("wg", [D, E], F32, kind="ExternalInput")
    we1 = dram("we1", [E, D, I_FF], F32, kind="ExternalInput")
    be1_in = dram("be1", [E, I_FF], F32, kind="ExternalInput")
    we2 = dram("we2", [E, I_FF, D], F32, kind="ExternalInput")
    be2_in = dram("be2", [E, D], F32, kind="ExternalInput")
    cos_in = dram("rope_cos", [128, T], F32, kind="ExternalInput")
    sin_in = dram("rope_sin", [128, T], F32, kind="ExternalInput")
    masks_in = dram("masks", [8, 128, 512], F32, kind="ExternalInput")
    nonce_in = dram("nonce", [1, 1], I32, kind="ExternalInput")
    slot_in = dram("slot", [1, 1], I32, kind="ExternalInput")
    y_out = dram("y", [T, D], F32, kind="ExternalOutput")

    q_out_d = dram("q_out_d", [D, T], BF16)
    v_out_d = dram("v_out_d", [D, T], BF16)
    x1_d = dram("x1_d", [T, D], F32)
    hin_d = dram("hin_d", [T, D], F32)
    h2_d = dram("h2_d", [T, D], F32)
    qin_d = dram("qin_d", [D, T], BF16)
    xnf_d = dram("xnf_d", [D, T], BF16)
    hn1f_d = dram("hn1f_d", [D, T], BF16)
    hn2f_d = dram("hn2f_d", [D, T], BF16)

    k_out_sh = dram("k_out_sh", [2, D, T], BF16, addr_space="Shared")
    k_in_sh = dram("k_in_sh", [2, D, T], BF16, addr_space="Shared")
    v_in_sh = dram("v_in_sh", [2, H, T, 128], BF16, addr_space="Shared")
    cs_sh = dram("cs_sh", [2, H, S], BF16, addr_space="Shared")
    flags_sh = dram("flags_sh", [2, 16], I32, addr_space="Shared")

    ident_np = np.zeros((128, 128), BF)
    np.fill_diagonal(ident_np, 1)
    ident_c = nc.inline_tensor(np.ascontiguousarray(ident_np), "ident_c").ap()
    sel4_np = np.zeros((4, 4 * 128), BF)
    for e_ in range(4):
        sel4_np[e_, e_ * 128:(e_ + 1) * 128] = 1
    sel4_c = nc.inline_tensor(np.ascontiguousarray(sel4_np), "sel4_c").ap()
    ones_c = nc.inline_tensor(np.ones((1, 512), BF), "ones_c").ap()
    tieb_np = np.zeros((128, 4), np.float32)
    tieb_np[:, :] = np.array([3e-6, 2e-6, 1e-6, 0.0], np.float32)[None, :]
    tieb_c = nc.inline_tensor(tieb_np, "tieb_c").ap()

    def sb(name, shape, dtype):
        return es.enter_context(nc.sbuf_tensor(name, shape, dtype))

    ident = sb("ident", [128, 128], BF16)
    sel4 = sb("sel4", [4, 4 * 128], BF16)
    onesr = sb("onesr", [1, 512], BF16)
    ones32 = sb("ones32", [128, 32], BF16)
    ones128 = sb("ones128", [128, 128], BF16)
    stats = sb("stats", [128, 64], F32)
    epsc = sb("epsc", [128, 1], F32)
    nco = sb("nco", [128, 1], F32)
    nci = sb("nci", [128, 1], F32)
    zcol = sb("zcol", [128, 8], F32)
    invz = sb("invz", [128, 1], F32)
    invzb = sb("invzb", [128, 32], BF16)
    invzf = sb("invzf", [128, 512], F32)
    graw = sb("graw", [128, 4], F32)
    tieb = sb("tieb", [128, 4], F32)
    gst = sb("gst", [128, 24], F32)
    combf = sb("combf", [128, 4], F32)
    combb = sb("combb", [128, 4], BF16)
    combT = sb("combT", [4, T], BF16)
    be2r = sb("be2r", [4, D], BF16)
    browR = sb("browR", [1, 4 * D], BF16)
    csr0 = sb("csr0", [1, T], BF16)
    csr1 = sb("csr1", [1, T], BF16)
    flg = sb("flg", [1, 32], I32)
    myflag = sb("myflag", [1, 16], I32)
    noncev = sb("noncev", [1, 1], I32)
    slotv = sb("slotv", [1, 1], I32)
    fmbuf = sb("fmbuf", [128, NC * T], BF16)
    AR_COLS = 44 * 1024
    arena = sb("arena", [128, AR_COLS], BF16)

    def arf32(off_b, n):
        assert off_b % 4 == 0 and off_b + 4 * n <= AR_COLS * 2, off_b
        return arena[:, off_b // 2: off_b // 2 + 2 * n].bitcast(F32)

    def arbf(off_b, n):
        assert off_b % 2 == 0 and off_b + 2 * n <= AR_COLS * 2, off_b
        return arena[:, off_b // 2: off_b // 2 + n]

    pbA = es.enter_context(nc.psum_tensor("pbA", [128, 1024], F32))
    pbB = es.enter_context(nc.psum_tensor("pbB", [128, 1024], F32))
    pbZ = es.enter_context(nc.psum_tensor("pbZ", [128, 512], F32))
    pbV = es.enter_context(nc.psum_tensor("pbV", [128, 512], F32))
    pb = [pbA[:, 0:512], pbA[:, 512:1024], pbZ[:, :], pbV[:, :],
          pbB[:, 0:512], pbB[:, 512:1024]]
    ptr = [es.enter_context(nc.psum_tensor(f"ptr{i}", [128, 1024], BF16))
           for i in range(2)]

    def sem(name):
        return SemC(es.enter_context(nc.semaphore(name)))

    LD, WT, MM, EV, AC, GP, ST, BR, GS = (sem(n) for n in
                                          "LD WT MM EV AC GP ST BR GS".split())

    REGS = {}

    def get_reg(eng, name):
        key = (eng.engine, name)
        if key not in REGS:
            REGS[key] = eng.alloc_register(name)
        return REGS[key]

    def dq(q, smc, out, in_, **kw):
        q.wait_ge(smc.h, smc.v)
        q.dma_start(out=out, in_=in_, **kw).then_inc(smc.h, 16)
        smc.v += 16
    BARS = [sem(f"PB{i}") for i in range(4)]

    block = es.enter_context(nc.Block())

    def emit_tail():
        fins = [(LD, LD.v), (WT, WT.v), (MM, MM.v), (EV, EV.v), (AC, AC.v),
                (GP, GP.v), (ST, ST.v), (BR, BR.v), (GS, GS.v)]

        @block.sync
        def _(q):
            for smc, vv in fins:
                if vv > 0:
                    q.wait_ge(smc.h, vv)
        es.close()
        return nc

    def barrier(stage, payload_sem, payload_val):
        PB = BARS[stage]

        @block.gpsimd
        def _(g):
            g.wait_ge(payload_sem.h, payload_val)
            slot = get_reg(g, "bar_s")
            g.reg_load(slot, slotv[0:1, 0:1])
            with g.If_eq(slot, 0):
                g.dma_start(out=flags_sh[0:1, stage:stage + 1],
                            in_=myflag[0:1, stage:stage + 1]).then_inc(PB.h, 16)
            with g.Else():
                g.dma_start(out=flags_sh[1:2, stage:stage + 1],
                            in_=myflag[0:1, stage:stage + 1]).then_inc(PB.h, 16)
            it = get_reg(g, "bar_i")
            g.reg_mov(it, 0)
            done = get_reg(g, "bar_d")
            g.reg_mov(done, 0)
            tgt = get_reg(g, "bar_t")
            c = get_reg(g, "bar_c")
            f0 = get_reg(g, "bar_f0")
            f1 = get_reg(g, "bar_f1")
            nn = get_reg(g, "bar_n")
            t1 = get_reg(g, "bar_u")
            t2 = get_reg(g, "bar_v")
            g.reg_load(nn, noncev[0:1, 0:1])

            def cond():
                g.reg_alu(c, done, 0, ALU.is_equal)
                return c
            with g.While(cond):
                g.dma_start(out=flg[0:1, :],
                            in_=flags_sh.rearrange("a b -> (a b)")[None, :]
                            ).then_inc(PB.h, 16)
                g.reg_add(it, it, 1)
                g.reg_mul(tgt, it, 16)
                g.reg_add(tgt, tgt, 16)
                g.wait_ge(PB.h, tgt)
                g.reg_load(f0, flg[0:1, stage:stage + 1])
                g.reg_load(f1, flg[0:1, 16 + stage:17 + stage])
                g.reg_alu(f0, f0, f1, ALU.min)
                g.reg_sub(f0, f0, nn)
                g.reg_alu(t1, f0, stage, ALU.is_ge)
                g.reg_alu(t2, f0, 1000000, ALU.is_le)
                g.reg_alu(t1, t1, t2, ALU.bitwise_and)
                g.reg_alu(t2, it, POLL_MAX, ALU.is_ge)
                g.reg_alu(done, t1, t2, ALU.bitwise_or)
            g.nop().then_inc(BR.h, 1)
            BR.v += 1
        return ("BARR", stage), (BR, BR.v)

    def slot_dma(g, dst0, dst1, src, sem_c, tag, inc=16):
        g.wait_ge(sem_c.h, sem_c.v)
        slot = get_reg(g, "slotreg")
        g.reg_load(slot, slotv[0:1, 0:1])
        with g.If_eq(slot, 0):
            g.dma_start(out=dst0, in_=src).then_inc(sem_c.h, inc)
        with g.Else():
            g.dma_start(out=dst1, in_=src).then_inc(sem_c.h, inc)
        sem_c.v += inc

    @block.sync
    def _(q):
        for dst, src in ((ident, ident_c), (sel4, sel4_c), (onesr, ones_c)):
            dq(q, LD, dst[:, :], src[:, :])
        dq(q, LD, tieb[:, :], tieb_c[:, :])
        dq(q, LD, noncev[:, :], nonce_in[:, :])
        dq(q, LD, slotv[:, :], slot_in[:, :])
    ld_consts = LD.v
    G["consts"] = (LD, ld_consts)

    @block.gpsimd
    def _(g):
        for tile_, val_ in ((ones32, 1.0), (ones128, 1.0), (epsc, EPS),
                            (nco, -C_OUT), (nci, -C_IN)):
            g.memset(tile_[:, :], val_).then_inc(GP.h, 1)
            GP.v += 1
        dq(g, WT, be2r[:, :], be2_in[:, :])
        g.wait_ge(LD.h, ld_consts)
        nn = get_reg(g, "nn0")
        g.reg_load(nn, noncev[0:1, 0:1])
        vv = get_reg(g, "vv0")
        for s_ in range(16):
            g.reg_add(vv, nn, s_)
            g.store(myflag[0:1, s_:s_ + 1], vv)
        g.nop().then_inc(GP.h, 1)
        GP.v += 1
    G["gp_init"] = (GP, GP.v)

    def ln_stats_apply(tag, tt, xa, dst, accum_col=None):
        @block.vector
        def _(v):
            guard(v, (tag, "ld", tt))
            v.reduce_sum(stats[:, 0:1], xa[:, :], axis=AXX).then_inc(EV.h, 1)
            EV.v += 1
        G[(tag, "sum", tt)] = (EV, EV.v)

        @block.scalar
        def _(s):
            guard(s, (tag, "ld", tt), (tag, "sqfree", tt - 1),
                  (tag, "var", tt - 1))
            s.activation(SQS[:, :], xa[:, :], AF.Square,
                         accum_out=stats[:, 1:2]).then_inc(AC.h, 1)
            AC.v += 1
        G[(tag, "sq", tt)] = (AC, AC.v)

        @block.vector
        def _(v):
            guard(v, (tag, "sum", tt), (tag, "sq", tt))
            v.tensor_scalar_mul(stats[:, 2:3], stats[:, 0:1], 1.0 / D
                                ).then_inc(EV.h, 1); EV.v += 1
            v.tensor_scalar_mul(stats[:, 3:4], stats[:, 1:2], 1.0 / D
                                ).then_inc(EV.h, 1); EV.v += 1
            v.wait_ge(EV.h, EV.v)
            v.tensor_tensor(stats[:, 4:5], stats[:, 2:3], stats[:, 2:3],
                            ALU.mult).then_inc(EV.h, 1); EV.v += 1
            v.wait_ge(EV.h, EV.v)
            v.tensor_tensor(stats[:, 5:6], stats[:, 3:4], stats[:, 4:5],
                            ALU.subtract).then_inc(EV.h, 1)
            EV.v += 1
        G[(tag, "var", tt)] = (EV, EV.v)

        @block.scalar
        def _(s):
            guard(s, (tag, "var", tt), ("gp_init",))
            s.activation(stats[:, 6:7], stats[:, 5:6], AF.Sqrt, bias=epsc[:, :]
                         ).then_inc(AC.h, 1)
            AC.v += 1
        G[(tag, "std", tt)] = (AC, AC.v)

        @block.vector
        def _(v):
            guard(v, (tag, "std", tt))
            v.reciprocal(stats[:, 7:8], stats[:, 6:7]).then_inc(EV.h, 1)
            EV.v += 1
            v.wait_ge(EV.h, EV.v)
            guard(v, (tag, "applyfree", tt), (tag, "sq2", tt - 1))
            v.tensor_scalar(dst[:, :], xa[:, :], stats[:, 2:3], stats[:, 7:8],
                            ALU.subtract, ALU.mult).then_inc(EV.h, 1)
            EV.v += 1
            if accum_col is not None:
                v.wait_ge(EV.h, EV.v)
                v.reduce_sum(stats[:, accum_col:accum_col + 1], dst[:, :],
                             axis=AXX).then_inc(EV.h, 1)
                EV.v += 1
        G[(tag, "h", tt)] = (EV, EV.v)

    def transpose_to_fm(tag, tt):
        for g4 in range(4):
            bank = ptr[g4 % 2]

            @block.tensor
            def _(t, g4=g4, bank=bank):
                if g4 == 0:
                    guard(t, (tag, "hn", tt))
                guard(t, (tag, "tb", tt, g4 - 2), (tag, "tb", tt - 1, g4 + 2))
                for c4 in range(4):
                    c = g4 * 4 + c4
                    ins = t.transpose(bank[:, c4 * 128:(c4 + 1) * 128],
                                      SQS[:, c * 128:(c + 1) * 128],
                                      ident[:, :])
                ins.then_inc(MM.h, 1)
                MM.v += 1
            G[(tag, "tmm", tt, g4)] = (MM, MM.v)

            @block.scalar
            def _(s, g4=g4, bank=bank):
                guard(s, (tag, "tmm", tt, g4))
                s.copy(fmbuf.rearrange("p (c t) -> p c t", c=NC)
                       [:, g4 * 4:g4 * 4 + 4, tt * 128:(tt + 1) * 128],
                       bank[:, 0:512].rearrange("p (c t) -> p c t", c=4)[:, :, :]
                       ).then_inc(AC.h, 1)
                AC.v += 1
            G[(tag, "tb", tt, g4)] = (AC, AC.v)
        G[(tag, "tr", tt)] = G[(tag, "tmm", tt, 3)]
        G[(tag, "sqfree", tt)] = G[(tag, "tmm", tt, 3)]

    SQS = arbf(16384, D)
    for tt in range(NT):
        xa = arf32(0 if tt % 2 == 0 else 8192, D)

        @block.sync
        def _(q, tt=tt, xa=xa):
            guard(q, ("l0", "xfree", tt - 2))
            dq(q, LD, xa[:, :], x_in[tt * 128:(tt + 1) * 128, :])
        G[("l0", "ld", tt)] = (LD, LD.v)

        G[("l0", "applyfree", tt)] = G.get(("l0", "tr", tt - 1))
        ln_stats_apply("l0", tt, xa, SQS)
        G[("l0", "hn", tt)] = G[("l0", "h", tt)]
        G[("l0", "xfree", tt)] = G[("l0", "h", tt)]
        transpose_to_fm("l0", tt)
    ph0_ac = AC.v
    G["xn_fm"] = (AC, ph0_ac)
    if "xnf_d" in dbg:
        @block.sync
        def _(q):
            q.wait_ge(AC.h, ph0_ac)
            dq(q, ST, xnf_d.rearrange("(c p) t -> p c t", p=128),
               fmbuf.rearrange("p (c t) -> p c t", c=NC))
    if upto == 0:
        return emit_tail()

    WCOL = [arbf(20480, NC * 128), arbf(28672, NC * 128)]
    EVST = [arf32(36864, 512), arf32(38912, 512)]
    PSTG = [arbf(40960, 512), arbf(41984, 512)]
    KT1 = arf32(43008, 512)
    KT2 = arf32(45056, 512)
    RCOS = arf32(49152, T)
    RSIN = arf32(57344, T)

    def load_wcol(g, dst, wsrc, col0):
        src3 = wsrc[:, col0:col0 + 128].rearrange("(c p) j -> p c j", p=128)
        dst3 = dst.rearrange("p (c j) -> p c j", c=NC)
        for hh in range(4):
            dq(g, WT, dst3[:, hh * 4:(hh + 1) * 4, :],
               src3[:, hh * 4:(hh + 1) * 4, :])

    def emit_fm_proj(name, wsrc, col_base, bias_slot, rope_flag, sink,
                     first_waits):
        for do in range(NC):
            wb = WCOL[do % 2]

            @block.gpsimd
            def _(g, do=do, wb=wb):
                guard(g, (name, "mmdone", do - 2))
                if do < 2:
                    for w_ in first_waits:
                        guard(g, w_)
                load_wcol(g, wb, wsrc, col_base + do * 128)
            G[(name, "wt", do)] = (WT, WT.v)

            for tc in range(NTC):
                bank = pb[tc % 2]
                idx = do * NTC + tc

                @block.tensor
                def _(t, do=do, tc=tc, wb=wb, bank=bank, idx=idx):
                    guard(t, (name, "wt", do))
                    if idx == 0:
                        for w_ in first_waits:
                            guard(t, w_)
                    guard(t, (name, "ev", idx - 2))
                    for c in range(NC):
                        ins = t.matmul(bank[:, :], wb[:, c * 128:(c + 1) * 128],
                                       fmbuf[:, c * T + tc * 512:
                                             c * T + (tc + 1) * 512],
                                       start=(c == 0),
                                       stop=(bias_slot is None and c == NC - 1))
                    if bias_slot is not None:
                        ins = t.matmul(bank[:, :],
                                       browR[0:1, bias_slot * D + do * 128:
                                             bias_slot * D + (do + 1) * 128],
                                       onesr[0:1, :], start=False, stop=True)
                    ins.then_inc(MM.h, 1)
                    MM.v += 1
                G[(name, "mm", idx)] = (MM, MM.v)

                stg = PSTG[idx % 2]
                if not rope_flag:
                    @block.scalar
                    def _(s, bank=bank, idx=idx, stg=stg):
                        guard(s, (name, "mm", idx), (name, "sink", idx - 2))
                        if idx < 2:
                            for w_ in first_waits:
                                guard(s, w_)
                        s.copy(stg[:, :], bank[:, :]).then_inc(AC.h, 1)
                        AC.v += 1
                    G[(name, "ev", idx)] = (AC, AC.v)
                    G[(name, "stg", idx)] = (AC, AC.v)
                else:
                    @block.scalar
                    def _(s, bank=bank, idx=idx):
                        guard(s, (name, "mm", idx), (name, "rope", idx - 1))
                        if idx < 2:
                            for w_ in first_waits:
                                guard(s, w_)
                        s.copy(EVST[0][:, :], bank[:, :]).then_inc(AC.h, 1)
                        AC.v += 1
                    G[(name, "ev", idx)] = (AC, AC.v)
                    ac_here = AC.v

                    @block.vector
                    def _(v, tc=tc, idx=idx, stg=stg, ac_here=ac_here):
                        v.wait_ge(AC.h, ac_here)
                        guard(v, (name, "sink", idx - 2))
                        cs_ = RCOS[:, tc * 512:(tc + 1) * 512]
                        sn_ = RSIN[:, tc * 512:(tc + 1) * 512]
                        ev = EVST[0]
                        v.tensor_tensor(KT1[:, :], ev[:, :], cs_, ALU.mult
                                        ).then_inc(EV.h, 1); EV.v += 1
                        v.tensor_copy(KT2[0:64, :], ev[64:128, :]
                                      ).then_inc(EV.h, 1); EV.v += 1
                        v.tensor_copy(KT2[64:128, :], ev[0:64, :]
                                      ).then_inc(EV.h, 1); EV.v += 1
                        v.wait_ge(EV.h, EV.v)
                        v.tensor_tensor(KT2[:, :], KT2[:, :], sn_, ALU.mult
                                        ).then_inc(EV.h, 1); EV.v += 1
                        v.wait_ge(EV.h, EV.v)
                        v.tensor_tensor(stg[:, :], KT1[:, :], KT2[:, :],
                                        ALU.add).then_inc(EV.h, 1)
                        EV.v += 1
                    G[(name, "rope", idx)] = (EV, EV.v)
                    G[(name, "stg", idx)] = (EV, EV.v)

                sink(do, tc, stg, (name, "stg", idx), (name, "sink", idx))
            G[(name, "mmdone", do)] = (MM, MM.v)
        G[(name, "alldone")] = (MM, MM.v)
        sk = G.get((name, "sink", NC * NTC - 1))
        if sk is not None:
            G[(name, "sinkdone")] = sk

    def sink_dram(dst):
        def f(do, tc, stg, ready_key, done_key):
            @block.sync
            def _(q, do=do, tc=tc, stg=stg):
                guard(q, ready_key)
                dq(q, ST, dst[do * 128:(do + 1) * 128, tc * 512:(tc + 1) * 512],
                   stg[:, :])
            G[done_key] = (ST, ST.v)
        return f

    def sink_shared(dst_sh):
        def f(do, tc, stg, ready_key, done_key):
            @block.gpsimd
            def _(g, do=do, tc=tc, stg=stg):
                guard(g, ready_key)
                slot_dma(g,
                         dst_sh[0, do * 128:(do + 1) * 128,
                                tc * 512:(tc + 1) * 512],
                         dst_sh[1, do * 128:(do + 1) * 128,
                                tc * 512:(tc + 1) * 512],
                         stg[:, :], GS, f"sk{do}_{tc}")
            G[done_key] = (GS, GS.v)
        return f

    if upto == 7:
        emit_fm_proj("Kout", w_qkv, D, None, False, sink_dram(q_out_d),
                     [("xn_fm",), ("gp_init",)])
        return emit_tail()
    emit_fm_proj("Kout", w_qkv, D, None, False, sink_shared(k_out_sh),
                 [("xn_fm",), ("gp_init",)])
    gp_kout = GS.v
    if upto == 8:
        return emit_tail()
    emit_fm_proj("Qout", w_qkv, 0, None, False, sink_dram(q_out_d),
                 [("Kout", "alldone"), ("Kout", "sinkdone")])
    emit_fm_proj("Vout", w_qkv, 2 * D, None, False, sink_dram(v_out_d),
                 [("Qout", "alldone"), ("Qout", "sinkdone")])
    G["qv_st"] = (ST, ST.v)
    if upto == 9:
        return emit_tail()
    _, brk = barrier(1, GS, gp_kout)
    G["br_kout"] = brk
    if upto == 1:
        return emit_tail()

    KH2 = [arbf(20480, S), arbf(28672, S)]
    QH2 = [arbf(36864, T), arbf(40960, T)]
    PR2 = [arbf(49152, S), arbf(57344, S)]
    CS32 = arbf(45056, 1024)
    VST2 = [arbf(65536, 512), arbf(66560, 512)]

    for h in (range(0) if skipo else range(H)):
        kh, qh = KH2[h % 2], QH2[h % 2]

        @block.sync
        def _(q, h=h, kh=kh, qh=qh):
            guard(q, ("oatt_mm", h - 2))
            if h < 2:
                guard(q, ("br_kout",), ("Vout", "alldone"),
                      ("Qout", "sink", NC * NTC - 1),
                      ("Vout", "sink", NC * NTC - 1))
            dq(q, LD, kh[:, 0:T], k_out_sh[0, h * 128:(h + 1) * 128, :])
            dq(q, LD, kh[:, T:S], k_out_sh[1, h * 128:(h + 1) * 128, :])
            dq(q, LD, qh[:, :], q_out_d[h * 128:(h + 1) * 128, :])
        G[("oatt_ld", h)] = (LD, LD.v)

        mm0, ac0, ev0 = MM.v, AC.v, EV.v

        @block.tensor
        def _(t, h=h, kh=kh, qh=qh, mm0=mm0, ac0=ac0, ev0=ev0):
            guard(t, ("oatt_ld", h), ("ocse", h - 1, 7))
            for qt in range(NT):
                for pair in range(4):
                    bankp = pbA if pair % 2 == 0 else pbB
                    pv = ac0 + 4 * qt + pair - 1
                    if pv > ac0 or h > 0:
                        t.wait_ge(AC.h, pv)
                    for half_ in range(2):
                        kc = 2 * pair + half_
                        ins = t.matmul(bankp[:, half_ * 512:(half_ + 1) * 512],
                                       qh[:, qt * 128:(qt + 1) * 128],
                                       kh[:, kc * 512:(kc + 1) * 512],
                                       start=True, stop=True)
                    ins.then_inc(MM.h, 1)
                    MM.v += 1
                t.wait_ge(EV.h, ev0 + 3 * qt + 3)
                prb = PR2[qt % 2]
                for kc in range(8):
                    pcs = pb[2 + kc // 4]
                    base = (kc % 4) * 32
                    ins = t.matmul(pcs[base:base + 32, :], invzb[:, :],
                                   prb[:, kc * 512:(kc + 1) * 512],
                                   start=(qt == 0), stop=(qt == NT - 1),
                                   tile_position=(0, base))
                ins.then_inc(MM.h, 1)
                MM.v += 1
        G[("oatt_mm", h)] = (MM, MM.v)

        @block.scalar
        def _(s, h=h, mm0=mm0, ac0=ac0, ev0=ev0):
            if h == 0:
                guard(s, ("gp_init",))
            for qt in range(NT):
                prb = PR2[qt % 2]
                for pair in range(4):
                    s.wait_ge(MM.h, mm0 + 5 * qt + pair + 1)
                    if pair == 0:
                        if qt >= 2:
                            s.wait_ge(MM.h, mm0 + 5 * (qt - 2) + 5)
                        elif h > 0:
                            s.wait_ge(MM.h, mm0 - 5 * (2 - qt) + 5)
                        if qt >= 1:
                            s.wait_ge(EV.h, ev0 + 3 * (qt - 1) + 1)
                        elif h > 0:
                            s.wait_ge(EV.h, ev0 - 2)
                    bankp = pbA if pair % 2 == 0 else pbB
                    s.activation(prb[:, pair * 1024:(pair + 1) * 1024],
                                 bankp[:, :], AF.Exp, bias=nco[:, :],
                                 scale=ISQ,
                                 accum_out=zcol[:, pair:pair + 1]
                                 ).then_inc(AC.h, 1)
                    AC.v += 1

        @block.vector
        def _(v, h=h, ac0=ac0):
            for qt in range(NT):
                v.wait_ge(AC.h, ac0 + 4 * qt + 4)
                v.reduce_sum(stats[:, 8:9], zcol[:, 0:4], axis=AXX
                             ).then_inc(EV.h, 1)
                EV.v += 1
                v.wait_ge(EV.h, EV.v)
                v.reciprocal(invz[:, :], stats[:, 8:9]).then_inc(EV.h, 1)
                EV.v += 1
                v.wait_ge(EV.h, EV.v)
                v.tensor_scalar_mul(invzb[:, :], ones32[:, :], invz[:, :]
                                    ).then_inc(EV.h, 1)
                EV.v += 1

        for kc in range(8):
            @block.vector
            def _(v, h=h, kc=kc):
                if kc == 0:
                    guard(v, ("oatt_mm", h))
                pcs = pb[2 + kc // 4]
                base = (kc % 4) * 32
                guard(v, ("ocst", h, kc - 1))
                v.tensor_copy(CS32[0:32, 0:512], pcs[base:base + 32, :]
                              ).then_inc(EV.h, 1)
                EV.v += 1
            G[("ocse", h, kc)] = (EV, EV.v)

            @block.gpsimd
            def _(g, h=h, kc=kc):
                guard(g, ("ocse", h, kc))
                slot_dma(g, cs_sh[0, h:h + 1, kc * 512:(kc + 1) * 512],
                         cs_sh[1, h:h + 1, kc * 512:(kc + 1) * 512],
                         CS32[0:1, 0:512], GS, f"cst{h}_{kc}")
            G[("ocst", h, kc)] = (GS, GS.v)
    gp_csall = GS.v
    _, brc = barrier(2, GS, gp_csall)
    G["br_cs"] = brc
    if upto == 2:
        return emit_tail()

    for c in range(NC):
        @block.gpsimd
        def _(g, c=c):
            if c == 0:
                guard(g, ("br_cs",))
            guard(g, ("valdone", c - 1))
            g.wait_ge(GS.h, GS.v)
            slot = get_reg(g, "slotreg")
            g.reg_load(slot, slotv[0:1, 0:1])
            with g.If_eq(slot, 0):
                g.dma_start(out=csr0[0:1, :], in_=cs_sh[0, c:c + 1, 0:T]
                            ).then_inc(GS.h, 16)
                g.dma_start(out=csr1[0:1, :], in_=cs_sh[1, c:c + 1, 0:T]
                            ).then_inc(GS.h, 16)
            with g.Else():
                g.dma_start(out=csr0[0:1, :], in_=cs_sh[0, c:c + 1, T:S]
                            ).then_inc(GS.h, 16)
                g.dma_start(out=csr1[0:1, :], in_=cs_sh[1, c:c + 1, T:S]
                            ).then_inc(GS.h, 16)
            GS.v += 32
        G[("csrld", c)] = (GS, GS.v)

        for tc in range(NTC):
            idx = c * NTC + tc
            bankb = pb[4 + idx % 2]

            @block.tensor
            def _(t, c=c, tc=tc, bankb=bankb, idx=idx):
                guard(t, ("csrld", c), ("vbe", idx - 2))
                t.matmul(bankb[:, :], onesr[0:1, 0:128],
                         csr0[0:1, tc * 512:(tc + 1) * 512],
                         start=True, stop=False)
                t.matmul(bankb[:, :], onesr[0:1, 0:128],
                         csr1[0:1, tc * 512:(tc + 1) * 512],
                         start=False, stop=True).then_inc(MM.h, 1)
                MM.v += 1
            G[("vbmm", idx)] = (MM, MM.v)

            @block.sync
            def _(q, c=c, tc=tc, idx=idx):
                guard(q, ("vbe", idx - 2))
                dq(q, LD, VST2[idx % 2][:, :],
                   v_out_d[c * 128:(c + 1) * 128, tc * 512:(tc + 1) * 512])
            G[("vld", idx)] = (LD, LD.v)

            @block.vector
            def _(v, c=c, tc=tc, bankb=bankb, idx=idx):
                guard(v, ("vbmm", idx), ("vld", idx))
                v.tensor_copy(invzf[:, :], bankb[:, :]).then_inc(EV.h, 1)
                EV.v += 1
                v.wait_ge(EV.h, EV.v)
                v.tensor_tensor(fmbuf[:, c * T + tc * 512:c * T + (tc + 1) * 512],
                                VST2[idx % 2][:, :], invzf[:, :], ALU.mult
                                ).then_inc(EV.h, 1)
                EV.v += 1
            G[("vbe", idx)] = (EV, EV.v)
        G[("valdone", c)] = (EV, EV.v)
    G["values"] = (EV, EV.v)

    WRHS = [arbf(20480, NC * 512), arbf(49152, NC * 512)]
    XS = [arf32(65536, 512), arf32(67584, 512)]
    YS = [arf32(69632, 512), arf32(71680, 512)]

    def emit_tm_proj(name, wsrc, bias_slot, writer, first_waits):
        for do in range(4):
            wr = WRHS[do % 2]

            @block.gpsimd
            def _(g, do=do, wr=wr):
                guard(g, (name, "mmdone", do - 2))
                if do < 2:
                    for w_ in first_waits:
                        guard(g, w_)
                src3 = wsrc[:, do * 512:(do + 1) * 512].rearrange(
                    "(c p) j -> p c j", p=128)
                dst3 = wr.rearrange("p (c j) -> p c j", c=NC)
                for hh in range(4):
                    dq(g, WT, dst3[:, hh * 4:(hh + 1) * 4, :],
                       src3[:, hh * 4:(hh + 1) * 4, :])
            G[(name, "wt", do)] = (WT, WT.v)

            for tt in range(NT):
                bank = pb[tt % 2]
                idx = do * NT + tt

                @block.tensor
                def _(t, do=do, tt=tt, wr=wr, bank=bank, idx=idx):
                    guard(t, (name, "wt", do))
                    if idx == 0:
                        for w_ in first_waits:
                            guard(t, w_)
                    guard(t, (name, "pev", idx - 2))
                    for c in range(NC):
                        ins = t.matmul(bank[:, :],
                                       fmbuf[:, c * T + tt * 128:
                                             c * T + (tt + 1) * 128],
                                       wr[:, c * 512:(c + 1) * 512],
                                       start=(c == 0),
                                       stop=(bias_slot is None and c == NC - 1))
                    if bias_slot is not None:
                        ins = t.matmul(bank[:, :], onesr[0:1, 0:128],
                                       browR[0:1, bias_slot * D + do * 512:
                                             bias_slot * D + (do + 1) * 512],
                                       start=False, stop=True)
                    ins.then_inc(MM.h, 1)
                    MM.v += 1
                G[(name, "mm", idx)] = (MM, MM.v)
                writer(name, do, tt, bank, idx)
            G[(name, "mmdone", do)] = (MM, MM.v)
        G[(name, "alldone")] = (MM, MM.v)
        sk = G.get((name, "sink", NC * NTC - 1))
        if sk is not None:
            G[(name, "sinkdone")] = sk

    def write_resid(src_d, dst_d, entry_key):
        def f(name, do, tt, bank, idx):
            @block.sync
            def _(q, do=do, tt=tt, idx=idx):
                guard(q, (name, "radd", idx - 2))
                if idx < 2:
                    guard(q, entry_key)
                dq(q, LD, XS[idx % 2][:, :],
                   src_d[tt * 128:(tt + 1) * 128, do * 512:(do + 1) * 512])
            G[(name, "rld", idx)] = (LD, LD.v)

            @block.vector
            def _(v, do=do, tt=tt, bank=bank, idx=idx):
                guard(v, (name, "mm", idx), (name, "rld", idx),
                      (name, "rst", idx - 2))
                v.tensor_add(YS[idx % 2][:, :], bank[:, :], XS[idx % 2][:, :]
                             ).then_inc(EV.h, 1)
                EV.v += 1
            G[(name, "radd", idx)] = (EV, EV.v)
            G[(name, "pev", idx)] = (EV, EV.v)

            @block.sync
            def _(q, do=do, tt=tt, idx=idx):
                guard(q, (name, "radd", idx))
                dq(q, ST, dst_d[tt * 128:(tt + 1) * 128, do * 512:(do + 1) * 512],
                   YS[idx % 2][:, :])
            G[(name, "rst", idx)] = (ST, ST.v)
        return f

    emit_tm_proj("oproj", w_o, None, write_resid(x_in, x1_d, ("values",)), [("values",)])
    G["x1_done"] = (ST, ST.v)
    if upto == 3:
        return emit_tail()

    @block.gpsimd
    def _(g):
        guard(g, ("x1_done",))
        dq(g, WT, browR[0:1, 0:D], brow_in[4:5, :])
        dq(g, WT, browR[0:1, D:2 * D], brow_in[5:6, :])
    G["g1rows"] = (WT, WT.v)

    G1T = arbf(36864, D)
    B1T = arbf(40960, D)

    def build_bcast_tiles(key, row_offs, dsts, waitkey):
        for r in range(2):
            for ch in range(4):
                @block.tensor
                def _(t, r=r, ch=ch):
                    if r == 0 and ch == 0:
                        guard(t, waitkey)
                    guard(t, (key, "ev", r, ch - 2),
                          (key, "ev", r - 1, ch + 2))
                    t.matmul(pb[4 + ch % 2][:, :], onesr[0:1, 0:128],
                             browR[0:1, row_offs[r] + ch * 512:
                                   row_offs[r] + (ch + 1) * 512],
                             start=True, stop=True).then_inc(MM.h, 1)
                    MM.v += 1
                G[(key, "mm", r, ch)] = (MM, MM.v)

                @block.vector
                def _(v, r=r, ch=ch):
                    guard(v, (key, "mm", r, ch))
                    v.tensor_copy(dsts[r][:, ch * 512:(ch + 1) * 512],
                                  pb[4 + ch % 2][:, :]).then_inc(EV.h, 1)
                    EV.v += 1
                G[(key, "ev", r, ch)] = (EV, EV.v)
        G[(key, "done")] = G[(key, "ev", 1, 3)]

    build_bcast_tiles("gb1", [0, D], [G1T, B1T], ("g1rows",))

    TMC = arf32(73728, D)

    def emit_ln_stack(tag, src_d, store_hd, gt, bt, extra_first):
        for tt in range(NT):
            xa = arf32(0 if tt % 2 == 0 else 8192, D)

            @block.sync
            def _(q, tt=tt, xa=xa):
                guard(q, (tag, "xfree", tt - 2))
                if tt < 2:
                    for w_ in extra_first:
                        guard(q, w_)
                dq(q, LD, xa[:, :], src_d[tt * 128:(tt + 1) * 128, :])
            G[(tag, "ld", tt)] = (LD, LD.v)

            G[(tag, "applyfree", tt)] = G.get((tag, "st1", tt - 1))
            ln_stats_apply(tag, tt, xa, TMC, accum_col=9)
            G[(tag, "xfree", tt)] = G[(tag, "h", tt)]

            if store_hd is not None:
                @block.sync
                def _(q, tt=tt):
                    guard(q, (tag, "h", tt))
                    dq(q, ST, store_hd[tt * 128:(tt + 1) * 128, :], TMC[:, :])
                G[(tag, "hst", tt)] = (ST, ST.v)
                G[(tag, "st1", tt)] = (ST, ST.v)
            else:
                G[(tag, "st1", tt)] = G[(tag, "h", tt)]

            @block.scalar
            def _(s, tt=tt):
                guard(s, (tag, "h", tt), (tag, "sqfree2", tt - 1),
                      (tag, "var2", tt - 1))
                s.activation(SQS[:, :], TMC[:, :], AF.Square,
                             accum_out=stats[:, 10:11]).then_inc(AC.h, 1)
                AC.v += 1
            G[(tag, "sq2", tt)] = (AC, AC.v)

            @block.vector
            def _(v, tt=tt):
                guard(v, (tag, "sq2", tt))
                v.tensor_scalar_mul(stats[:, 12:13], stats[:, 9:10], 1.0 / D
                                    ).then_inc(EV.h, 1); EV.v += 1
                v.tensor_scalar_mul(stats[:, 13:14], stats[:, 10:11], 1.0 / D
                                    ).then_inc(EV.h, 1); EV.v += 1
                v.wait_ge(EV.h, EV.v)
                v.tensor_tensor(stats[:, 14:15], stats[:, 12:13],
                                stats[:, 12:13], ALU.mult).then_inc(EV.h, 1)
                EV.v += 1
                v.wait_ge(EV.h, EV.v)
                v.tensor_tensor(stats[:, 15:16], stats[:, 13:14],
                                stats[:, 14:15], ALU.subtract).then_inc(EV.h, 1)
                EV.v += 1
            G[(tag, "var2", tt)] = (EV, EV.v)

            @block.scalar
            def _(s, tt=tt):
                guard(s, (tag, "var2", tt), ("gp_init",))
                s.activation(stats[:, 16:17], stats[:, 15:16], AF.Sqrt,
                             bias=epsc[:, :]).then_inc(AC.h, 1)
                AC.v += 1
            G[(tag, "std2", tt)] = (AC, AC.v)

            @block.vector
            def _(v, tt=tt, gt=gt, bt=bt):
                guard(v, (tag, "std2", tt))
                v.reciprocal(stats[:, 17:18], stats[:, 16:17]).then_inc(EV.h, 1)
                EV.v += 1
                v.wait_ge(EV.h, EV.v)
                guard(v, (tag, "sqfree", tt - 1), (tag, "st1", tt))
                v.tensor_scalar(TMC[:, :], TMC[:, :], stats[:, 12:13],
                                stats[:, 17:18], ALU.subtract, ALU.mult
                                ).then_inc(EV.h, 1); EV.v += 1
                v.wait_ge(EV.h, EV.v)
                v.tensor_tensor(TMC[:, :], TMC[:, :], gt[:, :], ALU.mult
                                ).then_inc(EV.h, 1); EV.v += 1
                v.wait_ge(EV.h, EV.v)
                v.tensor_tensor(SQS[:, :], TMC[:, :], bt[:, :], ALU.add
                                ).then_inc(EV.h, 1)
                EV.v += 1
            G[(tag, "hn", tt)] = (EV, EV.v)
            transpose_to_fm(tag, tt)
            G[(tag, "sqfree2", tt)] = G[(tag, "sqfree", tt)]
        G[(tag, "fmdone")] = (AC, AC.v)

    emit_ln_stack("l1", x1_d, hin_d, G1T, B1T, [("gb1", "done")])
    G["hn1_fm"] = G[("l1", "fmdone")]
    if "hn1f_d" in dbg:
        @block.sync
        def _(q):
            guard(q, ("hn1_fm",))
            dq(q, ST, hn1f_d.rearrange("(c p) t -> p c t", p=128),
               fmbuf.rearrange("p (c t) -> p c t", c=NC))
    if upto == 35:
        return emit_tail()

    @block.gpsimd
    def _(g):
        guard(g, ("gb1", "done"), ("hn1_fm",))
        for r in range(4):
            dq(g, WT, browR[0:1, r * D:(r + 1) * D], brow_in[r:r + 1, :])
    G["biasrows"] = (WT, WT.v)

    @block.sync
    def _(q):
        guard(q, ("hn1_fm",))
        dq(q, LD, RCOS[:, :], cos_in[:, :])
        dq(q, LD, RSIN[:, :], sin_in[:, :])
    G["ropeld"] = (LD, LD.v)

    emit_fm_proj("Kin", wk, 0, 1, True, sink_shared(k_in_sh),
                 [("hn1_fm",), ("ropeld",), ("biasrows",)])
    gp_kin = GS.v
    emit_fm_proj("Qin", wq, 0, 0, True, sink_dram(qin_d),
                 [("Kin", "alldone"), ("Kin", "sinkdone")])

    def write_vin(name, do, tt, bank, idx):
        @block.vector
        def _(v, do=do, tt=tt, bank=bank, idx=idx):
            guard(v, (name, "mm", idx), (name, "vst", idx - 2))
            if idx < 2:
                guard(v, ("Qin", "sink", NC * NTC - 1))
            v.tensor_copy(PSTG[idx % 2][:, :], bank[:, :]).then_inc(EV.h, 1)
            EV.v += 1
        G[(name, "vev", idx)] = (EV, EV.v)
        G[(name, "pev", idx)] = (EV, EV.v)

        @block.gpsimd
        def _(g, do=do, tt=tt, idx=idx):
            guard(g, (name, "vev", idx))
            src3 = PSTG[idx % 2].rearrange("p (h d) -> p h d", h=4)
            slot_dma(g, v_in_sh[0, 4 * do:4 * do + 4,
                                tt * 128:(tt + 1) * 128, :]
                     .rearrange("h t d -> t h d"),
                     v_in_sh[1, 4 * do:4 * do + 4,
                             tt * 128:(tt + 1) * 128, :]
                     .rearrange("h t d -> t h d"),
                     src3, GS, f"vi{idx}")
        G[(name, "vst", idx)] = (GS, GS.v)

    emit_tm_proj("vin", wv, 2, write_vin, [("Qin", "alldone"), ("biasrows",)])
    gp_vin = GS.v
    _, brkv = barrier(3, GS, max(gp_kin, gp_vin))
    G["br_kv"] = brkv
    if upto == 4:
        return emit_tail()

    KH5 = [arbf(20480, S), arbf(28672, S)]
    VH5 = [arbf(57344, S), arbf(65536, S)]
    QH5 = [arbf(49152, T), arbf(53248, T)]
    MSK = arbf(73728, 8 * 512)
    PB5 = [arbf(81920, 512), arbf(82944, 512)]
    PBP = [arbf(81920, 1024), arbf(83968, 1024)]

    @block.gpsimd
    def _(g):
        guard(g, ("br_kv",), ("Kin", "alldone"), ("Qin", "alldone"))
        dq(g, GS, MSK.rearrange("p (m f) -> p m f", m=8),
           masks_in.rearrange("m p f -> p m f"))
    G["mskld"] = (GS, GS.v)

    for h in range(H):
        kh, vh, qh = KH5[h % 2], VH5[h % 2], QH5[h % 2]

        @block.sync
        def _(q, h=h, kh=kh, vh=vh, qh=qh):
            guard(q, ("iatt_mm", h - 2))
            if h < 2:
                guard(q, ("mskld",), ("vin", "alldone"),
                      ("vin", "pev", 4 * NT - 1), ("vin", "vst", 4 * NT - 1),
                      ("Qin", "sink", NC * NTC - 1))
            dq(q, LD, kh[:, 0:T], k_in_sh[0, h * 128:(h + 1) * 128, :])
            dq(q, LD, kh[:, T:S], k_in_sh[1, h * 128:(h + 1) * 128, :])
            dq(q, LD, qh[:, :], qin_d[h * 128:(h + 1) * 128, :])
            for half in range(2):
                dq(q, LD, vh.rearrange("p (kt d) -> p kt d", kt=2 * NT)
                   [:, half * NT:(half + 1) * NT, :],
                   v_in_sh[half, h].rearrange("(kt p) d -> p kt d", p=128))
        G[("iatt_ld", h)] = (LD, LD.v)

        mm0, ac0, ev0 = MM.v, AC.v, EV.v
        _Moff = lambda qc: 8 * qc * qc + 8 * qc
        _Aoff = lambda qc: 2 * qc * qc + 2 * qc

        @block.tensor
        def _(t, h=h, kh=kh, vh=vh, qh=qh, mm0=mm0, ac0=ac0, ev0=ev0):
            guard(t, ("iatt_ld", h))
            for qc in range(NTC):
                blocks = [(sh_, j) for sh_ in range(2)
                          for j in range(4 * qc + 4)]
                nbp = 4 * qc + 4
                acq0 = ac0 + _Aoff(qc)

                def zav(bpz, qc=qc, blocks=blocks, nbp=nbp, acq0=acq0):
                    for half_ in range(2):
                        bi = 2 * bpz + half_
                        sh_, j = blocks[bi]
                        if bi == 0 and not (h == 0 and qc == 0):
                            t.wait_ge(EV.h, ev0 + 10 * qc)
                        jrel = j - 4 * qc
                        if jrel >= 0:
                            t.wait_ge(EV.h,
                                      ev0 + 10 * qc + sh_ * 4 + jrel + 1)
                        else:
                            t.wait_ge(AC.h, acq0 + bpz + 1)
                        t.matmul(pb[2][:, :], ones128[:, :],
                                 PBP[bpz % 2][:, half_ * 512:(half_ + 1) * 512],
                                 start=(bi == 0), stop=(bi == 2 * nbp - 1))
                        t.matmul(pb[3][:, :],
                                 vh[:, (sh_ * NT + j) * 128:
                                    (sh_ * NT + j + 1) * 128],
                                 PBP[bpz % 2][:, half_ * 512:(half_ + 1) * 512],
                                 start=(bi == 0), stop=(bi == 2 * nbp - 1)
                                 ).then_inc(MM.h, 1)
                        MM.v += 1

                for bp in range(nbp):
                    bankp = pbA if bp % 2 == 0 else pbB
                    t.wait_ge(AC.h, acq0 + max(0, bp - 1))
                    for half_ in range(2):
                        bi = 2 * bp + half_
                        sh_, j = blocks[bi]
                        t.matmul(bankp[:, half_ * 512:(half_ + 1) * 512],
                                 kh[:, (sh_ * NT + j) * 128:
                                    (sh_ * NT + j + 1) * 128],
                                 qh[:, qc * 512:(qc + 1) * 512],
                                 start=True, stop=True).then_inc(MM.h, 1)
                        MM.v += 1
                    if bp >= 1:
                        zav(bp - 1)
                zav(nbp - 1)

        @block.scalar
        def _(s, h=h, mm0=mm0):
            for qc in range(NTC):
                mmq0 = mm0 + _Moff(qc)
                for bp in range(4 * qc + 4):
                    s.wait_ge(MM.h, mmq0 + 2 * (bp + 1) + 2 * max(0, bp - 1))
                    bankp = pbA if bp % 2 == 0 else pbB
                    s.activation(PBP[bp % 2][:, :], bankp[:, :], AF.Exp,
                                 bias=nci[:, :], scale=ISQ).then_inc(AC.h, 1)
                    AC.v += 1

        @block.vector
        def _(v, h=h, mm0=mm0, ac0=ac0):
            for qc in range(NTC):
                nbp = 4 * qc + 4
                acq0 = ac0 + _Aoff(qc)
                for sh_ in range(2):
                    for jrel in range(4):
                        bi = sh_ * nbp + 4 * qc + jrel
                        bp, half_ = bi // 2, bi % 2
                        v.wait_ge(AC.h, acq0 + bp + 1)
                        mask_id = sh_ * 4 + jrel
                        v.tensor_tensor(
                            PBP[bp % 2][:, half_ * 512:(half_ + 1) * 512],
                            PBP[bp % 2][:, half_ * 512:(half_ + 1) * 512],
                            MSK[:, mask_id * 512:(mask_id + 1) * 512],
                            ALU.mult).then_inc(EV.h, 1)
                        EV.v += 1
                v.wait_ge(MM.h, mm0 + _Moff(qc) + 16 * qc + 16)
                v.reciprocal(invzf[:, :], pb[2][:, :]).then_inc(EV.h, 1)
                EV.v += 1
                v.wait_ge(EV.h, EV.v)
                v.tensor_tensor(
                    fmbuf[:, h * T + qc * 512:h * T + (qc + 1) * 512],
                    pb[3][:, :], invzf[:, :], ALU.mult).then_inc(EV.h, 1)
                EV.v += 1
        assert MM.v == mm0 + 160 and AC.v == ac0 + 40 and EV.v == ev0 + 40
        G[("iatt_mm", h)] = (MM, MM.v)
        G[("iatt_ev", h)] = (EV, EV.v)
    G["att_fm"] = (EV, EV.v)
    if upto == 5:
        return emit_tail()

    emit_tm_proj("aproj", wo2, 3, write_resid(hin_d, h2_d, ("att_fm",)), [("att_fm",)])
    G["h2_done"] = (ST, ST.v)

    @block.gpsimd
    def _(g):
        guard(g, ("h2_done",), ("aproj", "alldone"))
        dq(g, WT, browR[0:1, 0:D], brow_in[6:7, :])
        dq(g, WT, browR[0:1, D:2 * D], brow_in[7:8, :])
    G["g2rows"] = (WT, WT.v)

    G2T = arbf(36864, D)
    B2T = arbf(40960, D)
    build_bcast_tiles("gb2", [0, D], [G2T, B2T], ("g2rows",))
    emit_ln_stack("l2", h2_d, None, G2T, B2T, [("gb2", "done")])
    G["hn2_fm"] = G[("l2", "fmdone")]
    if "hn2f_d" in dbg:
        @block.sync
        def _(q):
            guard(q, ("hn2_fm",))
            dq(q, ST, hn2f_d.rearrange("(c p) t -> p c t", p=128),
               fmbuf.rearrange("p (c t) -> p c t", c=NC))
    if upto == 6:
        return emit_tail()

    WGB = arbf(84992, NC * 4)

    @block.gpsimd
    def _(g):
        guard(g, ("hn2_fm",))
        dq(g, WT, WGB.rearrange("p (c j) -> p c j", c=NC),
           wg_in.rearrange("(c p) j -> p c j", p=128))
    G["wtg"] = (WT, WT.v)

    for tt in range(NT):
        @block.tensor
        def _(t, tt=tt):
            guard(t, ("wtg",), ("gdone", tt - 1))
            for c in range(NC):
                ins = t.matmul(pb[4][:, 0:4],
                               fmbuf[:, c * T + tt * 128:c * T + (tt + 1) * 128],
                               WGB[:, c * 4:(c + 1) * 4],
                               start=(c == 0), stop=(c == NC - 1))
            ins.then_inc(MM.h, 1)
            MM.v += 1
        G[("gmm", tt)] = (MM, MM.v)

        @block.vector
        def _(v, tt=tt):
            guard(v, ("gmm", tt))
            v.tensor_copy(graw[:, :], pb[4][:, 0:4]).then_inc(EV.h, 1)
            EV.v += 1
            v.wait_ge(EV.h, EV.v)
            v.reduce_max(gst[:, 0:1], graw[:, :], axis=AXX, negate=True
                         ).then_inc(EV.h, 1)
            EV.v += 1
        G[("gnm", tt)] = (EV, EV.v)

        @block.scalar
        def _(s, tt=tt):
            guard(s, ("gnm", tt))
            s.activation(graw[:, :], graw[:, :], AF.Exp, bias=gst[:, 0:1],
                         accum_out=gst[:, 1:2]).then_inc(AC.h, 1)
            AC.v += 1
        G[("gexp", tt)] = (AC, AC.v)

        @block.vector
        def _(v, tt=tt):
            guard(v, ("gexp", tt))
            v.reciprocal(gst[:, 2:3], gst[:, 1:2]).then_inc(EV.h, 1); EV.v += 1
            v.wait_ge(EV.h, EV.v)
            v.tensor_scalar_mul(graw[:, :], graw[:, :], gst[:, 2:3]
                                ).then_inc(EV.h, 1); EV.v += 1
            v.wait_ge(EV.h, EV.v)
            v.tensor_tensor(gst[:, 12:16], graw[:, :], tieb[:, :], ALU.add
                            ).then_inc(EV.h, 1); EV.v += 1
            v.wait_ge(EV.h, EV.v)
            v.reduce_max(gst[:, 3:4], gst[:, 12:16], axis=AXX).then_inc(EV.h, 1)
            EV.v += 1
            v.wait_ge(EV.h, EV.v)
            v.tensor_scalar(gst[:, 8:12], gst[:, 12:16], gst[:, 3:4], None,
                            ALU.is_equal).then_inc(EV.h, 1); EV.v += 1
            v.wait_ge(EV.h, EV.v)
            v.scalar_tensor_tensor(gst[:, 12:16], gst[:, 8:12], -2.0,
                                   gst[:, 12:16], ALU.mult, ALU.add
                                   ).then_inc(EV.h, 1); EV.v += 1
            v.wait_ge(EV.h, EV.v)
            v.reduce_max(gst[:, 4:5], gst[:, 12:16], axis=AXX).then_inc(EV.h, 1)
            EV.v += 1
            v.wait_ge(EV.h, EV.v)
            v.tensor_scalar(gst[:, 16:20], gst[:, 12:16], gst[:, 4:5], None,
                            ALU.is_equal).then_inc(EV.h, 1); EV.v += 1
            v.wait_ge(EV.h, EV.v)
            v.reduce_max(gst[:, 21:22], graw[:, :], axis=AXX).then_inc(EV.h, 1)
            EV.v += 1
            v.tensor_tensor(gst[:, 12:16], graw[:, :], gst[:, 16:20], ALU.mult
                            ).then_inc(EV.h, 1); EV.v += 1
            v.wait_ge(EV.h, EV.v)
            v.reduce_sum(gst[:, 22:23], gst[:, 12:16], axis=AXX).then_inc(EV.h, 1)
            EV.v += 1
        G[("gtop", tt)] = (EV, EV.v)

        @block.scalar
        def _(s, tt=tt):
            guard(s, ("gtop", tt))
            s.activation(gst[:, 5:7], gst[:, 21:23], AF.Exp,
                         accum_out=gst[:, 7:8]).then_inc(AC.h, 1)
            AC.v += 1
        G[("gexp2", tt)] = (AC, AC.v)

        @block.vector
        def _(v, tt=tt):
            guard(v, ("gexp2", tt))
            v.reciprocal(gst[:, 20:21], gst[:, 7:8]).then_inc(EV.h, 1)
            EV.v += 1
            v.wait_ge(EV.h, EV.v)
            v.tensor_scalar_mul(gst[:, 5:7], gst[:, 5:7], gst[:, 20:21]
                                ).then_inc(EV.h, 1); EV.v += 1
            v.wait_ge(EV.h, EV.v)
            v.tensor_scalar_mul(combf[:, :], gst[:, 8:12], gst[:, 5:6]
                                ).then_inc(EV.h, 1); EV.v += 1
            v.tensor_scalar_mul(gst[:, 16:20], gst[:, 16:20], gst[:, 6:7]
                                ).then_inc(EV.h, 1); EV.v += 1
            v.wait_ge(EV.h, EV.v)
            v.tensor_tensor(combf[:, :], combf[:, :], gst[:, 16:20], ALU.add
                            ).then_inc(EV.h, 1); EV.v += 1
            v.wait_ge(EV.h, EV.v)
            v.tensor_copy(combb[:, :], combf[:, :]).then_inc(EV.h, 1)
            EV.v += 1
        G[("gcomb", tt)] = (EV, EV.v)

        @block.tensor
        def _(t, tt=tt):
            guard(t, ("gcomb", tt), ("gtrev", tt - 1))
            if tt == 0:
                guard(t, ("l2", "tb", NT - 1, 2), ("l2", "tb", NT - 1, 3))
            t.transpose(ptr[0][0:4, 0:128], combb[:, :], ident[:, :]
                        ).then_inc(MM.h, 1)
            MM.v += 1
        G[("gtr", tt)] = (MM, MM.v)

        @block.vector
        def _(v, tt=tt):
            guard(v, ("gtr", tt))
            v.tensor_copy(combT[0:4, tt * 128:(tt + 1) * 128],
                          ptr[0][0:4, 0:128]).then_inc(EV.h, 1)
            EV.v += 1
        G[("gtrev", tt)] = (EV, EV.v)
        G[("gdone", tt)] = (EV, EV.v)
    G["gate_done"] = (EV, EV.v)
    if upto == 65:
        return emit_tail()

    X7 = [arf32(0, 512), arf32(2048, 512)]
    H7 = [arf32(4096, 512), arf32(6144, 512)]
    HMID = arbf(8192, NC * 1024)
    W1C = [arbf(40960, NC * 128), arbf(49152, NC * 128)]
    W2R = [arbf(57344, 8 * 512), arbf(65536, 8 * 512)]
    Y7 = [arf32(73728, 512), arf32(75776, 512)]
    GE7 = [arf32(77824, 512), arf32(79872, 512)]
    CB7 = arbf(81920, 2 * 512)
    BE1R = arbf(86016, 2048)

    for half in range(2):
        tt0 = half * 8
        for e in range(n_experts):
            he = half * E + e

            pWT, pMM, pEV, pAC, pLD, pGS = WT.v, MM.v, EV.v, AC.v, LD.v, GS.v
            pWT += 16
            G[("be1ld", he)] = (WT, pWT)
            for tci in range(2):
                pMM += 1
                G[("cbmm", he, tci)] = (MM, pMM)
            for tci in range(2):
                pEV += 1
                G[("cbev", he, tci)] = (EV, pEV)
            G[("cbdone", he)] = (EV, pEV)
            for ic in range(NC):
                w1i = he * NC + ic
                pWT += 64
                G[("w1wt", w1i)] = (WT, pWT)
                for tci in range(2):
                    idx = ic * 2 + tci
                    fi = he * 32 + idx
                    pMM += 1
                    G[("w1mm", he, idx)] = (MM, pMM)
                    if tci == 1:
                        G[("w1use", w1i)] = (MM, pMM)
                    pAC += 1
                    G[("hmge", he, idx)] = (AC, pAC)
                    G[("hmgeF", fi)] = (AC, pAC)
                    pEV += 1
                    G[("hmsc", he, idx)] = (EV, pEV)
                    G[("hmscF", fi)] = (EV, pEV)
            G[("hmid_done", he)] = (EV, pEV)
            G[("be1free", he)] = (MM, pMM)
            G[("cbfree", he)] = (EV, pEV)
            for do in range(4):
                for hw in range(2):
                    w2i = he * 8 + do * 2 + hw
                    pWT += 16
                    G[("w2wt", w2i)] = (WT, pWT)
                for ti in range(8):
                    idx = do * 8 + ti
                    yc = he * 32 + idx
                    pMM += 1
                    G[("w2mmv", he, idx)] = (MM, pMM)
                    if ti == 7:
                        G[("w2use", he * 8 + do * 2)] = (MM, pMM)
                        G[("w2use", he * 8 + do * 2 + 1)] = (MM, pMM)
                    pLD += 32 if e == 0 else 16
                    G[("yld", he, idx)] = (LD, pLD)
                    pEV += 2 if e == 0 else 1
                    G[("yadd", he, idx)] = (EV, pEV)
                    G[("yevF", yc)] = (EV, pEV)
                    G[("yldF", yc)] = (EV, pEV)
                    pGS += 16
                    G[("ystF", yc)] = (GS, pGS)
                    G[("ywr", half, e, idx)] = (GS, pGS)
            G[("hmuse", he)] = (MM, pMM)

            @block.gpsimd
            def _(g, half=half, e=e, he=he, tt0=tt0):
                guard(g, ("be1free", he - 1), ("gate_done",))
                dq(g, WT, BE1R[0:1, :], be1_in[e:e + 1, :])
                for ic in range(NC):
                    w1i = he * NC + ic
                    guard(g, ("w1use", w1i - 2))
                    if w1i < 2:
                        guard(g, ("hn2_fm",), ("gate_done",))
                    load_wcol(g, W1C[w1i % 2], we1[e], ic * 128)
                for do in range(4):
                    for hw in range(2):
                        w2i = he * 8 + do * 2 + hw
                        guard(g, ("w2use", w2i - 2))
                        dq(g, WT,
                           W2R[w2i % 2].rearrange("p (c j) -> p c j", c=8),
                           we2[e, :, do * 512:(do + 1) * 512].rearrange(
                               "(c p) j -> p c j", p=128)
                           [:, hw * 8:(hw + 1) * 8, :])
                    for ti in range(8):
                        idx = do * 8 + ti
                        tt = tt0 + ti
                        guard(g, ("yadd", he, idx))
                        dq(g, GS, y_out[tt * 128:(tt + 1) * 128,
                                        do * 512:(do + 1) * 512],
                           Y7[idx % 2][:, :])

            @block.tensor
            def _(t, half=half, e=e, he=he, tt0=tt0):
                guard(t, ("gate_done",), ("cbfree", he - 1))
                for tci in range(2):
                    tc = half * 2 + tci
                    t.matmul(pb[4 + tci % 2][:, :],
                             sel4[0:4, e * 128:(e + 1) * 128],
                             combT[0:4, tc * 512:(tc + 1) * 512],
                             start=True, stop=True).then_inc(MM.h, 1)
                    MM.v += 1
                for ic in range(NC):
                    w1i = he * NC + ic
                    wb = W1C[w1i % 2]
                    for tci in range(2):
                        tc = half * 2 + tci
                        idx = ic * 2 + tci
                        fi = he * 32 + idx
                        bank = pb[idx % 2]
                        guard(t, ("w1wt", w1i), ("be1ld", he),
                              ("cbdone", he), ("hmgeF", fi - 2))
                        if idx == 0:
                            guard(t, ("hmuse", he - 1))
                        for c in range(NC):
                            t.matmul(bank[:, :], wb[:, c * 128:(c + 1) * 128],
                                     fmbuf[:, c * T + tc * 512:
                                           c * T + (tc + 1) * 512],
                                     start=(c == 0), stop=False)
                        t.matmul(bank[:, :],
                                 BE1R[0:1, ic * 128:(ic + 1) * 128],
                                 onesr[0:1, :], start=False, stop=True
                                 ).then_inc(MM.h, 1)
                        MM.v += 1
                for do in range(4):
                    w2i0 = he * 8 + do * 2
                    w2i1 = w2i0 + 1
                    for ti in range(8):
                        tt = tt0 + ti
                        idx = do * 8 + ti
                        yc = he * 32 + idx
                        bank = pb[2 + idx % 2]
                        guard(t, ("w2wt", w2i0), ("hmid_done", he),
                              ("yevF", yc - 2))
                        for c in range(NC):
                            if c == 8:
                                guard(t, ("w2wt", w2i1))
                            wr = W2R[(w2i0 if c < 8 else w2i1) % 2]
                            ins = t.matmul(bank[:, :],
                                           HMID.rearrange("p (c x) -> p c x",
                                                          c=NC)
                                           [:, c, ti * 128:(ti + 1) * 128],
                                           wr[:, (c % 8) * 512:
                                              (c % 8 + 1) * 512],
                                           start=(c == 0),
                                           stop=(e != 0 and c == NC - 1))
                        if e == 0:
                            ins = t.matmul(bank[:, :],
                                           combT[0:4, tt * 128:(tt + 1) * 128],
                                           be2r[0:4, do * 512:(do + 1) * 512],
                                           start=False, stop=True)
                        ins.then_inc(MM.h, 1)
                        MM.v += 1

            @block.scalar
            def _(s, he=he):
                for ic in range(NC):
                    for tci in range(2):
                        idx = ic * 2 + tci
                        fi = he * 32 + idx
                        guard(s, ("w1mm", he, idx), ("hmscF", fi - 2))
                        s.activation(GE7[idx % 2][:, :], pb[idx % 2][:, :],
                                     AF.Gelu).then_inc(AC.h, 1)
                        AC.v += 1

            @block.sync
            def _(q, he=he, e=e, half=half, tt0=tt0):
                for do in range(4):
                    for ti in range(8):
                        idx = do * 8 + ti
                        tt = tt0 + ti
                        yc = he * 32 + idx
                        guard(q, ("yldF", yc - 2), ("ywr", half, e - 1, idx))
                        if half == 0 and e == 0 and idx < 2:
                            guard(q, ("gate_done",))
                        if e == 0:
                            dq(q, LD, X7[idx % 2][:, :],
                               x1_d[tt * 128:(tt + 1) * 128,
                                    do * 512:(do + 1) * 512])
                            dq(q, LD, H7[idx % 2][:, :],
                               h2_d[tt * 128:(tt + 1) * 128,
                                    do * 512:(do + 1) * 512])
                        else:
                            dq(q, LD, X7[idx % 2][:, :],
                               y_out[tt * 128:(tt + 1) * 128,
                                     do * 512:(do + 1) * 512])

            @block.vector
            def _(v, he=he, e=e):
                for tci in range(2):
                    guard(v, ("cbmm", he, tci))
                    v.tensor_copy(CB7[:, tci * 512:(tci + 1) * 512],
                                  pb[4 + tci % 2][:, :]).then_inc(EV.h, 1)
                    EV.v += 1
                for ic in range(NC):
                    for tci in range(2):
                        idx = ic * 2 + tci
                        guard(v, ("hmge", he, idx))
                        if idx < 2:
                            guard(v, ("hmuse", he - 1))
                        v.tensor_tensor(HMID[:, ic * 1024 + tci * 512:
                                             ic * 1024 + (tci + 1) * 512],
                                        GE7[idx % 2][:, :],
                                        CB7[:, tci * 512:(tci + 1) * 512],
                                        ALU.mult).then_inc(EV.h, 1)
                        EV.v += 1
                for do in range(4):
                    for ti in range(8):
                        idx = do * 8 + ti
                        yc = he * 32 + idx
                        bank = pb[2 + idx % 2]
                        guard(v, ("w2mmv", he, idx), ("yld", he, idx),
                              ("ystF", yc - 2))
                        v.tensor_add(Y7[idx % 2][:, :], bank[:, :],
                                     X7[idx % 2][:, :]).then_inc(EV.h, 1)
                        EV.v += 1
                        if e == 0:
                            v.wait_ge(EV.h, EV.v)
                            v.tensor_add(Y7[idx % 2][:, :], Y7[idx % 2][:, :],
                                         H7[idx % 2][:, :]).then_inc(EV.h, 1)
                            EV.v += 1

            assert (WT.v, MM.v, EV.v, AC.v, LD.v, GS.v) == \
                (pWT, pMM, pEV, pAC, pLD, pGS), ("ph7 ledger", he)
            if upto == 71 + he or (upto in (172, 173, 176) and he == 1) or \
                    (upto == 70 and he == 0):
                return emit_tail()

    return emit_tail()


_CACHE = {}


def get_nc(dbg=(), upto=99):
    key = (tuple(sorted(dbg)), upto)
    if key not in _CACHE:
        _CACHE[key] = build_nc(dbg=dbg, upto=upto)
    return _CACHE[key]


def _rope_tables(parity):
    pos = np.zeros(T, np.float64)
    for j in range(LBLK):
        pos[j * 128:(j + 1) * 128] = (2 * j + parity) * 128 + np.arange(128)
    inv = 1.0 / (10000.0 ** (np.arange(0, 128, 2, dtype=np.float64) / 128.0))
    freqs = np.concatenate([inv, inv])[:, None] * pos[None, :]
    sin = np.sin(freqs)
    sin[:64, :] *= -1.0
    return (np.cos(freqs).astype(np.float32), sin.astype(np.float32))


def _masks(parity):
    m = np.zeros((8, 128, 512), np.float32)
    tri = (np.arange(128)[:, None] <= np.arange(128)[None, :]).astype(np.float32)
    for s in range(2):
        for jrel in range(4):
            blk = np.zeros((128, 512), np.float32)
            gk = 2 * jrel + s
            for ip in range(4):
                gq = 2 * ip + parity
                sub = slice(ip * 128, (ip + 1) * 128)
                if gk < gq:
                    blk[:, sub] = 1.0
                elif gk == gq:
                    blk[:, sub] = tri
            m[s * 4 + jrel] = blk
    return m


def _make_in_maps(inputs):
    xs = np.ascontiguousarray(inputs["L0_attn_x"])
    brow = np.ascontiguousarray(np.stack(
        [inputs["L0_ffn_bq"], inputs["L0_ffn_bk"], inputs["L0_ffn_bv"],
         inputs["L0_ffn_bo"], inputs["L0_ffn_ln1_w"], inputs["L0_ffn_ln1_b"],
         inputs["L0_ffn_ln2_w"], inputs["L0_ffn_ln2_b"]]).astype(np.float32))
    import os as _os
    nonce = int.from_bytes(_os.urandom(4), "little") % (1 << 28) + (1 << 20)
    shared = {
        "w_qkv": np.ascontiguousarray(inputs["L0_attn_w_qkv"]),
        "w_o": np.ascontiguousarray(inputs["L0_attn_w_o"]),
        "wq": np.ascontiguousarray(inputs["L0_ffn_wq"]),
        "wk": np.ascontiguousarray(inputs["L0_ffn_wk"]),
        "wv": np.ascontiguousarray(inputs["L0_ffn_wv"]),
        "wo2": np.ascontiguousarray(inputs["L0_ffn_wo"]),
        "brow_in": brow,
        "wg": np.ascontiguousarray(inputs["L0_ffn_w_gate"]),
        "we1": np.ascontiguousarray(inputs["L0_ffn_we1"]),
        "be1": np.ascontiguousarray(inputs["L0_ffn_be1"]),
        "we2": np.ascontiguousarray(inputs["L0_ffn_we2"]),
        "be2": np.ascontiguousarray(inputs["L0_ffn_be2"]),
    }
    in_maps = []
    for core in range(N_CORES):
        b, p = core // 2, core % 2
        xb = xs[b].reshape(NBLK, 128, D)[p::2].reshape(T, D)
        cos, sin = _rope_tables(p)
        im = dict(shared)
        im["x_shard"] = np.ascontiguousarray(xb)
        im["rope_cos"] = cos
        im["rope_sin"] = sin
        im["masks"] = _masks(p)
        im["nonce"] = np.array([[nonce]], np.int32)
        im["slot"] = np.array([[p]], np.int32)
        in_maps.append(im)
    return in_maps


def kernel(**inputs):
    nc = get_nc()
    in_maps = _make_in_maps(inputs)
    res = run_bass_kernel_spmd(nc, in_maps, core_ids=list(range(8)))
    out = np.zeros((B, S, D), np.float32)
    for core in range(8):
        b, p = core // 2, core % 2
        yb = np.asarray(res.results[core]["y"]).reshape(LBLK, 128, D)
        out[b].reshape(NBLK, 128, D)[p::2] = yb
    return out


# version 11
# speedup vs baseline: 1.9621x; 1.2452x over previous
import math
from contextlib import ExitStack

import numpy as np
import ml_dtypes

import concourse.bass as bass
import concourse.mybir as mybir
from concourse.bass_utils import run_bass_kernel_spmd

F32 = mybir.dt.float32
BF16 = mybir.dt.bfloat16
I32 = mybir.dt.int32
AF = mybir.ActivationFunctionType
ALU = mybir.AluOpType
AXX = mybir.AxisListType.X
BF = ml_dtypes.bfloat16

B, S, D = 4, 4096, 2048
H = 16
NBLK, LBLK = 32, 16
T = 2048
NC = 16
NT = 16
NTC = 4
E, I_FF = 4, 2048
EPS = 1e-5
ISQ = 1.0 / math.sqrt(128.0)
C_OUT = 8.0
C_IN = 24.0
POLL_MAX = 500000
N_CORES = 8


class SemC:
    def __init__(self, h):
        self.h = h
        self.v = 0


def build_nc(dbg=(), upto=99, skipo=False, n_experts=E):
    nc = bass.Bass("TRN2", target_bir_lowering=False)
    es = ExitStack()
    G = {}

    def guard(eng, *keys):
        for k in keys:
            if isinstance(k, tuple) and len(k) == 1:
                k = k[0]
            g = G.get(k)
            if g is not None:
                eng.wait_ge(g[0].h, g[1])

    def dram(name, shape, dtype, kind="Internal", **kw):
        k = "ExternalOutput" if name in dbg else kind
        return nc.dram_tensor(name, shape, dtype, kind=k, **kw).ap()

    x_in = dram("x_shard", [T, D], F32, kind="ExternalInput")
    w_qkv = dram("w_qkv", [D, 3 * D], F32, kind="ExternalInput")
    w_o = dram("w_o", [D, D], F32, kind="ExternalInput")
    wq = dram("wq", [D, D], F32, kind="ExternalInput")
    wk = dram("wk", [D, D], F32, kind="ExternalInput")
    wv = dram("wv", [D, D], F32, kind="ExternalInput")
    wo2 = dram("wo2", [D, D], F32, kind="ExternalInput")
    brow_in = dram("brow_in", [8, D], F32, kind="ExternalInput")
    wg_in = dram("wg", [D, E], F32, kind="ExternalInput")
    we1 = dram("we1", [E, D, I_FF], F32, kind="ExternalInput")
    be1_in = dram("be1", [E, I_FF], F32, kind="ExternalInput")
    we2 = dram("we2", [E, I_FF, D], F32, kind="ExternalInput")
    be2_in = dram("be2", [E, D], F32, kind="ExternalInput")
    cos_in = dram("rope_cos", [128, T], F32, kind="ExternalInput")
    sin_in = dram("rope_sin", [128, T], F32, kind="ExternalInput")
    masks_in = dram("masks", [8, 128, 512], F32, kind="ExternalInput")
    nonce_in = dram("nonce", [1, 1], I32, kind="ExternalInput")
    slot_in = dram("slot", [1, 1], I32, kind="ExternalInput")
    y_out = dram("y", [T, D], F32, kind="ExternalOutput")

    q_out_d = dram("q_out_d", [D, T], BF16)
    v_out_d = dram("v_out_d", [D, T], BF16)
    x1_d = dram("x1_d", [T, D], F32)
    hin_d = dram("hin_d", [T, D], F32)
    h2_d = dram("h2_d", [T, D], F32)
    qin_d = dram("qin_d", [D, T], BF16)
    xnf_d = dram("xnf_d", [D, T], BF16)
    hn1f_d = dram("hn1f_d", [D, T], BF16)
    hn2f_d = dram("hn2f_d", [D, T], BF16)

    k_out_sh = dram("k_out_sh", [2, D, T], BF16, addr_space="Shared")
    k_in_sh = dram("k_in_sh", [2, D, T], BF16, addr_space="Shared")
    v_in_sh = dram("v_in_sh", [2, H, T, 128], BF16, addr_space="Shared")
    cs_sh = dram("cs_sh", [2, H, S], BF16, addr_space="Shared")
    flags_sh = dram("flags_sh", [2, 16], I32, addr_space="Shared")

    ident_np = np.zeros((128, 128), BF)
    np.fill_diagonal(ident_np, 1)
    ident_c = nc.inline_tensor(np.ascontiguousarray(ident_np), "ident_c").ap()
    sel4_np = np.zeros((4, 4 * 128), BF)
    for e_ in range(4):
        sel4_np[e_, e_ * 128:(e_ + 1) * 128] = 1
    sel4_c = nc.inline_tensor(np.ascontiguousarray(sel4_np), "sel4_c").ap()
    ones_c = nc.inline_tensor(np.ones((1, 512), BF), "ones_c").ap()
    tieb_np = np.zeros((128, 4), np.float32)
    tieb_np[:, :] = np.array([3e-6, 2e-6, 1e-6, 0.0], np.float32)[None, :]
    tieb_c = nc.inline_tensor(tieb_np, "tieb_c").ap()

    def sb(name, shape, dtype):
        return es.enter_context(nc.sbuf_tensor(name, shape, dtype))

    ident = sb("ident", [128, 128], BF16)
    sel4 = sb("sel4", [4, 4 * 128], BF16)
    onesr = sb("onesr", [1, 512], BF16)
    ones32 = sb("ones32", [128, 32], BF16)
    ones128 = sb("ones128", [128, 128], BF16)
    stats = sb("stats", [128, 64], F32)
    epsc = sb("epsc", [128, 1], F32)
    nco = sb("nco", [128, 1], F32)
    nci = sb("nci", [128, 1], F32)
    zcol = sb("zcol", [128, 8], F32)
    invz = sb("invz", [128, 1], F32)
    invzb = sb("invzb", [128, 32], BF16)
    invzf = sb("invzf", [128, 512], F32)
    graw = sb("graw", [128, 4], F32)
    tieb = sb("tieb", [128, 4], F32)
    gst = sb("gst", [128, 24], F32)
    combf = sb("combf", [128, 4], F32)
    combb = sb("combb", [128, 4], BF16)
    combT = sb("combT", [4, T], BF16)
    be2r = sb("be2r", [4, D], BF16)
    flg = sb("flg", [1, 32], I32)
    myflag = sb("myflag", [1, 16], I32)
    noncev = sb("noncev", [1, 1], I32)
    slotv = sb("slotv", [1, 1], I32)
    fmbuf = sb("fmbuf", [128, NC * T], BF16)
    AR_COLS = 56 * 1024
    arena = sb("arena", [128, AR_COLS], BF16)

    def arf32(off_b, n):
        assert off_b % 4 == 0 and off_b + 4 * n <= AR_COLS * 2, off_b
        return arena[:, off_b // 2: off_b // 2 + 2 * n].bitcast(F32)

    def arbf(off_b, n):
        assert off_b % 2 == 0 and off_b + 2 * n <= AR_COLS * 2, off_b
        return arena[:, off_b // 2: off_b // 2 + n]
    csr0 = arbf(90112, T)
    csr1 = arbf(94208, T)
    browR = arbf(98304, 4 * D)


    pbA = es.enter_context(nc.psum_tensor("pbA", [128, 1024], F32))
    pbB = es.enter_context(nc.psum_tensor("pbB", [128, 1024], F32))
    pbZ = es.enter_context(nc.psum_tensor("pbZ", [128, 512], F32))
    pbV = es.enter_context(nc.psum_tensor("pbV", [128, 512], F32))
    pb = [pbA[:, 0:512], pbA[:, 512:1024], pbZ[:, :], pbV[:, :],
          pbB[:, 0:512], pbB[:, 512:1024]]
    ptr = [es.enter_context(nc.psum_tensor(f"ptr{i}", [128, 1024], BF16))
           for i in range(2)]

    def sem(name):
        return SemC(es.enter_context(nc.semaphore(name)))

    LD, WT, MM, EV, AC, GP, ST, BR, GS = (sem(n) for n in
                                          "LD WT MM EV AC GP ST BR GS".split())

    REGS = {}

    def get_reg(eng, name):
        key = (eng.engine, name)
        if key not in REGS:
            REGS[key] = eng.alloc_register(name)
        return REGS[key]

    def dq(q, smc, out, in_, **kw):
        q.wait_ge(smc.h, smc.v)
        q.dma_start(out=out, in_=in_, **kw).then_inc(smc.h, 16)
        smc.v += 16
    BARS = [sem(f"PB{i}") for i in range(4)]

    block = es.enter_context(nc.Block())

    def emit_tail():
        fins = [(LD, LD.v), (WT, WT.v), (MM, MM.v), (EV, EV.v), (AC, AC.v),
                (GP, GP.v), (ST, ST.v), (BR, BR.v), (GS, GS.v)]

        @block.sync
        def _(q):
            for smc, vv in fins:
                if vv > 0:
                    q.wait_ge(smc.h, vv)
        es.close()
        return nc

    def barrier(stage, payload_sem, payload_val):
        PB = BARS[stage]

        @block.gpsimd
        def _(g):
            g.wait_ge(payload_sem.h, payload_val)
            slot = get_reg(g, "bar_s")
            g.reg_load(slot, slotv[0:1, 0:1])
            with g.If_eq(slot, 0):
                g.dma_start(out=flags_sh[0:1, stage:stage + 1],
                            in_=myflag[0:1, stage:stage + 1]).then_inc(PB.h, 16)
            with g.Else():
                g.dma_start(out=flags_sh[1:2, stage:stage + 1],
                            in_=myflag[0:1, stage:stage + 1]).then_inc(PB.h, 16)
            it = get_reg(g, "bar_i")
            g.reg_mov(it, 0)
            done = get_reg(g, "bar_d")
            g.reg_mov(done, 0)
            tgt = get_reg(g, "bar_t")
            c = get_reg(g, "bar_c")
            f0 = get_reg(g, "bar_f0")
            f1 = get_reg(g, "bar_f1")
            nn = get_reg(g, "bar_n")
            t1 = get_reg(g, "bar_u")
            t2 = get_reg(g, "bar_v")
            g.reg_load(nn, noncev[0:1, 0:1])

            def cond():
                g.reg_alu(c, done, 0, ALU.is_equal)
                return c
            with g.While(cond):
                g.dma_start(out=flg[0:1, :],
                            in_=flags_sh.rearrange("a b -> (a b)")[None, :]
                            ).then_inc(PB.h, 16)
                g.reg_add(it, it, 1)
                g.reg_mul(tgt, it, 16)
                g.reg_add(tgt, tgt, 16)
                g.wait_ge(PB.h, tgt)
                g.reg_load(f0, flg[0:1, stage:stage + 1])
                g.reg_load(f1, flg[0:1, 16 + stage:17 + stage])
                g.reg_alu(f0, f0, f1, ALU.min)
                g.reg_sub(f0, f0, nn)
                g.reg_alu(t1, f0, stage, ALU.is_ge)
                g.reg_alu(t2, f0, 1000000, ALU.is_le)
                g.reg_alu(t1, t1, t2, ALU.bitwise_and)
                g.reg_alu(t2, it, POLL_MAX, ALU.is_ge)
                g.reg_alu(done, t1, t2, ALU.bitwise_or)
            g.nop().then_inc(BR.h, 1)
            BR.v += 1
        return ("BARR", stage), (BR, BR.v)

    def slot_dma(g, dst0, dst1, src, sem_c, tag, inc=16):
        g.wait_ge(sem_c.h, sem_c.v)
        slot = get_reg(g, "slotreg")
        g.reg_load(slot, slotv[0:1, 0:1])
        with g.If_eq(slot, 0):
            g.dma_start(out=dst0, in_=src).then_inc(sem_c.h, inc)
        with g.Else():
            g.dma_start(out=dst1, in_=src).then_inc(sem_c.h, inc)
        sem_c.v += inc

    @block.sync
    def _(q):
        for dst, src in ((ident, ident_c), (sel4, sel4_c), (onesr, ones_c)):
            dq(q, LD, dst[:, :], src[:, :])
        dq(q, LD, tieb[:, :], tieb_c[:, :])
        dq(q, LD, noncev[:, :], nonce_in[:, :])
        dq(q, LD, slotv[:, :], slot_in[:, :])
    ld_consts = LD.v
    G["consts"] = (LD, ld_consts)

    @block.gpsimd
    def _(g):
        for tile_, val_ in ((ones32, 1.0), (ones128, 1.0), (epsc, EPS),
                            (nco, -C_OUT), (nci, -C_IN)):
            g.memset(tile_[:, :], val_).then_inc(GP.h, 1)
            GP.v += 1
        dq(g, WT, be2r[:, :], be2_in[:, :])
        g.wait_ge(LD.h, ld_consts)
        nn = get_reg(g, "nn0")
        g.reg_load(nn, noncev[0:1, 0:1])
        vv = get_reg(g, "vv0")
        for s_ in range(16):
            g.reg_add(vv, nn, s_)
            g.store(myflag[0:1, s_:s_ + 1], vv)
        g.nop().then_inc(GP.h, 1)
        GP.v += 1
    G["gp_init"] = (GP, GP.v)

    def ln_stats_apply(tag, tt, xa, dst, accum_col=None):
        @block.vector
        def _(v):
            guard(v, (tag, "ld", tt))
            v.reduce_sum(stats[:, 0:1], xa[:, :], axis=AXX).then_inc(EV.h, 1)
            EV.v += 1
        G[(tag, "sum", tt)] = (EV, EV.v)

        @block.scalar
        def _(s):
            guard(s, (tag, "ld", tt), (tag, "sqfree", tt - 1),
                  (tag, "var", tt - 1))
            s.activation(SQS[:, :], xa[:, :], AF.Square,
                         accum_out=stats[:, 1:2]).then_inc(AC.h, 1)
            AC.v += 1
        G[(tag, "sq", tt)] = (AC, AC.v)

        @block.vector
        def _(v):
            guard(v, (tag, "sum", tt), (tag, "sq", tt))
            v.tensor_scalar_mul(stats[:, 2:3], stats[:, 0:1], 1.0 / D
                                ).then_inc(EV.h, 1); EV.v += 1
            v.tensor_scalar_mul(stats[:, 3:4], stats[:, 1:2], 1.0 / D
                                ).then_inc(EV.h, 1); EV.v += 1
            v.wait_ge(EV.h, EV.v)
            v.tensor_tensor(stats[:, 4:5], stats[:, 2:3], stats[:, 2:3],
                            ALU.mult).then_inc(EV.h, 1); EV.v += 1
            v.wait_ge(EV.h, EV.v)
            v.tensor_tensor(stats[:, 5:6], stats[:, 3:4], stats[:, 4:5],
                            ALU.subtract).then_inc(EV.h, 1)
            EV.v += 1
        G[(tag, "var", tt)] = (EV, EV.v)

        @block.scalar
        def _(s):
            guard(s, (tag, "var", tt), ("gp_init",))
            s.activation(stats[:, 6:7], stats[:, 5:6], AF.Sqrt, bias=epsc[:, :]
                         ).then_inc(AC.h, 1)
            AC.v += 1
        G[(tag, "std", tt)] = (AC, AC.v)

        @block.vector
        def _(v):
            guard(v, (tag, "std", tt))
            v.reciprocal(stats[:, 7:8], stats[:, 6:7]).then_inc(EV.h, 1)
            EV.v += 1
            v.wait_ge(EV.h, EV.v)
            guard(v, (tag, "applyfree", tt), (tag, "sq2", tt - 1))
            v.tensor_scalar(dst[:, :], xa[:, :], stats[:, 2:3], stats[:, 7:8],
                            ALU.subtract, ALU.mult).then_inc(EV.h, 1)
            EV.v += 1
            if accum_col is not None:
                v.wait_ge(EV.h, EV.v)
                v.reduce_sum(stats[:, accum_col:accum_col + 1], dst[:, :],
                             axis=AXX).then_inc(EV.h, 1)
                EV.v += 1
        G[(tag, "h", tt)] = (EV, EV.v)

    def transpose_to_fm(tag, tt):
        for g4 in range(4):
            bank = ptr[g4 % 2]

            @block.tensor
            def _(t, g4=g4, bank=bank):
                if g4 == 0:
                    guard(t, (tag, "hn", tt))
                guard(t, (tag, "tb", tt, g4 - 2), (tag, "tb", tt - 1, g4 + 2))
                for c4 in range(4):
                    c = g4 * 4 + c4
                    ins = t.transpose(bank[:, c4 * 128:(c4 + 1) * 128],
                                      SQS[:, c * 128:(c + 1) * 128],
                                      ident[:, :])
                ins.then_inc(MM.h, 1)
                MM.v += 1
            G[(tag, "tmm", tt, g4)] = (MM, MM.v)

            @block.vector
            def _(v, g4=g4, bank=bank):
                guard(v, (tag, "tmm", tt, g4))
                v.tensor_copy(fmbuf.rearrange("p (c t) -> p c t", c=NC)
                              [:, g4 * 4:g4 * 4 + 4, tt * 128:(tt + 1) * 128],
                              bank[:, 0:512].rearrange("p (c t) -> p c t", c=4)
                              [:, :, :]).then_inc(EV.h, 1)
                EV.v += 1
            G[(tag, "tb", tt, g4)] = (EV, EV.v)
        G[(tag, "tr", tt)] = G[(tag, "tmm", tt, 3)]
        G[(tag, "sqfree", tt)] = G[(tag, "tmm", tt, 3)]

    SQS = arbf(16384, D)
    for tt in range(NT):
        xa = arf32(0 if tt % 2 == 0 else 8192, D)

        @block.sync
        def _(q, tt=tt, xa=xa):
            guard(q, ("l0", "xfree", tt - 2))
            dq(q, LD, xa[:, :], x_in[tt * 128:(tt + 1) * 128, :])
        G[("l0", "ld", tt)] = (LD, LD.v)

        G[("l0", "applyfree", tt)] = G.get(("l0", "tr", tt - 1))
        ln_stats_apply("l0", tt, xa, SQS)
        G[("l0", "hn", tt)] = G[("l0", "h", tt)]
        G[("l0", "xfree", tt)] = G[("l0", "h", tt)]
        transpose_to_fm("l0", tt)
    ph0_ac = AC.v
    G["xn_fm"] = (AC, ph0_ac)
    if "xnf_d" in dbg:
        @block.sync
        def _(q):
            q.wait_ge(AC.h, ph0_ac)
            dq(q, ST, xnf_d.rearrange("(c p) t -> p c t", p=128),
               fmbuf.rearrange("p (c t) -> p c t", c=NC))
    if upto == 0:
        return emit_tail()

    WCOL = [arbf(20480, NC * 128), arbf(28672, NC * 128)]
    EVST = [arf32(36864, 512), arf32(38912, 512)]
    PSTG = [arbf(40960, 512), arbf(41984, 512)]
    KT1 = arf32(43008, 512)
    KT2 = arf32(45056, 512)
    RCOS = arf32(49152, T)
    RSIN = arf32(57344, T)

    def load_wcol(g, dst, wsrc, col0):
        src3 = wsrc[:, col0:col0 + 128].rearrange("(c p) j -> p c j", p=128)
        dst3 = dst.rearrange("p (c j) -> p c j", c=NC)
        for hh in range(4):
            dq(g, WT, dst3[:, hh * 4:(hh + 1) * 4, :],
               src3[:, hh * 4:(hh + 1) * 4, :])

    def emit_fm_proj(name, wsrc, col_base, bias_slot, rope_flag, sink,
                     first_waits):
        for do in range(NC):
            wb = WCOL[do % 2]

            @block.gpsimd
            def _(g, do=do, wb=wb):
                guard(g, (name, "mmdone", do - 2))
                if do < 2:
                    for w_ in first_waits:
                        guard(g, w_)
                load_wcol(g, wb, wsrc, col_base + do * 128)
            G[(name, "wt", do)] = (WT, WT.v)

            for tc in range(NTC):
                bank = pb[tc % 2]
                idx = do * NTC + tc

                @block.tensor
                def _(t, do=do, tc=tc, wb=wb, bank=bank, idx=idx):
                    guard(t, (name, "wt", do))
                    if idx == 0:
                        for w_ in first_waits:
                            guard(t, w_)
                    guard(t, (name, "ev", idx - 2))
                    for c in range(NC):
                        ins = t.matmul(bank[:, :], wb[:, c * 128:(c + 1) * 128],
                                       fmbuf[:, c * T + tc * 512:
                                             c * T + (tc + 1) * 512],
                                       start=(c == 0),
                                       stop=(bias_slot is None and c == NC - 1))
                    if bias_slot is not None:
                        ins = t.matmul(bank[:, :],
                                       browR[0:1, bias_slot * D + do * 128:
                                             bias_slot * D + (do + 1) * 128],
                                       onesr[0:1, :], start=False, stop=True)
                    ins.then_inc(MM.h, 1)
                    MM.v += 1
                G[(name, "mm", idx)] = (MM, MM.v)

                stg = PSTG[idx % 2]
                if not rope_flag:
                    @block.vector
                    def _(v, bank=bank, idx=idx, stg=stg):
                        guard(v, (name, "mm", idx), (name, "sink", idx - 2))
                        if idx < 2:
                            for w_ in first_waits:
                                guard(v, w_)
                        v.tensor_copy(stg[:, :], bank[:, :]).then_inc(EV.h, 1)
                        EV.v += 1
                    G[(name, "ev", idx)] = (EV, EV.v)
                    G[(name, "stg", idx)] = (EV, EV.v)
                else:
                    @block.vector
                    def _(v, tc=tc, idx=idx, stg=stg, bank=bank):
                        guard(v, (name, "mm", idx), (name, "rope", idx - 1))
                        if idx < 2:
                            for w_ in first_waits:
                                guard(v, w_)
                        v.tensor_copy(EVST[0][:, :], bank[:, :]
                                      ).then_inc(EV.h, 1)
                        EV.v += 1
                        G[(name, "ev", idx)] = (EV, EV.v)
                        v.wait_ge(EV.h, EV.v)
                        guard(v, (name, "sink", idx - 2))
                        cs_ = RCOS[:, tc * 512:(tc + 1) * 512]
                        sn_ = RSIN[:, tc * 512:(tc + 1) * 512]
                        ev = EVST[0]
                        v.tensor_tensor(KT1[:, :], ev[:, :], cs_, ALU.mult
                                        ).then_inc(EV.h, 1); EV.v += 1
                        v.tensor_copy(KT2[0:64, :], ev[64:128, :]
                                      ).then_inc(EV.h, 1); EV.v += 1
                        v.tensor_copy(KT2[64:128, :], ev[0:64, :]
                                      ).then_inc(EV.h, 1); EV.v += 1
                        v.wait_ge(EV.h, EV.v)
                        v.tensor_tensor(KT2[:, :], KT2[:, :], sn_, ALU.mult
                                        ).then_inc(EV.h, 1); EV.v += 1
                        v.wait_ge(EV.h, EV.v)
                        v.tensor_tensor(stg[:, :], KT1[:, :], KT2[:, :],
                                        ALU.add).then_inc(EV.h, 1)
                        EV.v += 1
                    G[(name, "rope", idx)] = (EV, EV.v)
                    G[(name, "stg", idx)] = (EV, EV.v)

                sink(do, tc, stg, (name, "stg", idx), (name, "sink", idx))
            G[(name, "mmdone", do)] = (MM, MM.v)
        G[(name, "alldone")] = (MM, MM.v)
        sk = G.get((name, "sink", NC * NTC - 1))
        if sk is not None:
            G[(name, "sinkdone")] = sk

    def sink_dram(dst):
        def f(do, tc, stg, ready_key, done_key):
            @block.sync
            def _(q, do=do, tc=tc, stg=stg):
                guard(q, ready_key)
                dq(q, ST, dst[do * 128:(do + 1) * 128, tc * 512:(tc + 1) * 512],
                   stg[:, :])
            G[done_key] = (ST, ST.v)
        return f

    def sink_shared(dst_sh):
        def f(do, tc, stg, ready_key, done_key):
            @block.gpsimd
            def _(g, do=do, tc=tc, stg=stg):
                guard(g, ready_key)
                slot_dma(g,
                         dst_sh[0, do * 128:(do + 1) * 128,
                                tc * 512:(tc + 1) * 512],
                         dst_sh[1, do * 128:(do + 1) * 128,
                                tc * 512:(tc + 1) * 512],
                         stg[:, :], GS, f"sk{do}_{tc}")
            G[done_key] = (GS, GS.v)
        return f

    if upto == 7:
        emit_fm_proj("Kout", w_qkv, D, None, False, sink_dram(q_out_d),
                     [("xn_fm",), ("gp_init",)])
        return emit_tail()
    emit_fm_proj("Kout", w_qkv, D, None, False, sink_shared(k_out_sh),
                 [("xn_fm",), ("gp_init",)])
    gp_kout = GS.v
    if upto == 8:
        return emit_tail()
    emit_fm_proj("Qout", w_qkv, 0, None, False, sink_dram(q_out_d),
                 [("Kout", "alldone"), ("Kout", "sinkdone")])
    emit_fm_proj("Vout", w_qkv, 2 * D, None, False, sink_dram(v_out_d),
                 [("Qout", "alldone"), ("Qout", "sinkdone")])
    G["qv_st"] = (ST, ST.v)
    if upto == 9:
        return emit_tail()
    _, brk = barrier(1, GS, gp_kout)
    G["br_kout"] = brk
    if upto == 1:
        return emit_tail()

    KH2 = [arbf(20480, S), arbf(28672, S)]
    QH2 = [arbf(36864, T), arbf(40960, T)]
    PR2 = [arbf(49152, S), arbf(57344, S)]
    CS32 = arbf(45056, 1024)
    VST2 = [arbf(65536, 512), arbf(66560, 512)]

    CSLAG = 2
    _ord2 = []
    for _qt in range(NT):
        if _qt >= CSLAG:
            _ord2.append(("C", _qt - CSLAG))
        _ord2.append(("S", _qt))
    for _j in range(NT - CSLAG, NT):
        _ord2.append(("C", _j))
    _sv2, _cv2 = {}, {}
    _c = 0
    for _k, _q in _ord2:
        if _k == "S":
            for _p in range(4):
                _c += 1
                _sv2[(_q, _p)] = _c
        else:
            _c += 1
            _cv2[_q] = _c
    MM2TOT = _c

    for h in (range(0) if skipo else range(H)):
        kh, qh = KH2[h % 2], QH2[h % 2]

        @block.sync
        def _(q, h=h, kh=kh, qh=qh):
            guard(q, ("oatt_mm", h - 2))
            if h < 2:
                guard(q, ("br_kout",), ("Vout", "alldone"),
                      ("Qout", "sink", NC * NTC - 1),
                      ("Vout", "sink", NC * NTC - 1))
            dq(q, LD, kh[:, 0:T], k_out_sh[0, h * 128:(h + 1) * 128, :])
            dq(q, LD, kh[:, T:S], k_out_sh[1, h * 128:(h + 1) * 128, :])
            dq(q, LD, qh[:, :], q_out_d[h * 128:(h + 1) * 128, :])
        G[("oatt_ld", h)] = (LD, LD.v)

        mm0, ac0, ev0 = MM.v, AC.v, EV.v

        @block.tensor
        def _(t, h=h, kh=kh, qh=qh, mm0=mm0, ac0=ac0, ev0=ev0):
            guard(t, ("oatt_ld", h))
            firstC = True
            for kind, qt in _ord2:
                if kind == "S":
                    for pair in range(4):
                        bankp = pbA if pair % 2 == 0 else pbB
                        pv = ac0 + 4 * qt + pair - 1
                        if pv > ac0 or h > 0:
                            t.wait_ge(AC.h, pv)
                        for half_ in range(2):
                            kc = 2 * pair + half_
                            ins = t.matmul(
                                bankp[:, half_ * 512:(half_ + 1) * 512],
                                qh[:, qt * 128:(qt + 1) * 128],
                                kh[:, kc * 512:(kc + 1) * 512],
                                start=True, stop=True)
                        ins.then_inc(MM.h, 1)
                        MM.v += 1
                else:
                    if firstC:
                        guard(t, ("ocse", h - 1, 7))
                        firstC = False
                    t.wait_ge(EV.h, ev0 + 3 * qt + 3)
                    prb = PR2[qt % 2]
                    for kc in range(8):
                        pcs = pb[2 + kc // 4]
                        base = (kc % 4) * 32
                        ins = t.matmul(pcs[base:base + 32, :], invzb[:, :],
                                       prb[:, kc * 512:(kc + 1) * 512],
                                       start=(qt == 0), stop=(qt == NT - 1),
                                       tile_position=(0, base))
                    ins.then_inc(MM.h, 1)
                    MM.v += 1
        G[("oatt_mm", h)] = (MM, MM.v)

        @block.scalar
        def _(s, h=h, mm0=mm0, ac0=ac0, ev0=ev0):
            if h == 0:
                guard(s, ("gp_init",))
            for qt in range(NT):
                prb = PR2[qt % 2]
                for pair in range(4):
                    s.wait_ge(MM.h, mm0 + _sv2[(qt, pair)])
                    if pair == 0:
                        if qt >= 2:
                            s.wait_ge(MM.h, mm0 + _cv2[qt - 2])
                        elif h > 0:
                            s.wait_ge(MM.h, mm0 - MM2TOT + _cv2[14 + qt])
                        if qt >= 1:
                            s.wait_ge(EV.h, ev0 + 3 * (qt - 1) + 1)
                        elif h > 0:
                            s.wait_ge(EV.h, ev0 - 2)
                    bankp = pbA if pair % 2 == 0 else pbB
                    s.activation(prb[:, pair * 1024:(pair + 1) * 1024],
                                 bankp[:, :], AF.Exp, bias=nco[:, :],
                                 scale=ISQ,
                                 accum_out=zcol[:, pair:pair + 1]
                                 ).then_inc(AC.h, 1)
                    AC.v += 1

        @block.vector
        def _(v, h=h, mm0=mm0, ac0=ac0):
            for qt in range(NT):
                v.wait_ge(AC.h, ac0 + 4 * qt + 4)
                v.reduce_sum(stats[:, 8:9], zcol[:, 0:4], axis=AXX
                             ).then_inc(EV.h, 1)
                EV.v += 1
                v.wait_ge(EV.h, EV.v)
                v.reciprocal(invz[:, :], stats[:, 8:9]).then_inc(EV.h, 1)
                EV.v += 1
                v.wait_ge(EV.h, EV.v)
                if qt >= 1:
                    v.wait_ge(MM.h, mm0 + _cv2[qt - 1])
                elif h > 0:
                    v.wait_ge(MM.h, mm0)
                v.tensor_scalar_mul(invzb[:, :], ones32[:, :], invz[:, :]
                                    ).then_inc(EV.h, 1)
                EV.v += 1

        for kc in range(8):
            @block.vector
            def _(v, h=h, kc=kc):
                if kc == 0:
                    guard(v, ("oatt_mm", h))
                pcs = pb[2 + kc // 4]
                base = (kc % 4) * 32
                guard(v, ("ocst", h, kc - 2))
                v.tensor_copy(CS32[0:32, (kc % 2) * 512:(kc % 2 + 1) * 512],
                              pcs[base:base + 32, :]).then_inc(EV.h, 1)
                EV.v += 1
            G[("ocse", h, kc)] = (EV, EV.v)

            @block.gpsimd
            def _(g, h=h, kc=kc):
                guard(g, ("ocse", h, kc))
                slot_dma(g, cs_sh[0, h:h + 1, kc * 512:(kc + 1) * 512],
                         cs_sh[1, h:h + 1, kc * 512:(kc + 1) * 512],
                         CS32[0:1, (kc % 2) * 512:(kc % 2 + 1) * 512],
                         GS, f"cst{h}_{kc}")
            G[("ocst", h, kc)] = (GS, GS.v)
    gp_csall = GS.v
    _, brc = barrier(2, GS, gp_csall)
    G["br_cs"] = brc
    if upto == 2:
        return emit_tail()

    for c in range(NC):
        @block.gpsimd
        def _(g, c=c):
            if c == 0:
                guard(g, ("br_cs",))
            guard(g, ("valdone", c - 1))
            g.wait_ge(GS.h, GS.v)
            slot = get_reg(g, "slotreg")
            g.reg_load(slot, slotv[0:1, 0:1])
            with g.If_eq(slot, 0):
                g.dma_start(out=csr0[0:1, :], in_=cs_sh[0, c:c + 1, 0:T]
                            ).then_inc(GS.h, 16)
                g.dma_start(out=csr1[0:1, :], in_=cs_sh[1, c:c + 1, 0:T]
                            ).then_inc(GS.h, 16)
            with g.Else():
                g.dma_start(out=csr0[0:1, :], in_=cs_sh[0, c:c + 1, T:S]
                            ).then_inc(GS.h, 16)
                g.dma_start(out=csr1[0:1, :], in_=cs_sh[1, c:c + 1, T:S]
                            ).then_inc(GS.h, 16)
            GS.v += 32
        G[("csrld", c)] = (GS, GS.v)

        for tc in range(NTC):
            idx = c * NTC + tc
            bankb = pb[4 + idx % 2]

            @block.tensor
            def _(t, c=c, tc=tc, bankb=bankb, idx=idx):
                guard(t, ("csrld", c), ("vbe", idx - 2))
                t.matmul(bankb[:, :], onesr[0:1, 0:128],
                         csr0[0:1, tc * 512:(tc + 1) * 512],
                         start=True, stop=False)
                t.matmul(bankb[:, :], onesr[0:1, 0:128],
                         csr1[0:1, tc * 512:(tc + 1) * 512],
                         start=False, stop=True).then_inc(MM.h, 1)
                MM.v += 1
            G[("vbmm", idx)] = (MM, MM.v)

            @block.sync
            def _(q, c=c, tc=tc, idx=idx):
                guard(q, ("vbe", idx - 2))
                dq(q, LD, VST2[idx % 2][:, :],
                   v_out_d[c * 128:(c + 1) * 128, tc * 512:(tc + 1) * 512])
            G[("vld", idx)] = (LD, LD.v)

            @block.vector
            def _(v, c=c, tc=tc, bankb=bankb, idx=idx):
                guard(v, ("vbmm", idx), ("vld", idx))
                v.tensor_copy(invzf[:, :], bankb[:, :]).then_inc(EV.h, 1)
                EV.v += 1
                v.wait_ge(EV.h, EV.v)
                v.tensor_tensor(fmbuf[:, c * T + tc * 512:c * T + (tc + 1) * 512],
                                VST2[idx % 2][:, :], invzf[:, :], ALU.mult
                                ).then_inc(EV.h, 1)
                EV.v += 1
            G[("vbe", idx)] = (EV, EV.v)
        G[("valdone", c)] = (EV, EV.v)
    G["values"] = (EV, EV.v)

    WRHS = [arbf(20480, NC * 512), arbf(49152, NC * 512)]
    XS = [arf32(65536, 512), arf32(67584, 512)]
    YS = [arf32(69632, 512), arf32(71680, 512)]

    def emit_tm_proj(name, wsrc, bias_slot, writer, first_waits):
        for do in range(4):
            wr = WRHS[do % 2]

            @block.gpsimd
            def _(g, do=do, wr=wr):
                guard(g, (name, "mmdone", do - 2))
                if do < 2:
                    for w_ in first_waits:
                        guard(g, w_)
                src3 = wsrc[:, do * 512:(do + 1) * 512].rearrange(
                    "(c p) j -> p c j", p=128)
                dst3 = wr.rearrange("p (c j) -> p c j", c=NC)
                for hh in range(4):
                    dq(g, WT, dst3[:, hh * 4:(hh + 1) * 4, :],
                       src3[:, hh * 4:(hh + 1) * 4, :])
            G[(name, "wt", do)] = (WT, WT.v)

            for tt in range(NT):
                bank = pb[tt % 2]
                idx = do * NT + tt

                @block.tensor
                def _(t, do=do, tt=tt, wr=wr, bank=bank, idx=idx):
                    guard(t, (name, "wt", do))
                    if idx == 0:
                        for w_ in first_waits:
                            guard(t, w_)
                    guard(t, (name, "pev", idx - 2))
                    for c in range(NC):
                        ins = t.matmul(bank[:, :],
                                       fmbuf[:, c * T + tt * 128:
                                             c * T + (tt + 1) * 128],
                                       wr[:, c * 512:(c + 1) * 512],
                                       start=(c == 0),
                                       stop=(bias_slot is None and c == NC - 1))
                    if bias_slot is not None:
                        ins = t.matmul(bank[:, :], onesr[0:1, 0:128],
                                       browR[0:1, bias_slot * D + do * 512:
                                             bias_slot * D + (do + 1) * 512],
                                       start=False, stop=True)
                    ins.then_inc(MM.h, 1)
                    MM.v += 1
                G[(name, "mm", idx)] = (MM, MM.v)
                writer(name, do, tt, bank, idx)
            G[(name, "mmdone", do)] = (MM, MM.v)
        G[(name, "alldone")] = (MM, MM.v)
        sk = G.get((name, "sink", NC * NTC - 1))
        if sk is not None:
            G[(name, "sinkdone")] = sk

    def write_resid(src_d, dst_d, entry_key):
        def f(name, do, tt, bank, idx):
            @block.sync
            def _(q, do=do, tt=tt, idx=idx):
                guard(q, (name, "radd", idx - 2))
                if idx < 2:
                    guard(q, entry_key)
                dq(q, LD, XS[idx % 2][:, :],
                   src_d[tt * 128:(tt + 1) * 128, do * 512:(do + 1) * 512])
            G[(name, "rld", idx)] = (LD, LD.v)

            @block.vector
            def _(v, do=do, tt=tt, bank=bank, idx=idx):
                guard(v, (name, "mm", idx), (name, "rld", idx),
                      (name, "rst", idx - 2))
                v.tensor_add(YS[idx % 2][:, :], bank[:, :], XS[idx % 2][:, :]
                             ).then_inc(EV.h, 1)
                EV.v += 1
            G[(name, "radd", idx)] = (EV, EV.v)
            G[(name, "pev", idx)] = (EV, EV.v)

            @block.sync
            def _(q, do=do, tt=tt, idx=idx):
                guard(q, (name, "radd", idx))
                dq(q, ST, dst_d[tt * 128:(tt + 1) * 128, do * 512:(do + 1) * 512],
                   YS[idx % 2][:, :])
            G[(name, "rst", idx)] = (ST, ST.v)
        return f

    emit_tm_proj("oproj", w_o, None, write_resid(x_in, x1_d, ("values",)), [("values",)])
    G["x1_done"] = (ST, ST.v)
    if upto == 3:
        return emit_tail()

    @block.gpsimd
    def _(g):
        guard(g, ("x1_done",))
        dq(g, WT, browR[0:1, 0:D], brow_in[4:5, :])
        dq(g, WT, browR[0:1, D:2 * D], brow_in[5:6, :])
    G["g1rows"] = (WT, WT.v)

    G1T = arbf(36864, D)
    B1T = arbf(40960, D)

    def build_bcast_tiles(key, row_offs, dsts, waitkey):
        for r in range(2):
            for ch in range(4):
                @block.tensor
                def _(t, r=r, ch=ch):
                    if r == 0 and ch == 0:
                        guard(t, waitkey)
                    guard(t, (key, "ev", r, ch - 2),
                          (key, "ev", r - 1, ch + 2))
                    t.matmul(pb[4 + ch % 2][:, :], onesr[0:1, 0:128],
                             browR[0:1, row_offs[r] + ch * 512:
                                   row_offs[r] + (ch + 1) * 512],
                             start=True, stop=True).then_inc(MM.h, 1)
                    MM.v += 1
                G[(key, "mm", r, ch)] = (MM, MM.v)

                @block.vector
                def _(v, r=r, ch=ch):
                    guard(v, (key, "mm", r, ch))
                    v.tensor_copy(dsts[r][:, ch * 512:(ch + 1) * 512],
                                  pb[4 + ch % 2][:, :]).then_inc(EV.h, 1)
                    EV.v += 1
                G[(key, "ev", r, ch)] = (EV, EV.v)
        G[(key, "done")] = G[(key, "ev", 1, 3)]

    build_bcast_tiles("gb1", [0, D], [G1T, B1T], ("g1rows",))

    TMC = arf32(73728, D)

    def emit_ln_stack(tag, src_d, store_hd, gt, bt, extra_first):
        for tt in range(NT):
            xa = arf32(0 if tt % 2 == 0 else 8192, D)

            @block.sync
            def _(q, tt=tt, xa=xa):
                guard(q, (tag, "xfree", tt - 2))
                if tt < 2:
                    for w_ in extra_first:
                        guard(q, w_)
                dq(q, LD, xa[:, :], src_d[tt * 128:(tt + 1) * 128, :])
            G[(tag, "ld", tt)] = (LD, LD.v)

            G[(tag, "applyfree", tt)] = G.get((tag, "st1", tt - 1))
            ln_stats_apply(tag, tt, xa, TMC, accum_col=9)
            G[(tag, "xfree", tt)] = G[(tag, "h", tt)]

            if store_hd is not None:
                @block.sync
                def _(q, tt=tt):
                    guard(q, (tag, "h", tt))
                    dq(q, ST, store_hd[tt * 128:(tt + 1) * 128, :], TMC[:, :])
                G[(tag, "hst", tt)] = (ST, ST.v)
                G[(tag, "st1", tt)] = (ST, ST.v)
            else:
                G[(tag, "st1", tt)] = G[(tag, "h", tt)]

            @block.scalar
            def _(s, tt=tt):
                guard(s, (tag, "h", tt), (tag, "sqfree2", tt - 1),
                      (tag, "var2", tt - 1))
                s.activation(SQS[:, :], TMC[:, :], AF.Square,
                             accum_out=stats[:, 10:11]).then_inc(AC.h, 1)
                AC.v += 1
            G[(tag, "sq2", tt)] = (AC, AC.v)

            @block.vector
            def _(v, tt=tt):
                guard(v, (tag, "sq2", tt))
                v.tensor_scalar_mul(stats[:, 12:13], stats[:, 9:10], 1.0 / D
                                    ).then_inc(EV.h, 1); EV.v += 1
                v.tensor_scalar_mul(stats[:, 13:14], stats[:, 10:11], 1.0 / D
                                    ).then_inc(EV.h, 1); EV.v += 1
                v.wait_ge(EV.h, EV.v)
                v.tensor_tensor(stats[:, 14:15], stats[:, 12:13],
                                stats[:, 12:13], ALU.mult).then_inc(EV.h, 1)
                EV.v += 1
                v.wait_ge(EV.h, EV.v)
                v.tensor_tensor(stats[:, 15:16], stats[:, 13:14],
                                stats[:, 14:15], ALU.subtract).then_inc(EV.h, 1)
                EV.v += 1
            G[(tag, "var2", tt)] = (EV, EV.v)

            @block.scalar
            def _(s, tt=tt):
                guard(s, (tag, "var2", tt), ("gp_init",))
                s.activation(stats[:, 16:17], stats[:, 15:16], AF.Sqrt,
                             bias=epsc[:, :]).then_inc(AC.h, 1)
                AC.v += 1
            G[(tag, "std2", tt)] = (AC, AC.v)

            @block.vector
            def _(v, tt=tt, gt=gt, bt=bt):
                guard(v, (tag, "std2", tt))
                v.reciprocal(stats[:, 17:18], stats[:, 16:17]).then_inc(EV.h, 1)
                EV.v += 1
                v.wait_ge(EV.h, EV.v)
                guard(v, (tag, "sqfree", tt - 1), (tag, "st1", tt))
                v.tensor_scalar(TMC[:, :], TMC[:, :], stats[:, 12:13],
                                stats[:, 17:18], ALU.subtract, ALU.mult
                                ).then_inc(EV.h, 1); EV.v += 1
                v.wait_ge(EV.h, EV.v)
                v.tensor_tensor(TMC[:, :], TMC[:, :], gt[:, :], ALU.mult
                                ).then_inc(EV.h, 1); EV.v += 1
                v.wait_ge(EV.h, EV.v)
                v.tensor_tensor(SQS[:, :], TMC[:, :], bt[:, :], ALU.add
                                ).then_inc(EV.h, 1)
                EV.v += 1
            G[(tag, "hn", tt)] = (EV, EV.v)
            transpose_to_fm(tag, tt)
            G[(tag, "sqfree2", tt)] = G[(tag, "sqfree", tt)]
        G[(tag, "fmdone")] = (AC, AC.v)

    emit_ln_stack("l1", x1_d, hin_d, G1T, B1T, [("gb1", "done")])
    G["hn1_fm"] = G[("l1", "fmdone")]
    if "hn1f_d" in dbg:
        @block.sync
        def _(q):
            guard(q, ("hn1_fm",))
            dq(q, ST, hn1f_d.rearrange("(c p) t -> p c t", p=128),
               fmbuf.rearrange("p (c t) -> p c t", c=NC))
    if upto == 35:
        return emit_tail()

    @block.gpsimd
    def _(g):
        guard(g, ("gb1", "done"), ("hn1_fm",))
        for r in range(4):
            dq(g, WT, browR[0:1, r * D:(r + 1) * D], brow_in[r:r + 1, :])
    G["biasrows"] = (WT, WT.v)

    @block.sync
    def _(q):
        guard(q, ("hn1_fm",))
        dq(q, LD, RCOS[:, :], cos_in[:, :])
        dq(q, LD, RSIN[:, :], sin_in[:, :])
    G["ropeld"] = (LD, LD.v)

    emit_fm_proj("Kin", wk, 0, 1, True, sink_shared(k_in_sh),
                 [("hn1_fm",), ("ropeld",), ("biasrows",)])
    gp_kin = GS.v
    emit_fm_proj("Qin", wq, 0, 0, True, sink_dram(qin_d),
                 [("Kin", "alldone"), ("Kin", "sinkdone")])

    def write_vin(name, do, tt, bank, idx):
        @block.vector
        def _(v, do=do, tt=tt, bank=bank, idx=idx):
            guard(v, (name, "mm", idx), (name, "vst", idx - 2))
            if idx < 2:
                guard(v, ("Qin", "sink", NC * NTC - 1))
            v.tensor_copy(PSTG[idx % 2][:, :], bank[:, :]).then_inc(EV.h, 1)
            EV.v += 1
        G[(name, "vev", idx)] = (EV, EV.v)
        G[(name, "pev", idx)] = (EV, EV.v)

        @block.gpsimd
        def _(g, do=do, tt=tt, idx=idx):
            guard(g, (name, "vev", idx))
            src3 = PSTG[idx % 2].rearrange("p (h d) -> p h d", h=4)
            slot_dma(g, v_in_sh[0, 4 * do:4 * do + 4,
                                tt * 128:(tt + 1) * 128, :]
                     .rearrange("h t d -> t h d"),
                     v_in_sh[1, 4 * do:4 * do + 4,
                             tt * 128:(tt + 1) * 128, :]
                     .rearrange("h t d -> t h d"),
                     src3, GS, f"vi{idx}")
        G[(name, "vst", idx)] = (GS, GS.v)

    emit_tm_proj("vin", wv, 2, write_vin, [("Qin", "alldone"), ("biasrows",)])
    gp_vin = GS.v
    _, brkv = barrier(3, GS, max(gp_kin, gp_vin))
    G["br_kv"] = brkv
    if upto == 4:
        return emit_tail()

    KH5 = [arbf(20480, S), arbf(28672, S)]
    VH5 = [arbf(57344, S), arbf(65536, S)]
    QH5 = [arbf(49152, T), arbf(53248, T)]
    MSK = arbf(73728, 8 * 512)
    PB5 = [arbf(81920, 512), arbf(82944, 512)]
    PBP = [arbf(81920, 1024), arbf(83968, 1024)]

    @block.gpsimd
    def _(g):
        guard(g, ("br_kv",), ("Kin", "alldone"), ("Qin", "alldone"))
        dq(g, GS, MSK.rearrange("p (m f) -> p m f", m=8),
           masks_in.rearrange("m p f -> p m f"))
    G["mskld"] = (GS, GS.v)

    for h in range(H):
        kh, vh, qh = KH5[h % 2], VH5[h % 2], QH5[h % 2]

        @block.sync
        def _(q, h=h, kh=kh, vh=vh, qh=qh):
            guard(q, ("iatt_mm", h - 2))
            if h < 2:
                guard(q, ("mskld",), ("vin", "alldone"),
                      ("vin", "pev", 4 * NT - 1), ("vin", "vst", 4 * NT - 1),
                      ("Qin", "sink", NC * NTC - 1))
            dq(q, LD, kh[:, 0:T], k_in_sh[0, h * 128:(h + 1) * 128, :])
            dq(q, LD, kh[:, T:S], k_in_sh[1, h * 128:(h + 1) * 128, :])
            dq(q, LD, qh[:, :], qin_d[h * 128:(h + 1) * 128, :])
            for half in range(2):
                dq(q, LD, vh.rearrange("p (kt d) -> p kt d", kt=2 * NT)
                   [:, half * NT:(half + 1) * NT, :],
                   v_in_sh[half, h].rearrange("(kt p) d -> p kt d", p=128))
        G[("iatt_ld", h)] = (LD, LD.v)

        mm0, ac0, ev0 = MM.v, AC.v, EV.v
        _Moff = lambda qc: 8 * qc * qc + 8 * qc
        _Aoff = lambda qc: 2 * qc * qc + 2 * qc

        @block.tensor
        def _(t, h=h, kh=kh, vh=vh, qh=qh, mm0=mm0, ac0=ac0, ev0=ev0):
            guard(t, ("iatt_ld", h))
            for qc in range(NTC):
                blocks = [(sh_, j) for sh_ in range(2)
                          for j in range(4 * qc + 4)]
                nbp = 4 * qc + 4
                acq0 = ac0 + _Aoff(qc)

                def zav(bpz, qc=qc, blocks=blocks, nbp=nbp, acq0=acq0):
                    for half_ in range(2):
                        bi = 2 * bpz + half_
                        sh_, j = blocks[bi]
                        if bi == 0 and not (h == 0 and qc == 0):
                            t.wait_ge(EV.h, ev0 + 10 * qc)
                        jrel = j - 4 * qc
                        if jrel >= 0:
                            t.wait_ge(EV.h,
                                      ev0 + 10 * qc + sh_ * 4 + jrel + 1)
                        else:
                            t.wait_ge(AC.h, acq0 + bpz + 1)
                        t.matmul(pb[2][:, :], ones128[:, :],
                                 PBP[bpz % 2][:, half_ * 512:(half_ + 1) * 512],
                                 start=(bi == 0), stop=(bi == 2 * nbp - 1))
                        t.matmul(pb[3][:, :],
                                 vh[:, (sh_ * NT + j) * 128:
                                    (sh_ * NT + j + 1) * 128],
                                 PBP[bpz % 2][:, half_ * 512:(half_ + 1) * 512],
                                 start=(bi == 0), stop=(bi == 2 * nbp - 1)
                                 ).then_inc(MM.h, 1)
                        MM.v += 1

                for bp in range(nbp):
                    bankp = pbA if bp % 2 == 0 else pbB
                    t.wait_ge(AC.h, acq0 + max(0, bp - 1))
                    for half_ in range(2):
                        bi = 2 * bp + half_
                        sh_, j = blocks[bi]
                        t.matmul(bankp[:, half_ * 512:(half_ + 1) * 512],
                                 kh[:, (sh_ * NT + j) * 128:
                                    (sh_ * NT + j + 1) * 128],
                                 qh[:, qc * 512:(qc + 1) * 512],
                                 start=True, stop=True).then_inc(MM.h, 1)
                        MM.v += 1
                    if bp >= 1:
                        zav(bp - 1)
                zav(nbp - 1)

        @block.scalar
        def _(s, h=h, mm0=mm0):
            for qc in range(NTC):
                mmq0 = mm0 + _Moff(qc)
                for bp in range(4 * qc + 4):
                    s.wait_ge(MM.h, mmq0 + 2 * (bp + 1) + 2 * max(0, bp - 1))
                    bankp = pbA if bp % 2 == 0 else pbB
                    s.activation(PBP[bp % 2][:, :], bankp[:, :], AF.Exp,
                                 bias=nci[:, :], scale=ISQ).then_inc(AC.h, 1)
                    AC.v += 1

        @block.vector
        def _(v, h=h, mm0=mm0, ac0=ac0):
            for qc in range(NTC):
                nbp = 4 * qc + 4
                acq0 = ac0 + _Aoff(qc)
                for sh_ in range(2):
                    for jrel in range(4):
                        bi = sh_ * nbp + 4 * qc + jrel
                        bp, half_ = bi // 2, bi % 2
                        v.wait_ge(AC.h, acq0 + bp + 1)
                        mask_id = sh_ * 4 + jrel
                        v.tensor_tensor(
                            PBP[bp % 2][:, half_ * 512:(half_ + 1) * 512],
                            PBP[bp % 2][:, half_ * 512:(half_ + 1) * 512],
                            MSK[:, mask_id * 512:(mask_id + 1) * 512],
                            ALU.mult).then_inc(EV.h, 1)
                        EV.v += 1
                v.wait_ge(MM.h, mm0 + _Moff(qc) + 16 * qc + 16)
                v.reciprocal(invzf[:, :], pb[2][:, :]).then_inc(EV.h, 1)
                EV.v += 1
                v.wait_ge(EV.h, EV.v)
                v.tensor_tensor(
                    fmbuf[:, h * T + qc * 512:h * T + (qc + 1) * 512],
                    pb[3][:, :], invzf[:, :], ALU.mult).then_inc(EV.h, 1)
                EV.v += 1
        assert MM.v == mm0 + 160 and AC.v == ac0 + 40 and EV.v == ev0 + 40
        G[("iatt_mm", h)] = (MM, MM.v)
        G[("iatt_ev", h)] = (EV, EV.v)
    G["att_fm"] = (EV, EV.v)
    if upto == 5:
        return emit_tail()

    emit_tm_proj("aproj", wo2, 3, write_resid(hin_d, h2_d, ("att_fm",)), [("att_fm",)])
    G["h2_done"] = (ST, ST.v)

    @block.gpsimd
    def _(g):
        guard(g, ("h2_done",), ("aproj", "alldone"))
        dq(g, WT, browR[0:1, 0:D], brow_in[6:7, :])
        dq(g, WT, browR[0:1, D:2 * D], brow_in[7:8, :])
    G["g2rows"] = (WT, WT.v)

    G2T = arbf(36864, D)
    B2T = arbf(40960, D)
    build_bcast_tiles("gb2", [0, D], [G2T, B2T], ("g2rows",))
    emit_ln_stack("l2", h2_d, None, G2T, B2T, [("gb2", "done")])
    G["hn2_fm"] = G[("l2", "fmdone")]
    if "hn2f_d" in dbg:
        @block.sync
        def _(q):
            guard(q, ("hn2_fm",))
            dq(q, ST, hn2f_d.rearrange("(c p) t -> p c t", p=128),
               fmbuf.rearrange("p (c t) -> p c t", c=NC))
    if upto == 6:
        return emit_tail()

    WGB = arbf(84992, NC * 4)

    @block.gpsimd
    def _(g):
        guard(g, ("hn2_fm",))
        dq(g, WT, WGB.rearrange("p (c j) -> p c j", c=NC),
           wg_in.rearrange("(c p) j -> p c j", p=128))
    G["wtg"] = (WT, WT.v)

    for tt in range(NT):
        @block.tensor
        def _(t, tt=tt):
            guard(t, ("wtg",), ("gdone", tt - 1))
            for c in range(NC):
                ins = t.matmul(pb[4][:, 0:4],
                               fmbuf[:, c * T + tt * 128:c * T + (tt + 1) * 128],
                               WGB[:, c * 4:(c + 1) * 4],
                               start=(c == 0), stop=(c == NC - 1))
            ins.then_inc(MM.h, 1)
            MM.v += 1
        G[("gmm", tt)] = (MM, MM.v)

        @block.vector
        def _(v, tt=tt):
            guard(v, ("gmm", tt))
            v.tensor_copy(graw[:, :], pb[4][:, 0:4]).then_inc(EV.h, 1)
            EV.v += 1
            v.wait_ge(EV.h, EV.v)
            v.reduce_max(gst[:, 0:1], graw[:, :], axis=AXX, negate=True
                         ).then_inc(EV.h, 1)
            EV.v += 1
        G[("gnm", tt)] = (EV, EV.v)

        @block.scalar
        def _(s, tt=tt):
            guard(s, ("gnm", tt))
            s.activation(graw[:, :], graw[:, :], AF.Exp, bias=gst[:, 0:1],
                         accum_out=gst[:, 1:2]).then_inc(AC.h, 1)
            AC.v += 1
        G[("gexp", tt)] = (AC, AC.v)

        @block.vector
        def _(v, tt=tt):
            guard(v, ("gexp", tt))
            v.reciprocal(gst[:, 2:3], gst[:, 1:2]).then_inc(EV.h, 1); EV.v += 1
            v.wait_ge(EV.h, EV.v)
            v.tensor_scalar_mul(graw[:, :], graw[:, :], gst[:, 2:3]
                                ).then_inc(EV.h, 1); EV.v += 1
            v.wait_ge(EV.h, EV.v)
            v.tensor_tensor(gst[:, 12:16], graw[:, :], tieb[:, :], ALU.add
                            ).then_inc(EV.h, 1); EV.v += 1
            v.wait_ge(EV.h, EV.v)
            v.reduce_max(gst[:, 3:4], gst[:, 12:16], axis=AXX).then_inc(EV.h, 1)
            EV.v += 1
            v.wait_ge(EV.h, EV.v)
            v.tensor_scalar(gst[:, 8:12], gst[:, 12:16], gst[:, 3:4], None,
                            ALU.is_equal).then_inc(EV.h, 1); EV.v += 1
            v.wait_ge(EV.h, EV.v)
            v.scalar_tensor_tensor(gst[:, 12:16], gst[:, 8:12], -2.0,
                                   gst[:, 12:16], ALU.mult, ALU.add
                                   ).then_inc(EV.h, 1); EV.v += 1
            v.wait_ge(EV.h, EV.v)
            v.reduce_max(gst[:, 4:5], gst[:, 12:16], axis=AXX).then_inc(EV.h, 1)
            EV.v += 1
            v.wait_ge(EV.h, EV.v)
            v.tensor_scalar(gst[:, 16:20], gst[:, 12:16], gst[:, 4:5], None,
                            ALU.is_equal).then_inc(EV.h, 1); EV.v += 1
            v.wait_ge(EV.h, EV.v)
            v.reduce_max(gst[:, 21:22], graw[:, :], axis=AXX).then_inc(EV.h, 1)
            EV.v += 1
            v.tensor_tensor(gst[:, 12:16], graw[:, :], gst[:, 16:20], ALU.mult
                            ).then_inc(EV.h, 1); EV.v += 1
            v.wait_ge(EV.h, EV.v)
            v.reduce_sum(gst[:, 22:23], gst[:, 12:16], axis=AXX).then_inc(EV.h, 1)
            EV.v += 1
        G[("gtop", tt)] = (EV, EV.v)

        @block.scalar
        def _(s, tt=tt):
            guard(s, ("gtop", tt))
            s.activation(gst[:, 5:7], gst[:, 21:23], AF.Exp,
                         accum_out=gst[:, 7:8]).then_inc(AC.h, 1)
            AC.v += 1
        G[("gexp2", tt)] = (AC, AC.v)

        @block.vector
        def _(v, tt=tt):
            guard(v, ("gexp2", tt))
            v.reciprocal(gst[:, 20:21], gst[:, 7:8]).then_inc(EV.h, 1)
            EV.v += 1
            v.wait_ge(EV.h, EV.v)
            v.tensor_scalar_mul(gst[:, 5:7], gst[:, 5:7], gst[:, 20:21]
                                ).then_inc(EV.h, 1); EV.v += 1
            v.wait_ge(EV.h, EV.v)
            v.tensor_scalar_mul(combf[:, :], gst[:, 8:12], gst[:, 5:6]
                                ).then_inc(EV.h, 1); EV.v += 1
            v.tensor_scalar_mul(gst[:, 16:20], gst[:, 16:20], gst[:, 6:7]
                                ).then_inc(EV.h, 1); EV.v += 1
            v.wait_ge(EV.h, EV.v)
            v.tensor_tensor(combf[:, :], combf[:, :], gst[:, 16:20], ALU.add
                            ).then_inc(EV.h, 1); EV.v += 1
            v.wait_ge(EV.h, EV.v)
            v.tensor_copy(combb[:, :], combf[:, :]).then_inc(EV.h, 1)
            EV.v += 1
        G[("gcomb", tt)] = (EV, EV.v)

        @block.tensor
        def _(t, tt=tt):
            guard(t, ("gcomb", tt), ("gtrev", tt - 1))
            if tt == 0:
                guard(t, ("l2", "tb", NT - 1, 2), ("l2", "tb", NT - 1, 3))
            t.transpose(ptr[0][0:4, 0:128], combb[:, :], ident[:, :]
                        ).then_inc(MM.h, 1)
            MM.v += 1
        G[("gtr", tt)] = (MM, MM.v)

        @block.vector
        def _(v, tt=tt):
            guard(v, ("gtr", tt))
            v.tensor_copy(combT[0:4, tt * 128:(tt + 1) * 128],
                          ptr[0][0:4, 0:128]).then_inc(EV.h, 1)
            EV.v += 1
        G[("gtrev", tt)] = (EV, EV.v)
        G[("gdone", tt)] = (EV, EV.v)
    G["gate_done"] = (EV, EV.v)
    if upto == 65:
        return emit_tail()

    X7 = [arf32(0, 512), arf32(2048, 512)]
    H7 = [arf32(4096, 512), arf32(6144, 512)]
    HMID = arbf(8192, NC * 2048)
    Y7 = [arf32(73728, 512), arf32(75776, 512)]
    GE7 = [arf32(77824, 512), arf32(79872, 512)]
    CB7 = arbf(81920, 4 * 512)
    BE1R = arbf(86016, 2048)
    W1C = [arbf(90112, NC * 128), arbf(94208, NC * 128)]
    W2R = [arbf(98304, 8 * 512), arbf(106496, 8 * 512)]

    for e in range(n_experts):
        pWT, pMM, pEV, pAC, pLD, pGS = WT.v, MM.v, EV.v, AC.v, LD.v, GS.v
        pWT += 16
        G[("be1ld", e)] = (WT, pWT)
        for tc in range(4):
            pMM += 1
            G[("cbmm", e, tc)] = (MM, pMM)
        for tc in range(4):
            pEV += 1
            G[("cbev", e, tc)] = (EV, pEV)
        G[("cbdone", e)] = (EV, pEV)
        for ic in range(NC):
            w1i = e * NC + ic
            pWT += 64
            G[("w1wt", w1i)] = (WT, pWT)
            for tc in range(4):
                idx = ic * 4 + tc
                fi = e * 64 + idx
                pMM += 1
                G[("w1mm", e, idx)] = (MM, pMM)
                if tc == 3:
                    G[("w1use", w1i)] = (MM, pMM)
                pAC += 1
                G[("hmge", e, idx)] = (AC, pAC)
                G[("hmgeF", fi)] = (AC, pAC)
                pEV += 1
                G[("hmsc", e, idx)] = (EV, pEV)
                G[("hmscF", fi)] = (EV, pEV)
        G[("hmid_done", e)] = (EV, pEV)
        G[("be1free", e)] = (MM, pMM)
        G[("cbfree", e)] = (EV, pEV)
        for do in range(4):
            for hw in range(2):
                w2i = e * 8 + do * 2 + hw
                pWT += 16
                G[("w2wt", w2i)] = (WT, pWT)
            for tt in range(16):
                idx = do * 16 + tt
                yc = e * 64 + idx
                pMM += 1
                G[("w2mmv", e, idx)] = (MM, pMM)
                if tt == 15:
                    G[("w2use", e * 8 + do * 2)] = (MM, pMM)
                    G[("w2use", e * 8 + do * 2 + 1)] = (MM, pMM)
                pLD += 32 if e == 0 else 16
                G[("yld", e, idx)] = (LD, pLD)
                pEV += 2 if e == 0 else 1
                G[("yadd", e, idx)] = (EV, pEV)
                G[("yevF", yc)] = (EV, pEV)
                G[("yldF", yc)] = (EV, pEV)
                pGS += 16
                G[("ystF", yc)] = (GS, pGS)
                G[("ywr", e, idx)] = (GS, pGS)
        G[("hmuse", e)] = (MM, pMM)

        @block.gpsimd
        def _(g, e=e):
            guard(g, ("be1free", e - 1), ("vbmm", 63))
            dq(g, WT, BE1R[0:1, :], be1_in[e:e + 1, :])
            for ic in range(NC):
                w1i = e * NC + ic
                guard(g, ("w1use", w1i - 2))
                load_wcol(g, W1C[w1i % 2], we1[e], ic * 128)
            for do in range(4):
                for hw in range(2):
                    w2i = e * 8 + do * 2 + hw
                    guard(g, ("w2use", w2i - 2))
                    if w2i < 2:
                        guard(g, ("gb2", "done"))
                    dq(g, WT,
                       W2R[w2i % 2].rearrange("p (c j) -> p c j", c=8),
                       we2[e, :, do * 512:(do + 1) * 512].rearrange(
                           "(c p) j -> p c j", p=128)
                       [:, hw * 8:(hw + 1) * 8, :])
                for tt in range(16):
                    idx = do * 16 + tt
                    guard(g, ("yadd", e, idx))
                    dq(g, GS, y_out[tt * 128:(tt + 1) * 128,
                                    do * 512:(do + 1) * 512],
                       Y7[idx % 2][:, :])

        @block.tensor
        def _(t, e=e):
            guard(t, ("gate_done",), ("cbfree", e - 1))
            for tc in range(4):
                guard(t, ("cbev", e, tc - 2))
                t.matmul(pb[4 + tc % 2][:, :],
                         sel4[0:4, e * 128:(e + 1) * 128],
                         combT[0:4, tc * 512:(tc + 1) * 512],
                         start=True, stop=True).then_inc(MM.h, 1)
                MM.v += 1
            for ic in range(NC):
                w1i = e * NC + ic
                wb = W1C[w1i % 2]
                for tc in range(4):
                    idx = ic * 4 + tc
                    fi = e * 64 + idx
                    bank = pb[idx % 2]
                    guard(t, ("w1wt", w1i), ("be1ld", e),
                          ("cbdone", e), ("hmgeF", fi - 2))
                    if idx == 0:
                        guard(t, ("hmuse", e - 1))
                    for c in range(NC):
                        t.matmul(bank[:, :], wb[:, c * 128:(c + 1) * 128],
                                 fmbuf[:, c * T + tc * 512:
                                       c * T + (tc + 1) * 512],
                                 start=(c == 0), stop=False)
                    t.matmul(bank[:, :],
                             BE1R[0:1, ic * 128:(ic + 1) * 128],
                             onesr[0:1, :], start=False, stop=True
                             ).then_inc(MM.h, 1)
                    MM.v += 1
            for do in range(4):
                w2i0 = e * 8 + do * 2
                w2i1 = w2i0 + 1
                for tt in range(16):
                    idx = do * 16 + tt
                    yc = e * 64 + idx
                    bank = pb[2 + idx % 2]
                    guard(t, ("w2wt", w2i0), ("hmid_done", e),
                          ("yevF", yc - 2))
                    for c in range(NC):
                        if c == 8:
                            guard(t, ("w2wt", w2i1))
                        wr = W2R[(w2i0 if c < 8 else w2i1) % 2]
                        ins = t.matmul(bank[:, :],
                                       HMID.rearrange("p (c x) -> p c x",
                                                      c=NC)
                                       [:, c, tt * 128:(tt + 1) * 128],
                                       wr[:, (c % 8) * 512:
                                          (c % 8 + 1) * 512],
                                       start=(c == 0),
                                       stop=(e != 0 and c == NC - 1))
                    if e == 0:
                        ins = t.matmul(bank[:, :],
                                       combT[0:4, tt * 128:(tt + 1) * 128],
                                       be2r[0:4, do * 512:(do + 1) * 512],
                                       start=False, stop=True)
                    ins.then_inc(MM.h, 1)
                    MM.v += 1

        @block.scalar
        def _(s, e=e):
            for ic in range(NC):
                for tc in range(4):
                    idx = ic * 4 + tc
                    fi = e * 64 + idx
                    guard(s, ("w1mm", e, idx), ("hmscF", fi - 2))
                    s.activation(GE7[idx % 2][:, :], pb[idx % 2][:, :],
                                 AF.Gelu).then_inc(AC.h, 1)
                    AC.v += 1

        @block.sync
        def _(q, e=e):
            for do in range(4):
                for tt in range(16):
                    idx = do * 16 + tt
                    yc = e * 64 + idx
                    guard(q, ("yldF", yc - 2), ("ywr", e - 1, idx))
                    if e == 0 and idx == 0:
                        guard(q, ("gate_done",), ("h2_done",))
                    if e == 0:
                        dq(q, LD, X7[idx % 2][:, :],
                           x1_d[tt * 128:(tt + 1) * 128,
                                do * 512:(do + 1) * 512])
                        dq(q, LD, H7[idx % 2][:, :],
                           h2_d[tt * 128:(tt + 1) * 128,
                                do * 512:(do + 1) * 512])
                    else:
                        dq(q, LD, X7[idx % 2][:, :],
                           y_out[tt * 128:(tt + 1) * 128,
                                 do * 512:(do + 1) * 512])

        @block.vector
        def _(v, e=e):
            for tc in range(4):
                guard(v, ("cbmm", e, tc))
                v.tensor_copy(CB7[:, tc * 512:(tc + 1) * 512],
                              pb[4 + tc % 2][:, :]).then_inc(EV.h, 1)
                EV.v += 1
            for ic in range(NC):
                for tc in range(4):
                    idx = ic * 4 + tc
                    guard(v, ("hmge", e, idx))
                    if idx < 2:
                        guard(v, ("hmuse", e - 1))
                    v.tensor_tensor(HMID[:, ic * 2048 + tc * 512:
                                         ic * 2048 + (tc + 1) * 512],
                                    GE7[idx % 2][:, :],
                                    CB7[:, tc * 512:(tc + 1) * 512],
                                    ALU.mult).then_inc(EV.h, 1)
                    EV.v += 1
            for do in range(4):
                for tt in range(16):
                    idx = do * 16 + tt
                    yc = e * 64 + idx
                    bank = pb[2 + idx % 2]
                    guard(v, ("w2mmv", e, idx), ("yld", e, idx),
                          ("ystF", yc - 2))
                    v.tensor_add(Y7[idx % 2][:, :], bank[:, :],
                                 X7[idx % 2][:, :]).then_inc(EV.h, 1)
                    EV.v += 1
                    if e == 0:
                        v.wait_ge(EV.h, EV.v)
                        v.tensor_add(Y7[idx % 2][:, :], Y7[idx % 2][:, :],
                                     H7[idx % 2][:, :]).then_inc(EV.h, 1)
                        EV.v += 1

        assert (WT.v, MM.v, EV.v, AC.v, LD.v, GS.v) == \
            (pWT, pMM, pEV, pAC, pLD, pGS), ("ph7 ledger", e)
        if upto == 71 + e or (upto in (172, 173, 176) and e == 1) or \
                (upto == 70 and e == 0):
            return emit_tail()

    return emit_tail()


_CACHE = {}


def get_nc(dbg=(), upto=99):
    key = (tuple(sorted(dbg)), upto)
    if key not in _CACHE:
        _CACHE[key] = build_nc(dbg=dbg, upto=upto)
    return _CACHE[key]


def _rope_tables(parity):
    pos = np.zeros(T, np.float64)
    for j in range(LBLK):
        pos[j * 128:(j + 1) * 128] = (2 * j + parity) * 128 + np.arange(128)
    inv = 1.0 / (10000.0 ** (np.arange(0, 128, 2, dtype=np.float64) / 128.0))
    freqs = np.concatenate([inv, inv])[:, None] * pos[None, :]
    sin = np.sin(freqs)
    sin[:64, :] *= -1.0
    return (np.cos(freqs).astype(np.float32), sin.astype(np.float32))


def _masks(parity):
    m = np.zeros((8, 128, 512), np.float32)
    tri = (np.arange(128)[:, None] <= np.arange(128)[None, :]).astype(np.float32)
    for s in range(2):
        for jrel in range(4):
            blk = np.zeros((128, 512), np.float32)
            gk = 2 * jrel + s
            for ip in range(4):
                gq = 2 * ip + parity
                sub = slice(ip * 128, (ip + 1) * 128)
                if gk < gq:
                    blk[:, sub] = 1.0
                elif gk == gq:
                    blk[:, sub] = tri
            m[s * 4 + jrel] = blk
    return m


def _make_in_maps(inputs):
    xs = np.ascontiguousarray(inputs["L0_attn_x"])
    brow = np.ascontiguousarray(np.stack(
        [inputs["L0_ffn_bq"], inputs["L0_ffn_bk"], inputs["L0_ffn_bv"],
         inputs["L0_ffn_bo"], inputs["L0_ffn_ln1_w"], inputs["L0_ffn_ln1_b"],
         inputs["L0_ffn_ln2_w"], inputs["L0_ffn_ln2_b"]]).astype(np.float32))
    import os as _os
    nonce = int.from_bytes(_os.urandom(4), "little") % (1 << 28) + (1 << 20)
    shared = {
        "w_qkv": np.ascontiguousarray(inputs["L0_attn_w_qkv"]),
        "w_o": np.ascontiguousarray(inputs["L0_attn_w_o"]),
        "wq": np.ascontiguousarray(inputs["L0_ffn_wq"]),
        "wk": np.ascontiguousarray(inputs["L0_ffn_wk"]),
        "wv": np.ascontiguousarray(inputs["L0_ffn_wv"]),
        "wo2": np.ascontiguousarray(inputs["L0_ffn_wo"]),
        "brow_in": brow,
        "wg": np.ascontiguousarray(inputs["L0_ffn_w_gate"]),
        "we1": np.ascontiguousarray(inputs["L0_ffn_we1"]),
        "be1": np.ascontiguousarray(inputs["L0_ffn_be1"]),
        "we2": np.ascontiguousarray(inputs["L0_ffn_we2"]),
        "be2": np.ascontiguousarray(inputs["L0_ffn_be2"]),
    }
    in_maps = []
    for core in range(N_CORES):
        b, p = core // 2, core % 2
        xb = xs[b].reshape(NBLK, 128, D)[p::2].reshape(T, D)
        cos, sin = _rope_tables(p)
        im = dict(shared)
        im["x_shard"] = np.ascontiguousarray(xb)
        im["rope_cos"] = cos
        im["rope_sin"] = sin
        im["masks"] = _masks(p)
        im["nonce"] = np.array([[nonce]], np.int32)
        im["slot"] = np.array([[p]], np.int32)
        in_maps.append(im)
    return in_maps


def kernel(**inputs):
    nc = get_nc()
    in_maps = _make_in_maps(inputs)
    res = run_bass_kernel_spmd(nc, in_maps, core_ids=list(range(8)))
    out = np.zeros((B, S, D), np.float32)
    for core in range(8):
        b, p = core // 2, core % 2
        yb = np.asarray(res.results[core]["y"]).reshape(LBLK, 128, D)
        out[b].reshape(NBLK, 128, D)[p::2] = yb
    return out
